# version 15
# baseline (speedup 1.0000x reference)
import numpy as np
import ml_dtypes
import concourse.bass as bass
import concourse.mybir as mybir
from concourse.bass_utils import run_bass_kernel_spmd

F32 = mybir.dt.float32
BF16 = mybir.dt.bfloat16
I32 = mybir.dt.int32
AF = mybir.ActivationFunctionType
ALU = mybir.AluOpType
AX = mybir.AxisListType

B, L, D = 16, 4096, 1024
DI, DSTATE, NH, HD = 2048, 128, 32, 64
CONVD = 2304
DIP = 4384
CK = 128
CAP = 1e4
EPS = 1e-5
NCORE = 8
HSTR = 160
HDAT = 16

ENGINES = ("sync", "tensor", "vector", "scalar", "gpsimd")
CSEMS = ("pe", "dve", "act", "pool", "dtr", "dld", "dsp", "dgp")
Z = ("pe", 0)


def bfq(x):
    return np.asarray(x, dtype=np.float32).astype(ml_dtypes.bfloat16)


class P:
    def __init__(self, nc, role, eng, sems):
        self.nc, self.role, self.eng, self.sems = nc, role, eng, sems
        self.c = {k: 0 for k in sems}
        self.tq = 0

    OWN = {"vector": "dve", "scalar": "act", "gpsimd": "pool"}

    def do(self, engine, emit, inc=None, n=1, chain=True):
        own = self.OWN.get(engine)
        serialize = own is not None and (inc is None or inc == own)
        if serialize:
            prev = self.c[own]
            inc, n = own, 1
        if inc is not None:
            self.c[inc] += n
        if self.role == engine:
            if serialize and chain and prev > 0:
                self.eng.wait_ge(self.sems[own], prev)
            inst = emit(self.eng)
            if inc is not None:
                inst.then_inc(self.sems[inc], n)

    def tqueue(self):
        return "sync"

    def ev(self, sem):
        return (sem, self.c[sem])

    def wait(self, engine, ev):
        if ev is None or ev[1] <= 0:
            return
        if self.role == engine:
            self.eng.wait_ge(self.sems[ev[0]], ev[1])

    def snap(self):
        return {s: self.c[s] for s in CSEMS}

    def wait_snap(self, engine, snap):
        if snap is None:
            return
        for s in CSEMS:
            self.wait(engine, (s, snap[s]))

    def barrier(self):
        own = {"sync": "b_sy", "tensor": "b_pe", "vector": "b_dv",
               "scalar": "b_ac", "gpsimd": "b_gp"}
        for e in ENGINES:
            self.do(e, lambda en: en.nop(), inc=own[e])
        for e in ENGINES:
            for s in self.sems:
                if self.c[s] > 0:
                    self.wait(e, (s, self.c[s]))


def build_nc(NSEQ=2, NCK=32, detect_races=True, dbg=False):
    LL = NCK * CK
    nc = bass.Bass("TRN2", target_bir_lowering=False,
                   detect_race_conditions=detect_races)

    x_in = nc.dram_tensor("x", [NSEQ, LL, D], F32, kind="ExternalInput")
    out_t = nc.dram_tensor("out", [NSEQ, LL, D], F32, kind="ExternalOutput")
    w_in = nc.dram_tensor("W_in", [DIP, D], F32, kind="ExternalInput")
    w_out = nc.dram_tensor("W_out", [D, DI], F32, kind="ExternalInput")
    w1 = nc.dram_tensor("W1", [4 * D, D], F32, kind="ExternalInput")
    b1 = nc.dram_tensor("b1", [4 * D], F32, kind="ExternalInput")
    w2 = nc.dram_tensor("W2", [D, 4 * D], F32, kind="ExternalInput")
    b2 = nc.dram_tensor("b2", [D], F32, kind="ExternalInput")
    pr = {}
    for dd in ("f", "r"):
        pr[f"convw_{dd}"] = nc.dram_tensor(f"conv_w_{dd}", [CONVD, 4], F32, kind="ExternalInput")
        pr[f"convb_{dd}"] = nc.dram_tensor(f"conv_b_{dd}", [CONVD], F32, kind="ExternalInput")
        pr[f"dtb_{dd}"] = nc.dram_tensor(f"dt_bias_{dd}", [NH], F32, kind="ExternalInput")
        pr[f"alog_{dd}"] = nc.dram_tensor(f"A_log_{dd}", [NH], F32, kind="ExternalInput")
        pr[f"dsk_{dd}"] = nc.dram_tensor(f"D_{dd}", [NH], F32, kind="ExternalInput")
        pr[f"nrm_{dd}"] = nc.dram_tensor(f"norm_w_{dd}", [DI], F32, kind="ExternalInput")

    E1 = np.repeat(np.eye(NH, dtype=np.float32), CK, axis=1)
    c_E2 = nc.inline_tensor(bfq(np.vstack([E1, E1])), "cE2")
    c_triu = nc.inline_tensor(bfq(np.triu(np.ones((CK, CK), np.float32))), "ctriu")
    c_tril = nc.inline_tensor(bfq(np.tril(np.ones((CK, CK), np.float32))), "ctril")
    c_identf = nc.inline_tensor(np.eye(CK, dtype=np.float32), "cidentf")
    c_identb = nc.inline_tensor(np.eye(CK).astype(ml_dtypes.bfloat16), "cidentb")
    c_bias = nc.inline_tensor(np.tile(np.array([[1.0, EPS, 0.0, 0.0]], dtype=np.float32), (128, 1)), "cbias")

    KD = "ExternalOutput" if dbg else "Internal"
    sp_yf = nc.dram_tensor("sp_yf", [NSEQ, NCK, CK, DI], BF16, kind=KD)
    sp_sz = nc.dram_tensor("sp_sz", [NSEQ, NCK, CK, DI], BF16, kind=KD)
    sp_xbc = nc.dram_tensor("sp_xbc", [NSEQ, NCK, CK, 18 * HSTR], BF16, kind=KD)
    sp_dt = nc.dram_tensor("sp_dt", [NSEQ, NCK, CK, NH], F32, kind=KD)
    sp_xn = nc.dram_tensor("sp_xn", [NSEQ, NCK, CK, D], BF16, kind=KD)
    sp_s = nc.dram_tensor("sp_s", [NSEQ, NCK, CK, D], F32, kind=KD)
    sp_flat = nc.dram_tensor("sp_flat", [NSEQ, NCK, 96, CK], BF16, kind=KD)
    sp_w1t = nc.dram_tensor("sp_w1t", [8, CK, 4 * D], BF16, kind=KD)
    sp_w2t = nc.dram_tensor("sp_w2t", [32, CK, D], BF16, kind=KD)
    sp_wot = nc.dram_tensor("sp_wot", [16, CK, D], BF16, kind=KD)

    SB_BASE = 16512
    SB_LIMIT = 229000
    cur = [0]

    def salloc(name, shape, dtype):
        esz = 2 if dtype == BF16 else 4
        nbytes = int(np.prod(shape[1:])) * esz
        o = (cur[0] + 31) // 32 * 32
        cur[0] = o + nbytes
        assert cur[0] <= SB_LIMIT, f"SBUF overflow at {name}: {cur[0]}"
        return nc.alloc_sbuf_tensor_at(name, list(shape), dtype, offset=o)

    def region(base, specs, pref=""):
        cur[0] = base + (SB_BASE if base < SB_BASE else 0)
        out = {}
        for nm, shape, dt_ in specs:
            out[nm] = salloc(pref + nm, shape, dt_)
        return out, cur[0]

    sb0, P0E = region(0, [
        ("onesbf", [2, 128], BF16),
        ("onescol", [128, 2], BF16),
        ("col1", [128, 16], F32),
        ("rowbf", [1, 16], BF16),
        ("rowf", [1, 256], F32),
        ("rowt", [1, 256], BF16),
        ("rowD", [1, 64], BF16),
        ("colbf", [128, 4], BF16),
        ("cbias", [128, 4], F32),
        ("identb", [128, 128], BF16),
        ("scrA", [128, 1024], BF16),
    ])
    SML = [
        ("dtraw", [128, NH], F32),
        ("dt", [128, NH], F32),
        ("sm1", [128, NH], F32),
        ("sm2", [128, NH], F32),
        ("acum", [128, 2 * NH], F32),
        ("totbc", [128, NH], F32),
        ("dstate", [128, NH], F32),
        ("exptot", [128, NH], F32),
        ("dtds", [128, NH], BF16),
        ("pk96", [128, 96], F32),
        ("pk64", [128, 64], F32),
        ("vtok", [128, 32], F32),
        ("bstg", [128, 32], BF16),
        ("flatpk", [96, 128], BF16),
        ("hilo", [128, 2 * NH], BF16),
        ("tot_hl", [1, 2 * NH], BF16),
        ("trow", [1, NH], F32),
        ("lhsT_mask", [128, 128], BF16),
    ]
    sb1, P1E0 = region(P0E, [
        ("rhs_mask", [128, 4096], BF16),
        ("expR", [1, 4096], BF16),
        ("triu", [128, 128], BF16),
        ("tril", [128, 128], BF16),
        ("identf", [128, 128], F32),
        ("convw_f", [128, 72], F32),
        ("convw_r", [128, 72], F32),
        ("convb_f", [128, 18], F32),
        ("convb_r", [128, 18], F32),
        ("dtb_f", [128, NH], F32),
        ("dtb_r", [128, NH], F32),
        ("negA_f", [128, NH], F32),
        ("negA_r", [128, NH], F32),
        ("Dbc_f", [128, NH], BF16),
        ("Dbc_r", [128, NH], BF16),
        ("nwbc_f", [128, DI], BF16),
        ("nwbc_r", [128, DI], BF16),
        ("cbst", [32, 128], F32),
    ])
    sml0, P1E1 = region(P1E0, SML, pref="s0_")
    sml1, P1E = region(P1E1, SML, pref="s1_")
    sml = [sml0, sml1]
    sbm = {}
    sbm.update(sb0)
    sbm.update(sb1)

    wa, WAE = region(P1E, [
        ("w_inT", [128, 8 * DIP], BF16),
        ("xA0", [128, D], F32),
        ("xA1", [128, D], F32),
        ("xn", [128, D], BF16),
        ("xnT", [128, 8 * 128], BF16),
        ("siluz0", [128, DI], BF16),
        ("siluz1", [128, DI], BF16),
        ("stage", [128, 4 * 512], BF16),
        ("halo", [128, 18 * HSTR], BF16),
        ("tail0", [128, 18 * 4], BF16),
        ("tail1", [128, 18 * 4], BF16),
        ("convT0", [128, CONVD], BF16),
        ("convT1", [128, CONVD], BF16),
        ("xstok", [128, DI], BF16),
        ("btok", [128, 128], BF16),
        ("gt", [128, 128], BF16),
        ("gmm", [128, 4096], BF16),
        ("gmv", [128, 4096], BF16),
        ("y0", [128, DI], BF16),
        ("dx", [128, DI], BF16),
        ("S0", [128, DI], F32),
        ("S1", [128, DI], F32),
        ("stil0", [128, DI], BF16),
        ("stil1", [128, DI], BF16),
        ("xdec", [128, DI], BF16),
    ], pref="a_")

    wb, WBE = region(P1E, [
        ("w_outT", [128, 16 * D], BF16),
        ("szB0", [128, DI], BF16),
        ("szB1", [128, DI], BF16),
        ("yfB0", [128, DI], BF16),
        ("yfB1", [128, DI], BF16),
        ("haloB0", [128, 18 * HSTR], BF16),
        ("haloB1", [128, 18 * HSTR], BF16),
        ("convT0", [128, CONVD], BF16),
        ("convT1", [128, CONVD], BF16),
        ("tailR0", [128, 18 * 4], BF16),
        ("tailR1", [128, 18 * 4], BF16),
        ("xstok", [128, DI], BF16),
        ("btok", [128, 128], BF16),
        ("gt", [128, 128], BF16),
        ("gmm", [128, 4096], BF16),
        ("gmv", [128, 4096], BF16),
        ("y0", [128, DI], BF16),
        ("dx", [128, DI], BF16),
        ("S0", [128, DI], F32),
        ("S1", [128, DI], F32),
        ("stil0", [128, DI], BF16),
        ("stil1", [128, DI], BF16),
        ("xdec", [128, DI], BF16),
        ("ycombT", [128, DI], BF16),
        ("sevac", [128, D], F32),
    ], pref="b_")

    wp, WPE = region(WBE, [
        ("wf0", [128, 2048], F32),
        ("wf1", [128, 2048], F32),
        ("wi0", [128, 2048], I32),
        ("wtmp0", [128, 2048], BF16),
        ("wtmp1", [128, 2048], BF16),
        ("trst", [128, 2048], BF16),
        ("acc", [128, 8], F32),
    ], pref="p_")

    wc, WCE = region(P0E, [
        ("w1T", [128, 8 * 4 * D], BF16),
        ("w2T", [128, 32 * D], BF16),
        ("xC0", [128, D], F32),
        ("xC1", [128, D], F32),
        ("xnC", [128, D], BF16),
        ("xnTC0", [128, 8 * 128], BF16),
        ("xnTC1", [128, 8 * 128], BF16),
        ("hbuf", [128, 4 * D], BF16),
        ("hT", [128, 32 * 128], BF16),
        ("sC0", [128, D], F32),
        ("sC1", [128, D], F32),
        ("thC", [128, D], BF16),
        ("gC", [128, D], F32),
        ("outC", [128, D], F32),
        ("b1row", [1, 4 * D], BF16),
        ("b2row", [1, D], BF16),
    ], pref="c_")

    nc.const_aps.register(F32, 1.0, sb0["cbias"][:, 0:1])
    nc.const_aps.register(F32, EPS, sb0["cbias"][:, 1:2])
    nc.const_aps.register(F32, 0.0, sb0["cbias"][:, 2:3])
    nc.const_aps.register(BF16, 0.0, sb0["cbias"][:, 3:4].bitcast(BF16)[:, 0:1])

    ps_a = nc.alloc_psum_tensor("ps_a", [128, 1024], F32)
    ps_s = nc.alloc_psum_tensor("ps_s", [128, 512], F32)
    ps_x = nc.alloc_psum_tensor("ps_x", [128, 512], F32)
    ps_y = nc.alloc_psum_tensor("ps_y", [128, 2048], F32)

    sems_proto = list(CSEMS) + ["b_sy", "b_pe", "b_dv", "b_ac", "b_gp"]

    def emit_hilo(p, src32, hi_bf, tmp32, lo_bf):
        p.do("vector", lambda e: e.tensor_copy(hi_bf, src32), inc="dve")
        p.do("vector", lambda e: e.tensor_copy(tmp32, hi_bf), inc="dve")
        p.do("vector", lambda e: e.tensor_sub(tmp32, src32, tmp32), inc="dve")
        p.do("vector", lambda e: e.tensor_copy(lo_bf, tmp32), inc="dve")

    def bview(t, nrep):
        return t[:, :, None].broadcast_to([t.shape[0], t.shape[1], nrep])

    def pe_tp(p, srcs, psview, dst, pre_waits_pe=(), pre_waits_dve=(), eng="vector"):
        for ev in pre_waits_pe:
            p.wait("tensor", ev)
        n = len(srcs)
        for i, s_ in enumerate(srcs):
            p.do("tensor", lambda e, i=i, s_=s_: e.transpose(
                psview[:, i * 128:(i + 1) * 128], s_, sbm["identb"][:]),
                inc=("pe" if i == n - 1 else None))
        pe_ev = p.ev("pe")
        p.wait(eng, pe_ev)
        for ev in pre_waits_dve:
            p.wait(eng, ev)
        srcv = psview[:, 0:n * 128]
        if len(dst.shape) == 3:
            srcv = srcv.rearrange("p (a b) -> p a b", b=128)
        if eng == "vector":
            p.do("vector", lambda e: e.tensor_copy(dst, srcv), inc="dve")
            return pe_ev, p.ev("dve")
        p.do("scalar", lambda e: e.activation(dst, srcv, AF.Identity), inc="act")
        return pe_ev, p.ev("act")

    def dt_chain(p, dd, par, st, fill=None):
        sb = sml[par]
        tri = sbm["triu"] if dd == "f" else sbm["tril"]
        def drain(n):
            if fill:
                for _ in range(min(n, len(fill))):
                    fill.pop(0)()
        p.do("vector", lambda e: e.tensor_add(sb["sm1"][:], sb["dtraw"][:],
                                              sbm[f"dtb_{dd}"][:]), inc="dve")
        t1_ev = p.ev("dve")
        p.wait("scalar", t1_ev)
        p.do("scalar", lambda e: e.activation(sb["sm2"][:], sb["sm1"][:], AF.Abs), inc="act")
        p.do("scalar", lambda e: e.activation(sb["sm2"][:], sb["sm2"][:], AF.Exp, scale=-1.0), inc="act")
        p.do("scalar", lambda e: e.activation(sb["sm2"][:], sb["sm2"][:], AF.Ln, bias=1.0), inc="act")
        p.do("scalar", lambda e: e.activation(sb["dt"][:], sb["sm1"][:], AF.Relu), inc="act")
        sp_ev = p.ev("act")
        drain(2)
        p.wait("vector", sp_ev)
        p.do("vector", lambda e: e.tensor_add(sb["dt"][:], sb["dt"][:], sb["sm2"][:]), inc="dve")
        p.do("vector", lambda e: e.tensor_scalar_max(sb["dt"][:], sb["dt"][:], 1e-30), inc="dve")
        dt_ev = p.ev("dve")
        p.wait("scalar", dt_ev)
        p.do("scalar", lambda e: e.activation(sb["acum"][:, NH:2 * NH], sb["dt"][:], AF.Ln), inc="act")
        lnd_ev = p.ev("act")
        p.do("vector", lambda e: e.tensor_mul(sb["sm1"][:], sb["dt"][:], sbm[f"negA_{dd}"][:]), inc="dve")
        emit_hilo(p, sb["sm1"][:], sb["hilo"][:, 0:NH], sb["sm2"][:], sb["hilo"][:, NH:2 * NH])
        hl_ev = p.ev("dve")
        p.wait("tensor", hl_ev)
        p.wait("tensor", st.get("ps_s_g"))
        p.do("tensor", lambda e: e.matmul(ps_s[:, 0:NH], tri[:], sb["hilo"][:, 0:NH],
                                          start=True, stop=False))
        p.do("tensor", lambda e: e.matmul(ps_s[:, 0:NH], tri[:], sb["hilo"][:, NH:2 * NH],
                                          start=False, stop=True), inc="pe")
        p.do("tensor", lambda e: e.matmul(ps_s[0:1, 32:32 + NH], sbm["onescol"][:, 0:1],
                                          sb["hilo"][:, 0:NH], start=True, stop=False))
        p.do("tensor", lambda e: e.matmul(ps_s[0:1, 32:32 + NH], sbm["onescol"][:, 0:1],
                                          sb["hilo"][:, NH:2 * NH], start=False, stop=True), inc="pe")
        cs_ev = p.ev("pe")
        drain(2)
        p.wait("vector", cs_ev)
        p.do("vector", lambda e: e.tensor_copy(sb["acum"][:, 0:NH], ps_s[:, 0:NH]), inc="dve")
        p.do("vector", lambda e: e.tensor_copy(sb["trow"][:], ps_s[0:1, 32:32 + NH]), inc="dve")
        emit_hilo(p, sb["trow"][:], sb["tot_hl"][0:1, 0:NH], sb["sm2"][0:1, 0:NH],
                  sb["tot_hl"][0:1, NH:2 * NH])
        th_ev = p.ev("dve")
        p.wait("tensor", th_ev)
        p.do("tensor", lambda e: e.matmul(ps_s[:, 64:64 + NH], sbm["onesbf"][0:1, :],
                                          sb["tot_hl"][0:1, 0:NH], start=True, stop=False))
        p.do("tensor", lambda e: e.matmul(ps_s[:, 64:64 + NH], sbm["onesbf"][0:1, :],
                                          sb["tot_hl"][0:1, NH:2 * NH], start=False, stop=True), inc="pe")
        tb_ev = p.ev("pe")
        drain(2)
        p.wait("vector", tb_ev)
        p.wait("vector", lnd_ev)
        p.do("vector", lambda e: e.tensor_copy(sb["totbc"][:], ps_s[:, 64:64 + NH]), inc="dve")
        p.do("vector", lambda e: e.tensor_sub(sb["dstate"][:], sb["totbc"][:], sb["acum"][:, 0:NH]), inc="dve")
        ds_ev = p.ev("dve")
        p.wait("scalar", ds_ev)
        p.do("scalar", lambda e: e.activation(sb["dstate"][:], sb["dstate"][:], AF.Exp), inc="act")
        p.do("scalar", lambda e: e.activation(sb["exptot"][:], sb["totbc"][:], AF.Exp), inc="act")
        p.do("scalar", lambda e: e.activation(sb["pk96"][:, 64:96], sb["acum"][:, 0:NH], AF.Exp), inc="act")
        ex_ev = p.ev("act")
        drain(1)
        p.wait("vector", ex_ev)
        p.do("vector", lambda e: e.tensor_mul(sb["dtds"][:], sb["dt"][:], sb["dstate"][:]), inc="dve")
        p.do("vector", lambda e: e.tensor_copy(sb["bstg"][:], sb["acum"][:, 0:NH]), inc="dve")
        p.do("vector", lambda e: e.tensor_copy(sb["pk96"][:, 0:32], sb["bstg"][:]), inc="dve")
        p.do("vector", lambda e: e.tensor_sub(sb["pk96"][:, 32:64], sb["acum"][:, 0:NH],
                                              sb["pk96"][:, 0:32]), inc="dve")
        p.do("vector", lambda e: e.tensor_sub(sb["vtok"][:], sb["acum"][:, NH:2 * NH],
                                              sb["acum"][:, 0:NH]), inc="dve")
        p.do("vector", lambda e: e.tensor_copy(sb["bstg"][:], sb["vtok"][:]), inc="dve")
        p.do("vector", lambda e: e.tensor_copy(sb["pk64"][:, 0:32], sb["bstg"][:]), inc="dve")
        p.do("vector", lambda e: e.tensor_sub(sb["pk64"][:, 32:64], sb["vtok"][:],
                                              sb["pk64"][:, 0:32]), inc="dve")
        pkd_ev = p.ev("dve")
        p.wait("tensor", pkd_ev)
        p.do("tensor", lambda e: e.transpose(ps_s[0:96, 128:256], sb["pk96"][:, 0:96],
                                             sbm["identf"][:]), inc="pe")
        p.do("tensor", lambda e: e.transpose(ps_s[0:64, 256:384], sb["pk64"][:, 0:64],
                                             sbm["identf"][:]), inc="pe")
        tr_ev = p.ev("pe")
        drain(1)
        p.wait("vector", tr_ev)
        p.do("vector", lambda e: e.tensor_copy(sb["flatpk"][0:96, :], ps_s[0:96, 128:256]), inc="dve")
        p.do("vector", lambda e: e.tensor_copy(sb["lhsT_mask"][0:64, :], ps_s[0:64, 256:384]), inc="dve")
        st["ps_s_g"] = p.ev("dve")
        drain(8)
        return p.ev("dve")

    def conv_batches(p, dd, halo, out_buf):
        wv = sbm[f"convw_{dd}"]
        offs = [13, 14, 15, 16] if dd == "f" else [19, 18, 17, 16]
        def hs(i, k):
            return halo[:, i * HSTR + offs[k]:i * HSTR + offs[k] + 128]
        evs = {}
        def tap(i0, i1, k, first):
            def go():
                for i in range(i0, i1):
                    if k == 0:
                        p.do("vector", lambda e, i=i: e.tensor_scalar(
                            out_buf[:, i * 128:(i + 1) * 128], hs(i, 0),
                            wv[:, i * 4:i * 4 + 1], sbm[f"convb_{dd}"][:, i:i + 1],
                            ALU.mult, ALU.add), inc="dve", chain=(first and i == i0))
                    else:
                        p.do("vector", lambda e, i=i, k=k: e.scalar_tensor_tensor(
                            out_buf[:, i * 128:(i + 1) * 128], hs(i, k),
                            wv[:, i * 4 + k:i * 4 + k + 1],
                            out_buf[:, i * 128:(i + 1) * 128],
                            ALU.mult, ALU.add), inc="dve", chain=False)
                if k == 3:
                    evs[i0] = p.ev("dve")
            return go
        batches = [tap(0, 8, k, True) for k in range(4)] +                   [tap(8, 18, k, False) for k in range(4)]
        def finish():
            out = []
            for i0, i1 in ((0, 8), (8, 18)):
                p.wait("scalar", evs[i0])
                p.do("scalar", lambda e, i0=i0, i1=i1: e.activation(
                    out_buf[:, i0 * 128:i1 * 128], out_buf[:, i0 * 128:i1 * 128],
                    AF.Silu), inc="act")
                out.append(p.ev("act"))
            return out
        return batches, finish

    def back_ssd(p, dd, seq, c, par, w, st, fev, first):
        sb = sml[par]
        tri = sbm["triu"] if dd == "f" else sbm["tril"]
        convT = w["convT0"] if par == 0 else w["convT1"]
        sq = seq % 2
        Sb = w["S0"] if sq == 0 else w["S1"]
        stilb = w["stil0"] if sq == 0 else w["stil1"]
        szb = fev["szbuf"]
        psyg = st["psyg"]
        p.wait("sync", fev["flat"])
        p.wait("sync", st.get("mask_pe"))
        flview = sp_flat.ap()[seq, c].rearrange("(a b) t -> a (b t)", a=3)
        p.do("sync", lambda e: e.dma_start(sbm["rhs_mask"][64:66, :], flview[0:2, :]), inc="dld", n=16)
        p.do("sync", lambda e: e.dma_start(sbm["expR"][:], flview[2:3, :]), inc="dld", n=16)
        fl_ev = p.ev("dld")
        for q in range(2):
            pw = [fev["conv"][q], psyg[q * 2], psyg[q * 2 + 1]]
            dw = [st.get("states_pe"), st.get("xdec_dve")] if q == 0 else []
            _, xe = pe_tp(p, [convT[:, (q * 8 + i) * 128:(q * 8 + i + 1) * 128]
                              for i in range(8)],
                          ps_y[:, q * 1024:(q + 1) * 1024].bitcast(BF16),
                          w["xstok"][:, q * 1024:(q + 1) * 1024],
                          pre_waits_pe=pw, pre_waits_dve=dw)
            psyg[q * 2] = xe
            psyg[q * 2 + 1] = xe
        _, be = pe_tp(p, [convT[:, 16 * 128:17 * 128]],
                      ps_x[:].bitcast(BF16)[:, 0:128], w["btok"][:],
                      pre_waits_pe=(st.get("gt_dve"), fev["conv"][1]))
        tr_ev = p.ev("dve")
        p.wait("tensor", fev["conv"][1])
        p.wait("tensor", tr_ev)
        p.do("tensor", lambda e: e.matmul(ps_x[:, 0:128], convT[:, 16 * 128:17 * 128],
                                          convT[:, 17 * 128:18 * 128], start=True, stop=True), inc="pe")
        g_ev = p.ev("pe")
        p.wait("vector", g_ev)
        p.do("vector", lambda e: e.tensor_mul(w["gt"][:], ps_x[:, 0:128], tri[:]), inc="dve")
        st["gt_dve"] = p.ev("dve")
        gt_ev = st["gt_dve"]
        first_exp = True
        cg_h = [None, None]
        for half in range(2):
            for jj in range(4):
                j = half * 4 + jj
                if jj == 0:
                    p.wait("tensor", fl_ev)
                    p.wait("tensor", fev["vrows"])
                p.wait("tensor", psyg[jj])
                p.do("tensor", lambda e, j=j, jj=jj, par=par: e.matmul(
                    ps_y[:, jj * 512:(jj + 1) * 512], sml[par]["lhsT_mask"][0:66, :],
                    sbm["rhs_mask"][0:66, j * 512:(j + 1) * 512],
                    start=True, stop=True), inc="pe")
                m_ev = p.ev("pe")
                p.wait("scalar", m_ev)
                if first_exp:
                    p.wait("scalar", st.get("ydyo_pe"))
                    p.wait("scalar", st.get("gm_dve"))
                    first_exp = False
                p.do("scalar", lambda e, j=j, jj=jj: e.activation(
                    w["gmm"][:, j * 512:(j + 1) * 512], ps_y[:, jj * 512:(jj + 1) * 512],
                    AF.Exp), inc="act", chain=(j == 0))
                psyg[jj] = p.ev("act")
            me_h = p.ev("act")
            p.wait("vector", me_h)
            p.do("vector", lambda e, half=half: e.scalar_tensor_tensor(
                w["gmm"][:, half * 2048:(half + 1) * 2048].rearrange("p (a b) -> p a b", a=16),
                w["gmm"][:, half * 2048:(half + 1) * 2048].rearrange("p (a b) -> p a b", a=16),
                CAP, w["gt"][:, None, :].broadcast_to([128, 16, 128]),
                ALU.min, ALU.mult), inc="dve")
            cg_h[half] = p.ev("dve")
        gv_h = [None, None]
        for half in range(2):
            for jj in range(4):
                j = half * 4 + jj
                p.wait("tensor", psyg[jj])
                p.do("tensor", lambda e, j=j, jj=jj: e.matmul(
                    ps_y[:, jj * 512:(jj + 1) * 512], sbm["onesbf"][0:1, :],
                    sbm["expR"][0:1, j * 512:(j + 1) * 512], start=True, stop=True), inc="pe")
                v_ev = p.ev("pe")
                p.wait("scalar", v_ev)
                p.do("scalar", lambda e, j=j, jj=jj: e.activation(
                    w["gmv"][:, j * 512:(j + 1) * 512], ps_y[:, jj * 512:(jj + 1) * 512],
                    AF.Identity), inc="act", chain=False)
                psyg[jj] = p.ev("act")
            ve_h = p.ev("act")
            p.wait("vector", ve_h)
            p.do("vector", lambda e, half=half: e.tensor_mul(
                w["gmv"][:, half * 2048:(half + 1) * 2048].rearrange("p (a b) -> p a b", a=16),
                w["gmv"][:, half * 2048:(half + 1) * 2048].rearrange("p (a b) -> p a b", a=16),
                convT[:, None, 17 * 128:18 * 128].broadcast_to([128, 16, 128])), inc="dve")
            gv_h[half] = p.ev("dve")
        st["gm_dve"] = p.ev("dve")
        gm_ev = st["gm_dve"]
        p.wait("vector", fev["vrd"])
        p.wait("vector", tr_ev)
        p.wait("vector", st.get("states_pe"))
        p.do("vector", lambda e: e.tensor_mul(
            w["xdec"][:].rearrange("p (a b) -> p a b", a=NH),
            w["xstok"][:].rearrange("p (a b) -> p a b", a=NH),
            bview(sb["dtds"][:], HD)), inc="dve")
        st["xdec_dve"] = p.ev("dve")
        xd_ev = st["xdec_dve"]
        p.do("vector", lambda e: e.tensor_mul(
            w["dx"][:].rearrange("p (a b) -> p a b", a=NH),
            w["xstok"][:].rearrange("p (a b) -> p a b", a=NH),
            bview(sbm[f"Dbc_{dd}"][:], HD)), inc="dve", chain=False)
        dx_ev = p.ev("dve")
        p.wait("tensor", tr_ev)
        scq_prev = st.get(("scan_q", sq)) or [None] * 4
        yq_ev = {}
        for q in range(4):
            p.wait("tensor", psyg[q])
            p.wait("tensor", cg_h[q // 2])
            p.wait("tensor", gv_h[q // 2])
            p.wait("tensor", scq_prev[q])
            for hh in range(8):
                h = q * 8 + hh
                p.do("tensor", lambda e, h=h, q=q, hh=hh: e.matmul(
                    ps_y[:, h * 64:(h + 1) * 64], w["gmm"][:, h * 128:(h + 1) * 128],
                    w["xstok"][:, h * 64:(h + 1) * 64], start=True, stop=False,
                    skip_group_check=True))
                p.do("tensor", lambda e, h=h, q=q, hh=hh, stilb=stilb: e.matmul(
                    ps_y[:, h * 64:(h + 1) * 64], w["gmv"][:, h * 128:(h + 1) * 128],
                    stilb[:, h * 64:(h + 1) * 64], start=False, stop=True,
                    skip_group_check=True), inc="pe")
            q_pe = p.ev("pe")
            p.wait("scalar", q_pe)
            if q == 0:
                p.wait("scalar", st.get("y0_free"))
                p.wait("scalar", st.get("yread_dve"))
            p.do("scalar", lambda e, q=q: e.activation(
                w["y0"][:, q * 512:(q + 1) * 512], ps_y[:, q * 512:(q + 1) * 512],
                AF.Identity), inc="act", chain=(q == 0))
            psyg[q] = p.ev("act")
            yq_ev[q] = psyg[q]
        st["ydyo_pe"] = p.ev("pe")
        ye_ev = p.ev("act")
        p.wait("tensor", xd_ev)
        stq = []
        for q in range(4):
            p.wait("tensor", psyg[q])
            p.do("tensor", lambda e, q=q: e.matmul(
                ps_y[:, q * 512:(q + 1) * 512], w["btok"][:],
                w["xdec"][:, q * 512:(q + 1) * 512], start=True, stop=True), inc="pe")
            stq.append(p.ev("pe"))
        st["states_pe"] = p.ev("pe")
        st_ev = st["states_pe"]
        p.wait("gpsimd", fev["exp"])
        saq, scq = [None] * 4, [None] * 4
        for q in range(4):
            p.wait("gpsimd", scq_prev[q])
            p.do("gpsimd", lambda e, q=q, Sb=Sb: e.tensor_mul(
                Sb[:, q * 512:(q + 1) * 512].rearrange("p (a b) -> p a b", a=8),
                Sb[:, q * 512:(q + 1) * 512].rearrange("p (a b) -> p a b", a=8),
                bview(sb["exptot"][:, q * 8:(q + 1) * 8], HD)), inc="pool")
            smq = p.ev("pool")
            p.wait("vector", smq)
            p.wait("vector", stq[q])
            p.do("vector", lambda e, q=q, Sb=Sb: e.tensor_add(
                Sb[:, q * 512:(q + 1) * 512], Sb[:, q * 512:(q + 1) * 512],
                ps_y[:, q * 512:(q + 1) * 512]), inc="dve")
            saq[q] = p.ev("dve")
            p.wait("scalar", saq[q])
            p.do("scalar", lambda e, q=q, Sb=Sb, stilb=stilb: e.activation(
                stilb[:, q * 512:(q + 1) * 512],
                Sb[:, q * 512:(q + 1) * 512], AF.Identity), inc="act", chain=False)
            scq[q] = p.ev("act")
            psyg[q] = saq[q]
        st[("sadd_q", sq)], st[("scan_q", sq)] = saq, scq
        p.wait("vector", ye_ev)
        p.wait("vector", dx_ev)
        for half in range(2):
            p.do("vector", lambda e, half=half: e.tensor_add(
                w["y0"][:, half * 1024:(half + 1) * 1024],
                w["y0"][:, half * 1024:(half + 1) * 1024],
                w["dx"][:, half * 1024:(half + 1) * 1024]), inc="dve",
                chain=(half == 0))
        p.wait("vector", fev["sz"])
        for half in range(2):
            p.do("vector", lambda e, half=half, szb=szb: e.tensor_mul(
                w["dx"][:, half * 1024:(half + 1) * 1024],
                w["y0"][:, half * 1024:(half + 1) * 1024],
                szb[:, half * 1024:(half + 1) * 1024]), inc="dve")
        y2_ev = p.ev("dve")
        p.wait("scalar", y2_ev)
        for half in range(2):
            p.do("scalar", lambda e, half=half: e.activation(
                sbm["scrA"][:, 0:1024], w["dx"][:, half * 1024:(half + 1) * 1024],
                AF.Square, accum_out=sbm["col1"][:, 8 + half:9 + half]), inc="act")
        sq_ev = p.ev("act")
        p.wait("vector", sq_ev)
        p.do("vector", lambda e: e.tensor_add(sbm["col1"][:, 10:11], sbm["col1"][:, 8:9],
                                              sbm["col1"][:, 9:10]), inc="dve")
        p.do("vector", lambda e: e.tensor_scalar(sbm["col1"][:, 11:12], sbm["col1"][:, 10:11],
                                                 1.0 / DI, None, ALU.mult), inc="dve")
        ms_ev = p.ev("dve")
        p.wait("scalar", ms_ev)
        p.do("scalar", lambda e: e.activation(sbm["col1"][:, 12:13], sbm["col1"][:, 11:12],
                                              AF.Ln, bias=EPS), inc="act")
        p.do("scalar", lambda e: e.activation(sbm["col1"][:, 13:14], sbm["col1"][:, 12:13],
                                              AF.Exp, scale=-0.5), inc="act")
        rs_ev = p.ev("act")
        p.wait("vector", rs_ev)
        p.do("vector", lambda e: e.scalar_tensor_tensor(
            w["y0"][:], w["dx"][:], sbm["col1"][:, 13:14], sbm[f"nwbc_{dd}"][:],
            ALU.mult, ALU.mult), inc="dve")
        y_ev = p.ev("dve")
        st["yread_dve"] = y_ev
        st["mask_pe"] = st["states_pe"]
        return y_ev

    def front_a(p, seq, c, par, st):
        sb = sml[par]
        fev = {}
        xbuf = wa["xA0"] if par == 0 else wa["xA1"]
        bend = st["bend"].get(par)
        for e_ in ("vector", "scalar", "sync", "tensor", "gpsimd"):
            p.wait_snap(e_, bend)
        p.do("sync", lambda e, seq=seq, c=c, xbuf=xbuf: e.dma_start(
            xbuf[:], x_in.ap()[seq, c * CK:(c + 1) * CK, :]), inc="dld", n=16)
        xld = p.ev("dld")
        p.wait("vector", xld)
        p.wait("scalar", xld)
        p.do("scalar", lambda e, xbuf=xbuf: e.activation(
            sbm["scrA"][:, 0:1024], xbuf[:], AF.Square,
            accum_out=sbm["col1"][:, 0:1]), inc="act")
        sq_ev = p.ev("act")
        p.do("vector", lambda e, xbuf=xbuf: e.reduce_sum(sbm["col1"][:, 1:2], xbuf[:], AX.X), inc="dve")
        p.wait("vector", sq_ev)
        p.do("vector", lambda e: e.tensor_scalar(sbm["col1"][:, 2:3], sbm["col1"][:, 0:1],
                                                 1.0 / D, None, ALU.mult), inc="dve")
        p.do("vector", lambda e: e.tensor_scalar(sbm["col1"][:, 3:4], sbm["col1"][:, 1:2],
                                                 1.0 / D, None, ALU.mult), inc="dve")
        p.do("vector", lambda e: e.tensor_mul(sbm["col1"][:, 4:5], sbm["col1"][:, 3:4],
                                              sbm["col1"][:, 3:4]), inc="dve")
        p.do("vector", lambda e: e.tensor_sub(sbm["col1"][:, 5:6], sbm["col1"][:, 2:3],
                                              sbm["col1"][:, 4:5]), inc="dve")
        var_ev = p.ev("dve")
        p.wait("scalar", var_ev)
        p.do("scalar", lambda e: e.activation(sbm["col1"][:, 6:7], sbm["col1"][:, 5:6],
                                              AF.Ln, bias=EPS), inc="act")
        p.do("scalar", lambda e: e.activation(sbm["col1"][:, 7:8], sbm["col1"][:, 6:7],
                                              AF.Exp, scale=-0.5), inc="act")
        rstd_ev = p.ev("act")
        p.wait("vector", rstd_ev)
        p.wait("vector", st.get("xn_free"))
        p.wait("vector", st.get("xn_free_sp"))
        p.do("vector", lambda e, xbuf=xbuf: e.scalar_tensor_tensor(
            wa["xn"][:], xbuf[:], sbm["col1"][:, 3:4],
            sbm["col1"][:, 7:8].broadcast_to([128, D]),
            ALU.subtract, ALU.mult), inc="dve")
        xn_ev = p.ev("dve")
        p.wait("scalar", xn_ev)
        p.do("scalar", lambda e, seq=seq, c=c: e.dma_start(
            sp_xn.ap()[seq, c], wa["xn"][:]), inc="dsp", n=16)
        xnsp_ev = p.ev("dsp")
        st["xn_free_sp"] = xnsp_ev
        psv = ps_s[:].bitcast(BF16)
        _, xnT_ev = pe_tp(p, [wa["xn"][:, k * 128:(k + 1) * 128] for k in range(8)],
                          psv, wa["xnT"][:],
                          pre_waits_pe=(xn_ev, st.get("ps_s_g"), st.get("inproj_pe")),
                          pre_waits_dve=(st.get("inproj_pe"),))
        st["ps_s_g"] = xnT_ev
        st["xn_free"] = p.ev("pe")
        szbuf = wa["siluz0"] if par == 0 else wa["siluz1"]
        p.wait("tensor", xnT_ev)
        jev = st.get("jev", {})
        halo_evs = []
        for j in range(9):
            psl = ps_a[:, (j % 2) * 512:(j % 2) * 512 + 512]
            ncols = 512 if j < 8 else 288
            p.wait("tensor", jev.get(j % 2))
            for k in range(8):
                p.do("tensor", lambda e, j=j, k=k, psl=psl, ncols=ncols: e.matmul(
                    psl[:, 0:ncols], wa["xnT"][:, k * 128:(k + 1) * 128],
                    wa["w_inT"][:, k * DIP + j * 512:k * DIP + j * 512 + ncols],
                    start=(k == 0), stop=(k == 7)), inc=("pe" if k == 7 else None))
            mm_ev = p.ev("pe")
            if j < 4:
                p.wait("scalar", mm_ev)
                if j == 0:
                    p.wait("scalar", st.get("gate_dve"))
                p.do("scalar", lambda e, j=j, psl=psl, szbuf=szbuf: e.activation(
                    szbuf[:, j * 512:(j + 1) * 512], psl[:], AF.Identity), inc="act",
                    chain=(j == 0))
                jev[j % 2] = p.ev("act")
            else:
                p.wait("scalar", mm_ev)
                nxb = 512 if j < 8 else 256
                stg = wa["stage"][:, (j % 4) * 512:(j % 4) * 512 + 512]
                p.wait("scalar", st.get(("stg", j % 4)))
                p.do("scalar", lambda e, psl=psl, nxb=nxb, stg=stg: e.activation(
                    stg[:, 0:nxb], psl[:, 0:nxb], AF.Identity), inc="act", chain=False)
                if j == 8:
                    p.do("scalar", lambda e, psl=psl, sb=sb: e.activation(
                        sb["dtraw"][:], psl[:, 256:288], AF.Identity), inc="act", chain=False)
                st_ev = p.ev("act")
                jev[j % 2] = st_ev
                pw = [st_ev, st.get("ps_s_g")]
                dw = []
                if j == 4:
                    dw = [st.get("halo_free"), st.get("halo_free_cv"),
                          st.get("halo_free_tl")]
                nt = nxb // 128
                i0 = (j - 4) * 4
                hdst = wa["halo"][:].rearrange("p (i s) -> p i s", i=18)[
                    :, i0:i0 + nt, HDAT:HDAT + 128]
                _, hev = pe_tp(p, [stg[:, bb * 128:(bb + 1) * 128] for bb in range(nt)],
                               ps_s[:].bitcast(BF16), hdst,
                               pre_waits_pe=pw, pre_waits_dve=dw, eng="scalar")
                st["ps_s_g"] = hev
                st[("stg", j % 4)] = p.ev("pe")
                halo_evs.append(hev)
        st["jev"] = jev
        st["inproj_pe"] = p.ev("pe")
        xbcT_ev = halo_evs[-1]
        convT = wa["convT0"] if par == 0 else wa["convT1"]
        p.wait("vector", xbcT_ev)
        p.wait("vector", p.ev("act"))
        tl = wa["tail0"] if seq % 2 == 0 else wa["tail1"]
        if c == 0:
            p.do("vector", lambda e, tl=tl: e.memset(tl[:], 0.0), inc="dve")
            p.do("vector", lambda e: e.memset(
                wa["halo"][:].rearrange("p (i s) -> p i s", i=18)[:, :, 0:16], 0.0), inc="dve")
            p.do("vector", lambda e: e.memset(
                wa["halo"][:].rearrange("p (i s) -> p i s", i=18)[:, :, 144:160], 0.0), inc="dve")
        p.do("vector", lambda e, tl=tl: e.tensor_copy(
            wa["halo"][:].rearrange("p (i s) -> p i s", i=18)[:, :, 13:16],
            tl[:].rearrange("p (i s) -> p i s", i=18)[:, :, 1:4]), inc="dve")
        batches, cfin = conv_batches(p, "f", wa["halo"], convT)
        vrd = dt_chain(p, "f", par, st, fill=batches)
        fev["vrd"] = vrd
        fev["vrows"] = vrd
        fev["exp"] = p.ev("act")
        fev["conv"] = cfin()
        p.do("scalar", lambda e, szbuf=szbuf: e.activation(
            szbuf[:], szbuf[:], AF.Silu), inc="act")
        fev["sz"] = sz_ev = p.ev("act")
        p.wait("scalar", sz_ev)
        p.do("scalar", lambda e, seq=seq, c=c, szbuf=szbuf: e.dma_start(
            sp_sz.ap()[seq, c], szbuf[:]), inc="dsp", n=16)
        p.do("vector", lambda e, tl=tl: e.tensor_copy(
            tl[:].rearrange("p (i s) -> p i s", i=18)[:, :, 1:4],
            wa["halo"][:].rearrange("p (i s) -> p i s", i=18)[:, :, 141:144]), inc="dve")
        tail_ev = p.ev("dve")
        p.wait("scalar", vrd)
        p.do("scalar", lambda e, seq=seq, c=c, sb=sb: e.dma_start(
            sp_flat.ap()[seq, c], sb["flatpk"][0:96, :]), inc="dsp", n=16)
        fev["flat"] = p.ev("dsp")
        p.wait("scalar", xbcT_ev)
        p.do("scalar", lambda e, seq=seq, c=c: e.dma_start(
            sp_xbc.ap()[seq, c], wa["halo"][:]), inc="dsp", n=16)
        hsp_ev = p.ev("dsp")
        p.do("scalar", lambda e, seq=seq, c=c, sb=sb: e.dma_start(
            sp_dt.ap()[seq, c], sb["dtraw"][:]), inc="dsp", n=16)
        st["halo_free"] = hsp_ev
        st["halo_free_cv"] = fev["conv"][1]
        st["halo_free_tl"] = tail_ev
        fev["szbuf"] = szbuf
        return fev

    def back_a(p, seq, c, par, st, fev):
        y_ev = back_ssd(p, "f", seq, c, par, wa, st, fev, c == 0)
        p.wait("scalar", y_ev)
        p.do("scalar", lambda e, seq=seq, c=c: e.dma_start(
            sp_yf.ap()[seq, c], wa["y0"][:]), inc="dsp", n=16)
        st["y0_free"] = p.ev("dsp")
        st["bend"][par] = p.snap()

    def front_b1(p, seq, c, par, st):
        sb = sml[par]
        fev = {}
        bend = st["bend"].get(par)
        for e_ in ENGINES:
            p.wait_snap(e_, bend)
        halo = wb["haloB0"] if par == 0 else wb["haloB1"]
        szb = wb["szB0"] if par == 0 else wb["szB1"]
        yfb = wb["yfB0"] if par == 0 else wb["yfB1"]
        p.do("sync", lambda e, seq=seq, c=c, halo=halo: e.dma_start(
            halo[:], sp_xbc.ap()[seq, c]), inc="dld", n=16)
        p.do("sync", lambda e, seq=seq, c=c, sb=sb: e.dma_start(
            sb["dtraw"][:], sp_dt.ap()[seq, c]), inc="dld", n=16)
        p.do("sync", lambda e, seq=seq, c=c, szb=szb: e.dma_start(
            szb[:], sp_sz.ap()[seq, c]), inc="dld", n=16)
        p.do("sync", lambda e, seq=seq, c=c, yfb=yfb: e.dma_start(
            yfb[:], sp_yf.ap()[seq, c]), inc="dld", n=16)
        ld_ev = p.ev("dld")
        fev["ld"] = ld_ev
        p.wait("vector", ld_ev)
        tlr = wb["tailR0"] if seq % 2 == 0 else wb["tailR1"]
        if c == NCK - 1:
            p.do("vector", lambda e, tlr=tlr: e.memset(tlr[:], 0.0), inc="dve")
        p.do("vector", lambda e, halo=halo, tlr=tlr: e.tensor_copy(
            halo[:].rearrange("p (i s) -> p i s", i=18)[:, :, 144:147],
            tlr[:].rearrange("p (i s) -> p i s", i=18)[:, :, 1:4]), inc="dve")
        p.do("vector", lambda e, halo=halo, tlr=tlr: e.tensor_copy(
            tlr[:].rearrange("p (i s) -> p i s", i=18)[:, :, 1:4],
            halo[:].rearrange("p (i s) -> p i s", i=18)[:, :, 16:19]), inc="dve")
        convT = wb["convT0"] if par == 0 else wb["convT1"]
        batches, cfin = conv_batches(p, "r", halo, convT)
        vrd = dt_chain(p, "r", par, st, fill=batches)
        fev["vrd"] = vrd
        fev["vrows"] = vrd
        fev["exp"] = p.ev("act")
        fev["conv"] = cfin()
        p.wait("scalar", vrd)
        p.do("scalar", lambda e, seq=seq, c=c, sb=sb: e.dma_start(
            sp_flat.ap()[seq, c], sb["flatpk"][0:96, :]), inc="dsp", n=16)
        fev["flat"] = p.ev("dsp")
        fev["sz"] = ld_ev
        fev["szbuf"] = szb
        fev["yfbuf"] = yfb
        return fev

    def back_b1(p, seq, c, par, st, fev):
        y_ev = back_ssd(p, "r", seq, c, par, wb, st, fev, c == NCK - 1)
        p.wait("vector", fev["ld"])
        p.do("vector", lambda e, yfb=fev["yfbuf"]: e.tensor_add(
            wb["y0"][:], wb["y0"][:], yfb[:]), inc="dve")
        yc_ev = p.ev("dve")
        st["yread_dve"] = yc_ev
        yes = []
        for q in range(2):
            pw = [yc_ev, st.get("sevac_dve")] if q == 0 else [yc_ev]
            _, ye_ = pe_tp(p, [wb["y0"][:, (q * 8 + i) * 128:(q * 8 + i + 1) * 128]
                               for i in range(8)],
                           ps_a[:, q * 512:(q + 1) * 512].bitcast(BF16),
                           wb["ycombT"][:, q * 1024:(q + 1) * 1024],
                           pre_waits_pe=pw, eng="scalar")
            yes.append(ye_)
        st["y0_free"] = p.ev("pe")
        p.wait("tensor", yes[0])
        p.wait("tensor", yes[1])
        for j in range(2):
            for k in range(16):
                p.do("tensor", lambda e, j=j, k=k: e.matmul(
                    ps_a[:, j * 512:(j + 1) * 512], wb["ycombT"][:, k * 128:(k + 1) * 128],
                    wb["w_outT"][:, k * D + j * 512:k * D + j * 512 + 512],
                    start=(k == 0), stop=(k == 15)), inc=("pe" if k == 15 else None))
        st["op_pe"] = p.ev("pe")
        p.wait("vector", st["op_pe"])
        p.wait("vector", st.get("s_spill"))
        p.do("vector", lambda e: e.tensor_scalar(wb["sevac"][:], ps_a[:], 0.5, None,
                                                 ALU.mult), inc="dve")
        st["sevac_dve"] = p.ev("dve")
        p.wait("scalar", st["sevac_dve"])
        p.do("scalar", lambda e, seq=seq, c=c: e.dma_start(
            sp_s.ap()[seq, c], wb["sevac"][:]), inc="dsp", n=16)
        st["s_spill"] = p.ev("dsp")
        st["bend"][par] = p.snap()

    def front_b2(p, seq, c, par, st):
        fev = {}
        bend = st["bend"].get(par)
        for e_ in ENGINES:
            p.wait_snap(e_, bend)
        xb = wc["xC0"] if par == 0 else wc["xC1"]
        sb_ = wc["sC0"] if par == 0 else wc["sC1"]
        xnT = wc["xnTC0"] if par == 0 else wc["xnTC1"]
        p.wait("sync", st.get("xnC_free"))
        p.do("sync", lambda e, seq=seq, c=c: e.dma_start(
            wc["xnC"][:], sp_xn.ap()[seq, c]), inc="dld", n=16)
        ld1 = p.ev("dld")
        p.do("sync", lambda e, seq=seq, c=c, xb=xb: e.dma_start(
            xb[:], x_in.ap()[seq, c * CK:(c + 1) * CK, :]), inc="dld", n=16)
        p.do("sync", lambda e, seq=seq, c=c, sb_=sb_: e.dma_start(
            sb_[:], sp_s.ap()[seq, c]), inc="dld", n=16)
        ld_ev = p.ev("dld")
        psv = ps_s[:].bitcast(BF16)
        _, xnT_ev2 = pe_tp(p, [wc["xnC"][:, k * 128:(k + 1) * 128] for k in range(8)],
                           psv, xnT[:],
                           pre_waits_pe=(ld1, st.get("ps_s_g2")))
        st["ps_s_g2"] = xnT_ev2
        st["xnC_free"] = p.ev("pe")
        fev["xnT_ev"] = xnT_ev2
        fev["ld"] = ld_ev
        fev["bufs"] = (xb, sb_, xnT)
        return fev

    def back_b2(p, seq, c, par, st, fev):
        xb, sb_, xnT = fev["bufs"]
        p.wait("tensor", fev["xnT_ev"])
        jev = st.get("jev2", {})
        for j in range(8):
            psl = ps_a[:, (j % 2) * 512:(j % 2) * 512 + 512]
            p.wait("tensor", jev.get(j % 2))
            for k in range(8):
                p.do("tensor", lambda e, j=j, k=k, psl=psl, xnT=xnT: e.matmul(
                    psl[:], xnT[:, k * 128:(k + 1) * 128],
                    wc["w1T"][:, k * 4 * D + j * 512:k * 4 * D + j * 512 + 512],
                    start=(k == 0), stop=False))
            p.do("tensor", lambda e, j=j, psl=psl: e.matmul(
                psl[:], sbm["onesbf"][0:1, :], wc["b1row"][0:1, j * 512:(j + 1) * 512],
                start=False, stop=True), inc="pe")
            mm_ev = p.ev("pe")
            p.wait("scalar", mm_ev)
            if j == 0:
                p.wait("scalar", st.get("hT_dtr"))
            p.do("scalar", lambda e, j=j, psl=psl: e.activation(
                wc["hbuf"][:, j * 512:(j + 1) * 512], psl[:], AF.Gelu), inc="act")
            jev[j % 2] = p.ev("act")
        st["jev2"] = jev
        h_ev = p.ev("act")
        hq = st.get("hq", {})
        for g in range(4):
            qq = 2 + (g % 2)
            pw = [h_ev, hq.get(qq)]
            _, he_ = pe_tp(p, [wc["hbuf"][:, (g * 8 + i) * 128:(g * 8 + i + 1) * 128]
                               for i in range(8)],
                           ps_y[:, qq * 512:(qq + 1) * 512].bitcast(BF16),
                           wc["hT"][:, g * 1024:(g + 1) * 1024],
                           pre_waits_pe=pw)
            hq[qq] = he_
        st["hq"] = hq
        hT_ev = p.ev("dve")
        st["hT_dtr"] = p.ev("pe")
        p.wait("tensor", hT_ev)
        p.wait("tensor", st.get("g_dve"))
        for j in range(2):
            for k in range(32):
                p.do("tensor", lambda e, j=j, k=k: e.matmul(
                    ps_y[:, j * 512:(j + 1) * 512], wc["hT"][:, k * 128:(k + 1) * 128],
                    wc["w2T"][:, k * D + j * 512:k * D + j * 512 + 512],
                    start=(k == 0), stop=False))
            p.do("tensor", lambda e, j=j: e.matmul(
                ps_y[:, j * 512:(j + 1) * 512], sbm["onesbf"][0:1, :],
                wc["b2row"][0:1, j * 512:(j + 1) * 512], start=False, stop=True), inc="pe")
        st["ffn2_pe"] = p.ev("pe")
        p.wait("scalar", st["ffn2_pe"])
        p.do("scalar", lambda e: e.activation(wc["thC"][:], ps_y[:, 0:1024], AF.Tanh,
                                              scale=0.5), inc="act")
        th_ev = p.ev("act")
        p.wait("vector", th_ev)
        p.wait("vector", fev["ld"])
        p.do("vector", lambda e: e.scalar_tensor_tensor(
            wc["gC"][:], wc["thC"][:], 1.0, ps_y[:, 0:1024], ALU.add, ALU.mult), inc="dve")
        st["g_dve"] = p.ev("dve")
        p.do("vector", lambda e, sb_=sb_: e.tensor_mul(sb_[:], sb_[:], wc["gC"][:]), inc="dve")
        p.wait("vector", st.get("out_spill"))
        p.do("vector", lambda e, xb=xb, sb_=sb_: e.tensor_add(wc["outC"][:], xb[:], sb_[:]), inc="dve")
        o_ev = p.ev("dve")
        p.wait("scalar", o_ev)
        p.do("scalar", lambda e, seq=seq, c=c: e.dma_start(
            out_t.ap()[seq, c * CK:(c + 1) * CK, :], wc["outC"][:]), inc="dsp", n=16)
        st["out_spill"] = p.ev("dsp")
        st["bend"][par] = p.snap()

    def prep(p: P):
        sb = sbm
        for dd in ("f", "r"):
            p.do("sync", lambda e, dd=dd: e.dma_start(
                sb[f"convw_{dd}"][:].rearrange("p (i k) -> p i k", k=4),
                pr[f"convw_{dd}"].ap().rearrange("(i p) k -> p i k", p=128)), inc="dld", n=16)
        p.do("sync", lambda e: e.dma_start(sb["triu"][:], c_triu.ap()), inc="dld", n=16)
        p.do("sync", lambda e: e.dma_start(sb["tril"][:], c_tril.ap()), inc="dld", n=16)
        p.do("sync", lambda e: e.dma_start(sb["identf"][:], c_identf.ap()), inc="dld", n=16)
        p.do("sync", lambda e: e.dma_start(sb["identb"][:], c_identb.ap()), inc="dld", n=16)
        p.do("sync", lambda e: e.dma_start(sb["rhs_mask"][0:64, :], c_E2.ap()), inc="dld", n=16)
        p.do("sync", lambda e: e.dma_start(sb["cbias"][:], c_bias.ap()), inc="dld", n=16)
        for di, dd in enumerate(("f", "r")):
            p.do("sync", lambda e, dd=dd: e.dma_start(
                sb["cbst"][0:18, :], pr[f"convb_{dd}"].ap().rearrange("(i p) -> i p", p=128)),
                inc="dld", n=16)
            cb_ld = p.ev("dld")
            p.wait("tensor", cb_ld)
            p.do("tensor", lambda e, di=di: e.transpose(
                ps_s[:, 384 + di * 32:384 + di * 32 + 32], sb["cbst"][0:32, :],
                sb["identf"][0:32, 0:32]), inc="pe")
            cb_tr = p.ev("pe")
            p.wait("vector", cb_tr)
            p.do("vector", lambda e, dd=dd, di=di: e.tensor_copy(
                sb[f"convb_{dd}"][:], ps_s[:, 384 + di * 32:384 + di * 32 + 18]), inc="dve")
            cb_cp = p.ev("dve")
            p.wait("sync", cb_cp)
        p.do("gpsimd", lambda e: e.dma_start(sb["rowf"][0:1, 0:32], pr["dtb_f"].ap()[None, :]), inc="dgp", n=16)
        p.do("gpsimd", lambda e: e.dma_start(sb["rowf"][0:1, 32:64], pr["dtb_r"].ap()[None, :]), inc="dgp", n=16)
        p.do("gpsimd", lambda e: e.dma_start(sb["rowf"][0:1, 64:96], pr["alog_f"].ap()[None, :]), inc="dgp", n=16)
        p.do("gpsimd", lambda e: e.dma_start(sb["rowf"][0:1, 96:128], pr["alog_r"].ap()[None, :]), inc="dgp", n=16)
        p.do("gpsimd", lambda e: e.dma_start(sb["rowD"][0:1, 0:32], pr["dsk_f"].ap()[None, :]), inc="dgp", n=16)
        p.do("gpsimd", lambda e: e.dma_start(sb["rowD"][0:1, 32:64], pr["dsk_r"].ap()[None, :]), inc="dgp", n=16)
        p.do("gpsimd", lambda e: e.dma_start(wp["wtmp0"][0:1, 0:DI], pr["nrm_f"].ap()[None, :]), inc="dgp", n=16)
        p.do("gpsimd", lambda e: e.dma_start(wp["wtmp1"][0:1, 0:DI], pr["nrm_r"].ap()[None, :]), inc="dgp", n=16)
        ld0 = p.ev("dgp")
        p.do("vector", lambda e: e.memset(sb["onesbf"][:], 1.0), inc="dve")
        p.do("vector", lambda e: e.memset(sb["onescol"][:], 1.0), inc="dve")
        for par in range(2):
            p.do("vector", lambda e, par=par: e.memset(sml[par]["lhsT_mask"][64:66, :], 1.0), inc="dve")
        p.wait("scalar", ld0)
        p.wait("scalar", p.ev("dld"))
        p.do("scalar", lambda e: e.activation(sb["rowf"][0:1, 64:128], sb["rowf"][0:1, 64:128],
                                              AF.Exp), inc="act")
        p.do("scalar", lambda e: e.mul(sb["rowf"][0:1, 64:128], sb["rowf"][0:1, 64:128], -1.0), inc="act")
        a_ev = p.ev("act")
        p.wait("vector", a_ev)
        emit_hilo(p, sb["rowf"][0:1, 0:128], sb["rowt"][0:1, 0:128], sb["rowf"][0:1, 128:256],
                  sb["rowt"][0:1, 128:256])
        hl_ev = p.ev("dve")
        p.wait("tensor", hl_ev)
        p.do("tensor", lambda e: e.matmul(ps_y[:, 0:128], sb["onesbf"][0:1, :],
                                          sb["rowt"][0:1, 0:128], start=True, stop=False))
        p.do("tensor", lambda e: e.matmul(ps_y[:, 0:128], sb["onesbf"][0:1, :],
                                          sb["rowt"][0:1, 128:256], start=False, stop=True), inc="pe")
        p.do("tensor", lambda e: e.matmul(ps_y[:, 128:192], sb["onesbf"][0:1, :],
                                          sb["rowD"][0:1, 0:64], start=True, stop=True), inc="pe")
        pb_ev = p.ev("pe")
        p.wait("vector", pb_ev)
        p.do("vector", lambda e: e.tensor_copy(sb["dtb_f"][:], ps_y[:, 0:32]), inc="dve")
        p.do("vector", lambda e: e.tensor_copy(sb["dtb_r"][:], ps_y[:, 32:64]), inc="dve")
        p.do("vector", lambda e: e.tensor_copy(sb["negA_f"][:], ps_y[:, 64:96]), inc="dve")
        p.do("vector", lambda e: e.tensor_copy(sb["negA_r"][:], ps_y[:, 96:128]), inc="dve")
        p.do("vector", lambda e: e.tensor_copy(sb["Dbc_f"][:], ps_y[:, 128:160]), inc="dve")
        p.do("vector", lambda e: e.tensor_copy(sb["Dbc_r"][:], ps_y[:, 160:192]), inc="dve")
        dcp_ev = p.ev("dve")
        p.wait("tensor", dcp_ev)
        for j in range(8):
            dd = "f" if j < 4 else "r"
            jj = j % 4
            wsrc_t = wp["wtmp0"] if j < 4 else wp["wtmp1"]
            p.do("tensor", lambda e, j=j, jj=jj, wsrc_t=wsrc_t: e.matmul(
                ps_y[:, 512 + (j % 2) * 512: 512 + (j % 2) * 512 + 512],
                sb["onesbf"][0:1, :],
                wsrc_t[0:1, jj * 512:(jj + 1) * 512],
                start=True, stop=True), inc="pe")
            nw_ev = p.ev("pe")
            p.wait("vector", nw_ev)
            p.do("vector", lambda e, dd=dd, jj=jj, j=j: e.tensor_copy(
                sb[f"nwbc_{dd}"][:, jj * 512:(jj + 1) * 512],
                ps_y[:, 512 + (j % 2) * 512:512 + (j % 2) * 512 + 512]), inc="dve")
            p.wait("tensor", p.ev("dve"))
        trx_prev = [Z]
        def tr8(wt, col0, ntile, rows, dst3, lev):
            psb = ps_x[:].bitcast(BF16)
            p.wait("tensor", lev)
            p.wait("tensor", trx_prev[0])
            for k in range(ntile):
                p.do("tensor", lambda e, k=k: e.transpose(
                    psb[:, k * 128:k * 128 + rows],
                    wt[0:rows, col0 + k * 128:col0 + (k + 1) * 128],
                    sbm["identb"][0:rows, 0:rows]),
                    inc=("pe" if k == ntile - 1 else None))
            tev = p.ev("pe")
            p.wait("vector", tev)
            p.do("vector", lambda e: e.tensor_copy(
                dst3, psb[:, 0:ntile * 128].rearrange(
                    "p (k c) -> p k c", k=ntile)[:, :, 0:rows]), inc="dve")
            trx_prev[0] = p.ev("dve")
            return tev, trx_prev[0]
        wfree = {}
        for jt in range(35):
            rows = 128 if jt < 34 else 32
            par = jt % 2
            wt = wp["wtmp0"] if par == 0 else wp["wtmp1"]
            p.wait("gpsimd", wfree.get(par, p.ev("pe") if jt < 2 else None))
            p.do("gpsimd", lambda e, jt=jt, wt=wt, rows=rows: e.dma_start(
                wt[0:rows, 0:1024], w_in.ap()[jt * 128:jt * 128 + rows, :]), inc="dgp", n=16)
            lev = p.ev("dgp")
            dst3 = wa["w_inT"][:].rearrange("p (k c) -> p k c", c=DIP)[
                :, :, jt * 128:jt * 128 + rows]
            tev, _ = tr8(wt, 0, 8, rows, dst3, lev)
            wfree[par] = tev
        for rt in range(8):
            par = rt % 2
            wt = wp["wtmp0"] if par == 0 else wp["wtmp1"]
            p.wait("gpsimd", wfree.get(par))
            p.wait("gpsimd", p.ev("dsp"))
            p.do("gpsimd", lambda e, rt=rt, wt=wt: e.dma_start(
                wt[:, 0:2048], w_out.ap()[rt * 128:(rt + 1) * 128, :]), inc="dgp", n=16)
            lev = p.ev("dgp")
            evs = []
            for hb in range(2):
                d3 = wp["trst"][:, hb * 1024:(hb + 1) * 1024].rearrange(
                    "p (k c) -> p k c", k=8)
                tev, dev = tr8(wt, hb * 1024, 8, 128, d3, lev)
                evs.append(dev)
            wfree[par] = tev
            p.wait("scalar", evs[1])
            for ct in range(16):
                p.do("scalar", lambda e, rt=rt, ct=ct: e.dma_start(
                    sp_wot.ap()[ct, :, rt * 128:(rt + 1) * 128],
                    wp["trst"][:, ct * 128:(ct + 1) * 128]), inc="dsp", n=16)

    def ternary_closures(p, wsrc, nrtile, ncol, spill, out, guard, trev):
        sb = sbm
        nhalf = (ncol + 2047) // 2048
        absev = {}
        addev = [None]

        def c_init(st):
            p.do("vector", lambda e: e.memset(wp["acc"][:], 0.0), inc="dve")
            addev[0] = p.ev("dve")
        out.append(c_init)

        def mk_p1(it, hf, cw, par):
            def cl(st):
                wf = wp["wf0"] if par == 0 else wp["wf1"]
                p.wait("gpsimd", guard.get(par))
                p.do("gpsimd", lambda e, wf=wf: e.dma_start(
                    wf[:, 0:cw], wsrc.ap()[it * 128:(it + 1) * 128,
                                           hf * 2048:hf * 2048 + cw]), inc="dgp", n=16)
                lev = p.ev("dgp")
                p.wait("scalar", lev)
                p.wait("scalar", addev[0])
                p.do("scalar", lambda e, wf=wf: e.activation(
                    wp["wi0"][:].bitcast(F32)[:, 0:cw],
                    wf[:, 0:cw], AF.Abs, accum_out=wp["acc"][:, 1:2]), inc="act")
                absev[par] = guard[par] = p.ev("act")
                p.wait("vector", absev[par])
                p.do("vector", lambda e: e.tensor_add(
                    wp["acc"][:, 0:1], wp["acc"][:, 0:1], wp["acc"][:, 1:2]), inc="dve")
                addev[0] = p.ev("dve")
            return cl
        for it in range(nrtile):
            for hf in range(nhalf):
                out.append(mk_p1(it, hf, min(2048, ncol - hf * 2048),
                                 (it * nhalf + hf) % 2))

        def c_scale(st):
            p.wait("vector", addev[0])
            emit_hilo(p, wp["acc"][:, 0:1], sb["colbf"][:, 0:1],
                      wp["acc"][:, 3:4], sb["colbf"][:, 1:2])
            hl2 = p.ev("dve")
            p.wait("tensor", hl2)
            p.wait("tensor", st.get("ps_s_g"))
            p.do("tensor", lambda e: e.matmul(ps_s[0:1, 448:449], sb["colbf"][:, 0:1],
                                              sb["onescol"][:, 0:1], start=True, stop=False))
            p.do("tensor", lambda e: e.matmul(ps_s[0:1, 448:449], sb["colbf"][:, 1:2],
                                              sb["onescol"][:, 0:1], start=False, stop=True), inc="pe")
            ps_ev = p.ev("pe")
            p.wait("vector", ps_ev)
            p.do("vector", lambda e: e.tensor_scalar(wp["acc"][0:1, 4:5], ps_s[0:1, 448:449],
                                                     1.0 / (nrtile * 128 * ncol), 1e-8,
                                                     ALU.mult, ALU.max), inc="dve")
            p.do("vector", lambda e: e.reciprocal(wp["acc"][0:1, 5:6], wp["acc"][0:1, 4:5]), inc="dve")
            emit_hilo(p, wp["acc"][0:1, 5:6], sb["rowbf"][0:1, 0:1], wp["acc"][0:1, 6:7],
                      sb["rowbf"][0:1, 1:2])
            iv_ev = p.ev("dve")
            p.wait("tensor", iv_ev)
            p.do("tensor", lambda e: e.matmul(ps_s[:, 452:453], sb["onesbf"][0:1, :],
                                              sb["rowbf"][0:1, 0:1], start=True, stop=False))
            p.do("tensor", lambda e: e.matmul(ps_s[:, 452:453], sb["onesbf"][0:1, :],
                                              sb["rowbf"][0:1, 1:2], start=False, stop=True), inc="pe")
            ib_ev = p.ev("pe")
            p.wait("vector", ib_ev)
            p.do("vector", lambda e: e.tensor_copy(wp["acc"][:, 7:8], ps_s[:, 452:453]), inc="dve")
            st["ps_s_g"] = p.ev("dve")
        out.append(c_scale)

        def mk_p2(it, hf, cw, par):
            def cl(st):
                wf = wp["wf0"] if par == 0 else wp["wf1"]
                wt = wp["wtmp0"] if par == 0 else wp["wtmp1"]
                av = wp["wi0"][:].bitcast(F32)
                p.wait("gpsimd", guard.get(par))
                p.do("gpsimd", lambda e, wf=wf: e.dma_start(
                    wf[:, 0:cw], wsrc.ap()[it * 128:(it + 1) * 128,
                                           hf * 2048:hf * 2048 + cw]), inc="dgp", n=16)
                lev = p.ev("dgp")
                p.wait("vector", lev)
                p.do("vector", lambda e, wf=wf: e.tensor_scalar(
                    wf[:, 0:cw], wf[:, 0:cw], wp["acc"][:, 7:8], None, ALU.mult), inc="dve")
                q1_ = p.ev("dve")
                p.wait("scalar", q1_)
                p.wait("scalar", trev.get(par))
                p.do("scalar", lambda e, wf=wf: e.activation(
                    av[:, 0:cw], wf[:, 0:cw], AF.Abs), inc="act")
                p.do("scalar", lambda e, wf=wf, wt=wt: e.activation(
                    wt[:, 0:cw], wf[:, 0:cw], AF.Sign), inc="act")
                q2_ = guard[par] = p.ev("act")
                p.wait("vector", q2_)
                p.do("vector", lambda e: e.tensor_scalar(
                    av[:, 0:cw], av[:, 0:cw], 0.5, None, ALU.is_ge), inc="dve")
                p.do("vector", lambda e, wt=wt: e.tensor_mul(
                    wt[:, 0:cw], wt[:, 0:cw], av[:, 0:cw]), inc="dve")
                qev = p.ev("dve")
                p.wait("sync", qev)
                p.wait("sync", p.ev("dsp"))
                for ct in range(cw // 128):
                    p.do("sync", lambda e, ct=ct, wt=wt: e.dma_start(
                        wp["trst"][:, ct * 128:(ct + 1) * 128],
                        wt[:, ct * 128:(ct + 1) * 128], transpose=True), inc="dtr", n=16)
                tev = p.ev("dtr")
                trev[par] = tev
                p.wait("scalar", tev)
                for ct in range(cw // 128):
                    ctg = hf * 16 + ct
                    p.do("scalar", lambda e, ctg=ctg, ct=ct: e.dma_start(
                        spill.ap()[ctg, :, it * 128:(it + 1) * 128],
                        wp["trst"][:, ct * 128:(ct + 1) * 128]), inc="dsp", n=16)
            return cl
        for it in range(nrtile):
            for hf in range(nhalf):
                out.append(mk_p2(it, hf, min(2048, ncol - hf * 2048),
                                 (it * nhalf + hf) % 2))

    def run_pass(p, front, back, order):
        st = {"bend": {}, "psyg": [None] * 4}
        fevs = {}
        items = list(order)
        for i in range(len(items) + 1):
            if i >= 1:
                seq, c = items[i - 1]
                back(p, seq, c, (i - 1) % 2, st, fevs.pop(i - 1))
            if i < len(items):
                seq, c = items[i]
                fevs[i] = front(p, seq, c, i % 2, st)
        return st

    def emit(p: P):
        prep(p)
        p.barrier()

        def fa(p_, seq, c, par, st):
            if c == 0:
                sq = seq % 2
                p_.do("vector", lambda e, sq=sq: e.memset(
                    wa["S0" if sq == 0 else "S1"][:], 0.0), inc="dve")
                p_.do("vector", lambda e, sq=sq: e.memset(
                    wa["stil0" if sq == 0 else "stil1"][:], 0.0), inc="dve")
                ev = p_.ev("dve")
                st[("sadd_q", sq)] = [ev] * 4
                st[("scan_q", sq)] = [ev] * 4
            return front_a(p_, seq, c, par, st)
        orderA = [(s, c) for s in range(NSEQ) for c in range(NCK)]
        run_pass(p, fa, back_a, orderA)
        p.barrier()
        for kt in range(16):
            p.do("sync", lambda e, kt=kt: e.dma_start(
                wb["w_outT"][:, kt * D:(kt + 1) * D], sp_wot.ap()[kt]), inc="dld", n=16)
        wo_ev = p.ev("dld")

        tern_fill = []
        tern_guard, tern_trev = {}, {}
        ternary_closures(p, w1, 32, 1024, sp_w1t, tern_fill, tern_guard, tern_trev)
        ternary_closures(p, w2, 8, 4096, sp_w2t, tern_fill, tern_guard, tern_trev)

        def fb1(p_, seq, c, par, st):
            p_.wait("tensor", wo_ev)
            if c == NCK - 1:
                sq = seq % 2
                p_.do("vector", lambda e, sq=sq: e.memset(
                    wb["S0" if sq == 0 else "S1"][:], 0.0), inc="dve")
                p_.do("vector", lambda e, sq=sq: e.memset(
                    wb["stil0" if sq == 0 else "stil1"][:], 0.0), inc="dve")
                ev = p_.ev("dve")
                st[("sadd_q", sq)] = [ev] * 4
                st[("scan_q", sq)] = [ev] * 4
            fev = front_b1(p_, seq, c, par, st)
            for _ in range(2):
                if tern_fill:
                    tern_fill.pop(0)(st)
            return fev
        orderB1 = [(s, NCK - 1 - ci) for s in range(NSEQ) for ci in range(NCK)]
        stB1 = run_pass(p, fb1, back_b1, orderB1)
        while tern_fill:
            tern_fill.pop(0)(stB1)
        p.barrier()
        for kt in range(8):
            p.do("sync", lambda e, kt=kt: e.dma_start(
                wc["w1T"][:, kt * 4 * D:(kt + 1) * 4 * D], sp_w1t.ap()[kt]), inc="dld", n=16)
        for kt in range(32):
            p.do("sync", lambda e, kt=kt: e.dma_start(
                wc["w2T"][:, kt * D:(kt + 1) * D], sp_w2t.ap()[kt]), inc="dld", n=16)
        p.do("gpsimd", lambda e: e.dma_start(wc["b1row"][:], b1.ap()[None, :]), inc="dgp", n=16)
        p.do("gpsimd", lambda e: e.dma_start(wc["b2row"][:], b2.ap()[None, :]), inc="dgp", n=16)
        wl_ev = p.ev("dld")
        bl_ev = p.ev("dgp")

        def fb2(p_, seq, c, par, st):
            p_.wait("tensor", wl_ev)
            p_.wait("tensor", bl_ev)
            p_.wait("sync", wl_ev)
            return front_b2(p_, seq, c, par, st)
        orderB2 = [(s, c) for s in range(NSEQ) for c in range(NCK)]
        run_pass(p, fb2, back_b2, orderB2)
        for e in ENGINES:
            for s in p.sems:
                if p.c[s] > 0:
                    p.wait(e, (s, p.c[s]))

    sems = {s: nc.alloc_semaphore(s) for s in sems_proto}
    with nc.Block() as blk:
        def mk(role):
            def body(eng):
                emit(P(nc, role, eng, sems))
            return body
        blk.sync(mk("sync"))
        blk.tensor(mk("tensor"))
        blk.vector(mk("vector"))
        blk.scalar(mk("scalar"))
        blk.gpsimd(mk("gpsimd"))
    return nc


def make_in_maps(inputs, xs, nseq):
    base = {
        "W_in": np.asarray(inputs["L0_attn_W_in"], np.float32),
        "W_out": np.asarray(inputs["L0_attn_W_out"], np.float32),
        "W1": np.asarray(inputs["L0_ffn_W1"], np.float32),
        "b1": np.asarray(inputs["L0_ffn_b1"], np.float32),
        "W2": np.asarray(inputs["L0_ffn_W2"], np.float32),
        "b2": np.asarray(inputs["L0_ffn_b2"], np.float32),
    }
    for dd in ("f", "r"):
        base[f"conv_w_{dd}"] = np.asarray(inputs[f"L0_attn_conv_w_{dd}"], np.float32)
        base[f"conv_b_{dd}"] = np.asarray(inputs[f"L0_attn_conv_b_{dd}"], np.float32)
        base[f"dt_bias_{dd}"] = np.asarray(inputs[f"L0_attn_dt_bias_{dd}"], np.float32)
        base[f"A_log_{dd}"] = np.asarray(inputs[f"L0_attn_A_log_{dd}"], np.float32)
        base[f"D_{dd}"] = np.asarray(inputs[f"L0_attn_D_{dd}"], np.float32)
        base[f"norm_w_{dd}"] = np.asarray(inputs[f"L0_attn_norm_w_{dd}"], np.float32)
    in_maps = []
    for core in range(NCORE):
        m = dict(base)
        m["x"] = np.ascontiguousarray(xs[nseq * core:nseq * core + nseq])
        in_maps.append(m)
    return in_maps


def kernel(**inputs):
    xs = np.asarray(inputs["L0_attn_x"], np.float32)
    nc = build_nc(NSEQ=2, NCK=32)
    in_maps = make_in_maps(inputs, xs, 2)
    res = run_bass_kernel_spmd(nc, in_maps, core_ids=list(range(NCORE)))
    return np.concatenate([res.results[i]["out"] for i in range(NCORE)], axis=0)


# version 18
# speedup vs baseline: 1.2073x; 1.2073x over previous
import numpy as np
import ml_dtypes
import concourse.bass as bass
import concourse.mybir as mybir
from concourse.bass_utils import run_bass_kernel_spmd

F32 = mybir.dt.float32
BF16 = mybir.dt.bfloat16
I32 = mybir.dt.int32
AF = mybir.ActivationFunctionType
ALU = mybir.AluOpType
AX = mybir.AxisListType

B, L, D = 16, 4096, 1024
DI, DSTATE, NH, HD = 2048, 128, 32, 64
CONVD = 2304
DIP = 4384
CK = 128
CAP = 1e4
EPS = 1e-5
NCORE = 8
HSTR = 160
HDAT = 16

ENGINES = ("sync", "tensor", "vector", "scalar", "gpsimd")
CSEMS = ("pe", "dve", "act", "pool", "dtr", "dld", "dsp", "dgp")
Z = ("pe", 0)


def bfq(x):
    return np.asarray(x, dtype=np.float32).astype(ml_dtypes.bfloat16)


class P:
    def __init__(self, nc, role, eng, sems):
        self.nc, self.role, self.eng, self.sems = nc, role, eng, sems
        self.c = {k: 0 for k in sems}
        self.tq = 0

    OWN = {"vector": "dve", "scalar": "act", "gpsimd": "pool"}

    def do(self, engine, emit, inc=None, n=1, chain=True):
        own = self.OWN.get(engine)
        serialize = own is not None and (inc is None or inc == own)
        if serialize:
            prev = self.c[own]
            inc, n = own, 1
        if inc is not None:
            self.c[inc] += n
        if self.role == engine:
            if serialize and chain and prev > 0:
                self.eng.wait_ge(self.sems[own], prev)
            inst = emit(self.eng)
            if inc is not None:
                inst.then_inc(self.sems[inc], n)

    def tqueue(self):
        return "sync"

    def ev(self, sem):
        return (sem, self.c[sem])

    def wait(self, engine, ev):
        if ev is None or ev[1] <= 0:
            return
        if self.role == engine:
            self.eng.wait_ge(self.sems[ev[0]], ev[1])

    def snap(self):
        return {s: self.c[s] for s in CSEMS}

    def wait_snap(self, engine, snap):
        if snap is None:
            return
        for s in CSEMS:
            self.wait(engine, (s, snap[s]))

    def barrier(self):
        own = {"sync": "b_sy", "tensor": "b_pe", "vector": "b_dv",
               "scalar": "b_ac", "gpsimd": "b_gp"}
        for e in ENGINES:
            self.do(e, lambda en: en.nop(), inc=own[e])
        for e in ENGINES:
            for s in self.sems:
                if self.c[s] > 0:
                    self.wait(e, (s, self.c[s]))


def build_nc(NSEQ=2, NCK=32, detect_races=True, dbg=False):
    LL = NCK * CK
    nc = bass.Bass("TRN2", target_bir_lowering=False,
                   detect_race_conditions=detect_races)

    x_in = nc.dram_tensor("x", [NSEQ, LL, D], F32, kind="ExternalInput")
    out_t = nc.dram_tensor("out", [NSEQ, LL, D], F32, kind="ExternalOutput")
    w_in = nc.dram_tensor("W_in", [DIP, D], F32, kind="ExternalInput")
    w_out = nc.dram_tensor("W_out", [D, DI], F32, kind="ExternalInput")
    w1 = nc.dram_tensor("W1", [4 * D, D], F32, kind="ExternalInput")
    b1 = nc.dram_tensor("b1", [4 * D], F32, kind="ExternalInput")
    w2 = nc.dram_tensor("W2", [D, 4 * D], F32, kind="ExternalInput")
    b2 = nc.dram_tensor("b2", [D], F32, kind="ExternalInput")
    pr = {}
    for dd in ("f", "r"):
        pr[f"convw_{dd}"] = nc.dram_tensor(f"conv_w_{dd}", [CONVD, 4], F32, kind="ExternalInput")
        pr[f"convb_{dd}"] = nc.dram_tensor(f"conv_b_{dd}", [CONVD], F32, kind="ExternalInput")
        pr[f"dtb_{dd}"] = nc.dram_tensor(f"dt_bias_{dd}", [NH], F32, kind="ExternalInput")
        pr[f"alog_{dd}"] = nc.dram_tensor(f"A_log_{dd}", [NH], F32, kind="ExternalInput")
        pr[f"dsk_{dd}"] = nc.dram_tensor(f"D_{dd}", [NH], F32, kind="ExternalInput")
        pr[f"nrm_{dd}"] = nc.dram_tensor(f"norm_w_{dd}", [DI], F32, kind="ExternalInput")

    E1 = np.repeat(np.eye(NH, dtype=np.float32), CK, axis=1)
    c_E2 = nc.inline_tensor(bfq(np.vstack([E1, E1])), "cE2")
    c_triu = nc.inline_tensor(bfq(np.triu(np.ones((CK, CK), np.float32))), "ctriu")
    c_tril = nc.inline_tensor(bfq(np.tril(np.ones((CK, CK), np.float32))), "ctril")
    c_identf = nc.inline_tensor(np.eye(CK, dtype=np.float32), "cidentf")
    c_identb = nc.inline_tensor(np.eye(CK).astype(ml_dtypes.bfloat16), "cidentb")
    c_bias = nc.inline_tensor(np.tile(np.array([[1.0, EPS, 0.0, 0.0]], dtype=np.float32), (128, 1)), "cbias")

    KD = "ExternalOutput" if dbg else "Internal"
    sp_yf = nc.dram_tensor("sp_yf", [NSEQ, NCK, CK, DI], BF16, kind=KD)
    sp_sz = nc.dram_tensor("sp_sz", [NSEQ, NCK, CK, DI], BF16, kind=KD)
    sp_xbc = nc.dram_tensor("sp_xbc", [NSEQ, NCK, CK, 18 * HSTR], BF16, kind=KD)
    sp_dt = nc.dram_tensor("sp_dt", [NSEQ, NCK, CK, NH], F32, kind=KD)
    sp_xn = nc.dram_tensor("sp_xn", [NSEQ, NCK, CK, D], BF16, kind=KD)
    sp_s = nc.dram_tensor("sp_s", [NSEQ, NCK, CK, D], F32, kind=KD)
    sp_flat = nc.dram_tensor("sp_flat", [NSEQ, NCK, 96, CK], BF16, kind=KD)
    sp_w1t = nc.dram_tensor("sp_w1t", [8, CK, 4 * D], BF16, kind=KD)
    sp_w2t = nc.dram_tensor("sp_w2t", [32, CK, D], BF16, kind=KD)
    sp_wot = nc.dram_tensor("sp_wot", [16, CK, D], BF16, kind=KD)

    SB_BASE = 16512
    SB_LIMIT = 229000
    cur = [0]

    def salloc(name, shape, dtype):
        esz = 2 if dtype == BF16 else 4
        nbytes = int(np.prod(shape[1:])) * esz
        o = (cur[0] + 31) // 32 * 32
        cur[0] = o + nbytes
        assert cur[0] <= SB_LIMIT, f"SBUF overflow at {name}: {cur[0]}"
        return nc.alloc_sbuf_tensor_at(name, list(shape), dtype, offset=o)

    def region(base, specs, pref=""):
        cur[0] = base + (SB_BASE if base < SB_BASE else 0)
        out = {}
        for nm, shape, dt_ in specs:
            out[nm] = salloc(pref + nm, shape, dt_)
        return out, cur[0]

    sb0, P0E = region(0, [
        ("onesbf", [2, 128], BF16),
        ("onescol", [128, 2], BF16),
        ("col1", [128, 16], F32),
        ("rowbf", [1, 16], BF16),
        ("rowf", [1, 256], F32),
        ("rowt", [1, 256], BF16),
        ("rowD", [1, 64], BF16),
        ("colbf", [128, 4], BF16),
        ("cbias", [128, 4], F32),
        ("identb", [128, 128], BF16),
        ("scrA", [128, 1024], BF16),
    ])
    SML = [
        ("dtraw", [128, NH], F32),
        ("dt", [128, NH], F32),
        ("sm1", [128, NH], F32),
        ("sm2", [128, NH], F32),
        ("acum", [128, 2 * NH], F32),
        ("totbc", [128, NH], F32),
        ("dstate", [128, NH], F32),
        ("exptot", [128, NH], F32),
        ("dtds", [128, NH], BF16),
        ("pk96", [128, 96], F32),
        ("pk64", [128, 64], F32),
        ("vtok", [128, 32], F32),
        ("bstg", [128, 32], BF16),
        ("flatpk", [96, 128], BF16),
        ("hilo", [128, 2 * NH], BF16),
        ("tot_hl", [1, 2 * NH], BF16),
        ("trow", [1, NH], F32),
        ("lhsT_mask", [128, 128], BF16),
    ]
    sb1, P1E0 = region(P0E, [
        ("rhs_mask", [128, 4096], BF16),
        ("expR", [1, 4096], BF16),
        ("triu", [128, 128], BF16),
        ("tril", [128, 128], BF16),
        ("identf", [128, 128], F32),
        ("convw_f", [128, 72], F32),
        ("convw_r", [128, 72], F32),
        ("convb_f", [128, 18], F32),
        ("convb_r", [128, 18], F32),
        ("dtb_f", [128, NH], F32),
        ("dtb_r", [128, NH], F32),
        ("negA_f", [128, NH], F32),
        ("negA_r", [128, NH], F32),
        ("Dbc_f", [128, NH], BF16),
        ("Dbc_r", [128, NH], BF16),
        ("nwbc_f", [128, DI], BF16),
        ("nwbc_r", [128, DI], BF16),
        ("cbst", [32, 128], F32),
    ])
    sml0, P1E1 = region(P1E0, SML, pref="s0_")
    sml1, P1E = region(P1E1, SML, pref="s1_")
    sml = [sml0, sml1]
    sbm = {}
    sbm.update(sb0)
    sbm.update(sb1)

    wa, WAE = region(P1E, [
        ("w_inT", [128, 8 * DIP], BF16),
        ("xA0", [128, D], F32),
        ("xA1", [128, D], F32),
        ("xn", [128, D], BF16),
        ("xnT", [128, 8 * 128], BF16),
        ("siluz0", [128, DI], BF16),
        ("siluz1", [128, DI], BF16),
        ("stage", [128, 4 * 512], BF16),
        ("halo", [128, 18 * HSTR], BF16),
        ("tail0", [128, 18 * 4], BF16),
        ("tail1", [128, 18 * 4], BF16),
        ("convT0", [128, CONVD], BF16),
        ("convT1", [128, CONVD], BF16),
        ("xstok", [128, DI], BF16),
        ("btok", [128, 128], BF16),
        ("gt", [128, 128], BF16),
        ("gmm", [128, 4096], BF16),
        ("gmv", [128, 4096], BF16),
        ("y0", [128, DI], BF16),
        ("dx", [128, DI], BF16),
        ("S0", [128, DI], F32),
        ("S1", [128, DI], F32),
        ("stil0", [128, DI], BF16),
        ("stil1", [128, DI], BF16),
        ("xdec", [128, DI], BF16),
    ], pref="a_")

    WPB = P1E + ((8 * DIP * 2 + 31) // 32 * 32)
    wp, WPE = region(WPB, [
        ("wf0", [128, 2048], F32),
        ("wf1", [128, 2048], F32),
        ("wi0", [128, 2048], I32),
        ("wtmp0", [128, 2048], BF16),
        ("wtmp1", [128, 2048], BF16),
        ("trst", [128, 2048], BF16),
        ("acc", [128, 8], F32),
    ], pref="p_")

    wb, WBE = region(P1E, [
        ("w_outT", [128, 16 * D], BF16),
        ("szB0", [128, DI], BF16),
        ("szB1", [128, DI], BF16),
        ("yfB0", [128, DI], BF16),
        ("yfB1", [128, DI], BF16),
        ("haloB0", [128, 18 * HSTR], BF16),
        ("haloB1", [128, 18 * HSTR], BF16),
        ("convT0", [128, CONVD], BF16),
        ("convT1", [128, CONVD], BF16),
        ("tailR0", [128, 18 * 4], BF16),
        ("tailR1", [128, 18 * 4], BF16),
        ("xstok", [128, DI], BF16),
        ("btok", [128, 128], BF16),
        ("gt", [128, 128], BF16),
        ("gmm", [128, 4096], BF16),
        ("gmv", [128, 4096], BF16),
        ("y0", [128, DI], BF16),
        ("dx", [128, DI], BF16),
        ("S0", [128, DI], F32),
        ("S1", [128, DI], F32),
        ("stil0", [128, DI], BF16),
        ("stil1", [128, DI], BF16),
        ("xdec", [128, DI], BF16),
        ("ycombT", [128, DI], BF16),
        ("sevac", [128, D], F32),
    ], pref="b_")

    wc, WCE = region(P0E, [
        ("w1T", [128, 8 * 4 * D], BF16),
        ("w2T", [128, 32 * D], BF16),
        ("xC0", [128, D], F32),
        ("xC1", [128, D], F32),
        ("xnC", [128, D], BF16),
        ("xnTC0", [128, 8 * 128], BF16),
        ("xnTC1", [128, 8 * 128], BF16),
        ("hbuf", [128, 4 * D], BF16),
        ("hT", [128, 32 * 128], BF16),
        ("sC0", [128, D], F32),
        ("sC1", [128, D], F32),
        ("thC", [128, D], BF16),
        ("gC", [128, D], F32),
        ("outC", [128, D], F32),
        ("b1row", [1, 4 * D], BF16),
        ("b2row", [1, D], BF16),
    ], pref="c_")

    nc.const_aps.register(F32, 1.0, sb0["cbias"][:, 0:1])
    nc.const_aps.register(F32, EPS, sb0["cbias"][:, 1:2])
    nc.const_aps.register(F32, 0.0, sb0["cbias"][:, 2:3])
    nc.const_aps.register(BF16, 0.0, sb0["cbias"][:, 3:4].bitcast(BF16)[:, 0:1])

    ps_a = nc.alloc_psum_tensor("ps_a", [128, 1024], F32)
    ps_s = nc.alloc_psum_tensor("ps_s", [128, 512], F32)
    ps_x = nc.alloc_psum_tensor("ps_x", [128, 512], F32)
    ps_y = nc.alloc_psum_tensor("ps_y", [128, 2048], F32)

    sems_proto = list(CSEMS) + ["b_sy", "b_pe", "b_dv", "b_ac", "b_gp"]

    def emit_hilo(p, src32, hi_bf, tmp32, lo_bf):
        p.do("vector", lambda e: e.tensor_copy(hi_bf, src32), inc="dve")
        p.do("vector", lambda e: e.tensor_copy(tmp32, hi_bf), inc="dve")
        p.do("vector", lambda e: e.tensor_sub(tmp32, src32, tmp32), inc="dve")
        p.do("vector", lambda e: e.tensor_copy(lo_bf, tmp32), inc="dve")

    def bview(t, nrep):
        return t[:, :, None].broadcast_to([t.shape[0], t.shape[1], nrep])

    def pe_tp(p, srcs, psview, dst, pre_waits_pe=(), pre_waits_dve=(), eng="vector"):
        for ev in pre_waits_pe:
            p.wait("tensor", ev)
        n = len(srcs)
        for i, s_ in enumerate(srcs):
            p.do("tensor", lambda e, i=i, s_=s_: e.transpose(
                psview[:, i * 128:(i + 1) * 128], s_, sbm["identb"][:]),
                inc=("pe" if i == n - 1 else None))
        pe_ev = p.ev("pe")
        p.wait(eng, pe_ev)
        for ev in pre_waits_dve:
            p.wait(eng, ev)
        srcv = psview[:, 0:n * 128]
        if len(dst.shape) == 3:
            srcv = srcv.rearrange("p (a b) -> p a b", b=128)
        if eng == "vector":
            p.do("vector", lambda e: e.tensor_copy(dst, srcv), inc="dve")
            return pe_ev, p.ev("dve")
        p.do("scalar", lambda e: e.activation(dst, srcv, AF.Identity), inc="act")
        return pe_ev, p.ev("act")

    def dt_chain(p, dd, par, st, fill=None):
        sb = sml[par]
        tri = sbm["triu"] if dd == "f" else sbm["tril"]
        def drain(n):
            if fill:
                for _ in range(min(n, len(fill))):
                    fill.pop(0)()
        p.do("vector", lambda e: e.tensor_add(sb["sm1"][:], sb["dtraw"][:],
                                              sbm[f"dtb_{dd}"][:]), inc="dve")
        t1_ev = p.ev("dve")
        p.wait("scalar", t1_ev)
        p.do("scalar", lambda e: e.activation(sb["sm2"][:], sb["sm1"][:], AF.Abs), inc="act")
        p.do("scalar", lambda e: e.activation(sb["sm2"][:], sb["sm2"][:], AF.Exp, scale=-1.0), inc="act")
        p.do("scalar", lambda e: e.activation(sb["sm2"][:], sb["sm2"][:], AF.Ln, bias=1.0), inc="act")
        p.do("scalar", lambda e: e.activation(sb["dt"][:], sb["sm1"][:], AF.Relu), inc="act")
        sp_ev = p.ev("act")
        drain(2)
        p.wait("vector", sp_ev)
        p.do("vector", lambda e: e.tensor_add(sb["dt"][:], sb["dt"][:], sb["sm2"][:]), inc="dve")
        p.do("vector", lambda e: e.tensor_scalar_max(sb["dt"][:], sb["dt"][:], 1e-30), inc="dve")
        dt_ev = p.ev("dve")
        p.wait("scalar", dt_ev)
        p.do("scalar", lambda e: e.activation(sb["acum"][:, NH:2 * NH], sb["dt"][:], AF.Ln), inc="act")
        lnd_ev = p.ev("act")
        p.do("vector", lambda e: e.tensor_mul(sb["sm1"][:], sb["dt"][:], sbm[f"negA_{dd}"][:]), inc="dve")
        emit_hilo(p, sb["sm1"][:], sb["hilo"][:, 0:NH], sb["sm2"][:], sb["hilo"][:, NH:2 * NH])
        hl_ev = p.ev("dve")
        p.wait("tensor", hl_ev)
        p.wait("tensor", st.get("ps_s_g"))
        p.do("tensor", lambda e: e.matmul(ps_s[:, 0:NH], tri[:], sb["hilo"][:, 0:NH],
                                          start=True, stop=False))
        p.do("tensor", lambda e: e.matmul(ps_s[:, 0:NH], tri[:], sb["hilo"][:, NH:2 * NH],
                                          start=False, stop=True), inc="pe")
        p.do("tensor", lambda e: e.matmul(ps_s[0:1, 32:32 + NH], sbm["onescol"][:, 0:1],
                                          sb["hilo"][:, 0:NH], start=True, stop=False))
        p.do("tensor", lambda e: e.matmul(ps_s[0:1, 32:32 + NH], sbm["onescol"][:, 0:1],
                                          sb["hilo"][:, NH:2 * NH], start=False, stop=True), inc="pe")
        cs_ev = p.ev("pe")
        drain(2)
        p.wait("vector", cs_ev)
        p.do("vector", lambda e: e.tensor_copy(sb["acum"][:, 0:NH], ps_s[:, 0:NH]), inc="dve")
        p.do("vector", lambda e: e.tensor_copy(sb["trow"][:], ps_s[0:1, 32:32 + NH]), inc="dve")
        emit_hilo(p, sb["trow"][:], sb["tot_hl"][0:1, 0:NH], sb["sm2"][0:1, 0:NH],
                  sb["tot_hl"][0:1, NH:2 * NH])
        th_ev = p.ev("dve")
        p.wait("tensor", th_ev)
        p.do("tensor", lambda e: e.matmul(ps_s[:, 64:64 + NH], sbm["onesbf"][0:1, :],
                                          sb["tot_hl"][0:1, 0:NH], start=True, stop=False))
        p.do("tensor", lambda e: e.matmul(ps_s[:, 64:64 + NH], sbm["onesbf"][0:1, :],
                                          sb["tot_hl"][0:1, NH:2 * NH], start=False, stop=True), inc="pe")
        tb_ev = p.ev("pe")
        drain(2)
        p.wait("vector", tb_ev)
        p.wait("vector", lnd_ev)
        p.do("vector", lambda e: e.tensor_copy(sb["totbc"][:], ps_s[:, 64:64 + NH]), inc="dve")
        p.do("vector", lambda e: e.tensor_sub(sb["dstate"][:], sb["totbc"][:], sb["acum"][:, 0:NH]), inc="dve")
        ds_ev = p.ev("dve")
        p.wait("scalar", ds_ev)
        p.do("scalar", lambda e: e.activation(sb["dstate"][:], sb["dstate"][:], AF.Exp), inc="act")
        p.do("scalar", lambda e: e.activation(sb["exptot"][:], sb["totbc"][:], AF.Exp), inc="act")
        p.do("scalar", lambda e: e.activation(sb["pk96"][:, 64:96], sb["acum"][:, 0:NH], AF.Exp), inc="act")
        ex_ev = p.ev("act")
        drain(1)
        p.wait("vector", ex_ev)
        p.do("vector", lambda e: e.tensor_mul(sb["dtds"][:], sb["dt"][:], sb["dstate"][:]), inc="dve")
        p.do("vector", lambda e: e.tensor_copy(sb["bstg"][:], sb["acum"][:, 0:NH]), inc="dve")
        p.do("vector", lambda e: e.tensor_copy(sb["pk96"][:, 0:32], sb["bstg"][:]), inc="dve")
        p.do("vector", lambda e: e.tensor_sub(sb["pk96"][:, 32:64], sb["acum"][:, 0:NH],
                                              sb["pk96"][:, 0:32]), inc="dve")
        p.do("vector", lambda e: e.tensor_sub(sb["vtok"][:], sb["acum"][:, NH:2 * NH],
                                              sb["acum"][:, 0:NH]), inc="dve")
        p.do("vector", lambda e: e.tensor_copy(sb["bstg"][:], sb["vtok"][:]), inc="dve")
        p.do("vector", lambda e: e.tensor_copy(sb["pk64"][:, 0:32], sb["bstg"][:]), inc="dve")
        p.do("vector", lambda e: e.tensor_sub(sb["pk64"][:, 32:64], sb["vtok"][:],
                                              sb["pk64"][:, 0:32]), inc="dve")
        pkd_ev = p.ev("dve")
        p.wait("tensor", pkd_ev)
        p.do("tensor", lambda e: e.transpose(ps_s[0:96, 128:256], sb["pk96"][:, 0:96],
                                             sbm["identf"][:]), inc="pe")
        p.do("tensor", lambda e: e.transpose(ps_s[0:64, 256:384], sb["pk64"][:, 0:64],
                                             sbm["identf"][:]), inc="pe")
        tr_ev = p.ev("pe")
        drain(1)
        p.wait("vector", tr_ev)
        p.do("vector", lambda e: e.tensor_copy(sb["flatpk"][0:96, :], ps_s[0:96, 128:256]), inc="dve")
        p.do("vector", lambda e: e.tensor_copy(sb["lhsT_mask"][0:64, :], ps_s[0:64, 256:384]), inc="dve")
        st["ps_s_g"] = p.ev("dve")
        drain(8)
        return p.ev("dve")

    def conv_batches(p, dd, halo, out_buf):
        wv = sbm[f"convw_{dd}"]
        offs = [13, 14, 15, 16] if dd == "f" else [19, 18, 17, 16]
        def hs(i, k):
            return halo[:, i * HSTR + offs[k]:i * HSTR + offs[k] + 128]
        evs = {}
        def tap(i0, i1, k, first):
            def go():
                for i in range(i0, i1):
                    if k == 0:
                        p.do("vector", lambda e, i=i: e.tensor_scalar(
                            out_buf[:, i * 128:(i + 1) * 128], hs(i, 0),
                            wv[:, i * 4:i * 4 + 1], sbm[f"convb_{dd}"][:, i:i + 1],
                            ALU.mult, ALU.add), inc="dve", chain=(first and i == i0))
                    else:
                        p.do("vector", lambda e, i=i, k=k: e.scalar_tensor_tensor(
                            out_buf[:, i * 128:(i + 1) * 128], hs(i, k),
                            wv[:, i * 4 + k:i * 4 + k + 1],
                            out_buf[:, i * 128:(i + 1) * 128],
                            ALU.mult, ALU.add), inc="dve", chain=False)
                if k == 3:
                    evs[i0] = p.ev("dve")
            return go
        batches = [tap(0, 8, k, True) for k in range(4)] +                   [tap(8, 18, k, False) for k in range(4)]
        def finish():
            out = []
            for i0, i1 in ((0, 8), (8, 18)):
                p.wait("scalar", evs[i0])
                p.do("scalar", lambda e, i0=i0, i1=i1: e.activation(
                    out_buf[:, i0 * 128:i1 * 128], out_buf[:, i0 * 128:i1 * 128],
                    AF.Silu), inc="act")
                out.append(p.ev("act"))
            return out
        return batches, finish

    def back_ssd(p, dd, seq, c, par, w, st, fev, first):
        sb = sml[par]
        tri = sbm["triu"] if dd == "f" else sbm["tril"]
        convT = w["convT0"] if par == 0 else w["convT1"]
        sq = seq % 2
        Sb = w["S0"] if sq == 0 else w["S1"]
        stilb = w["stil0"] if sq == 0 else w["stil1"]
        szb = fev["szbuf"]
        psyg = st["psyg"]
        p.wait("sync", fev["flat"])
        p.wait("sync", st.get("mask_pe"))
        flview = sp_flat.ap()[seq, c].rearrange("(a b) t -> a (b t)", a=3)
        p.do("sync", lambda e: e.dma_start(sbm["rhs_mask"][64:66, :], flview[0:2, :]), inc="dld", n=16)
        p.do("sync", lambda e: e.dma_start(sbm["expR"][:], flview[2:3, :]), inc="dld", n=16)
        fl_ev = p.ev("dld")
        for q in range(2):
            pw = [fev["conv"][q], psyg[q * 2], psyg[q * 2 + 1]]
            dw = [st.get("states_pe"), st.get("xdec_dve")] if q == 0 else []
            _, xe = pe_tp(p, [convT[:, (q * 8 + i) * 128:(q * 8 + i + 1) * 128]
                              for i in range(8)],
                          ps_y[:, q * 1024:(q + 1) * 1024].bitcast(BF16),
                          w["xstok"][:, q * 1024:(q + 1) * 1024],
                          pre_waits_pe=pw, pre_waits_dve=dw)
            psyg[q * 2] = xe
            psyg[q * 2 + 1] = xe
        _, be = pe_tp(p, [convT[:, 16 * 128:17 * 128]],
                      ps_x[:].bitcast(BF16)[:, 0:128], w["btok"][:],
                      pre_waits_pe=(st.get("gt_dve"), fev["conv"][1]))
        tr_ev = p.ev("dve")
        p.wait("tensor", fev["conv"][1])
        p.wait("tensor", tr_ev)
        p.do("tensor", lambda e: e.matmul(ps_x[:, 0:128], convT[:, 16 * 128:17 * 128],
                                          convT[:, 17 * 128:18 * 128], start=True, stop=True), inc="pe")
        g_ev = p.ev("pe")
        p.wait("vector", g_ev)
        p.do("vector", lambda e: e.tensor_mul(w["gt"][:], ps_x[:, 0:128], tri[:]), inc="dve")
        st["gt_dve"] = p.ev("dve")
        gt_ev = st["gt_dve"]
        first_exp = True
        cg_h = [None, None]
        for half in range(2):
            for jj in range(4):
                j = half * 4 + jj
                if jj == 0:
                    p.wait("tensor", fl_ev)
                    p.wait("tensor", fev["vrows"])
                p.wait("tensor", psyg[jj])
                p.do("tensor", lambda e, j=j, jj=jj, par=par: e.matmul(
                    ps_y[:, jj * 512:(jj + 1) * 512], sml[par]["lhsT_mask"][0:66, :],
                    sbm["rhs_mask"][0:66, j * 512:(j + 1) * 512],
                    start=True, stop=True), inc="pe")
                m_ev = p.ev("pe")
                p.wait("scalar", m_ev)
                if first_exp:
                    p.wait("scalar", st.get("ydyo_pe"))
                    p.wait("scalar", st.get("gm_dve"))
                    first_exp = False
                p.do("scalar", lambda e, j=j, jj=jj: e.activation(
                    w["gmm"][:, j * 512:(j + 1) * 512], ps_y[:, jj * 512:(jj + 1) * 512],
                    AF.Exp), inc="act", chain=(j == 0))
                psyg[jj] = p.ev("act")
            me_h = p.ev("act")
            p.wait("vector", me_h)
            p.do("vector", lambda e, half=half: e.scalar_tensor_tensor(
                w["gmm"][:, half * 2048:(half + 1) * 2048].rearrange("p (a b) -> p a b", a=16),
                w["gmm"][:, half * 2048:(half + 1) * 2048].rearrange("p (a b) -> p a b", a=16),
                CAP, w["gt"][:, None, :].broadcast_to([128, 16, 128]),
                ALU.min, ALU.mult), inc="dve")
            cg_h[half] = p.ev("dve")
        gv_h = [None, None]
        for half in range(2):
            for jj in range(4):
                j = half * 4 + jj
                p.wait("tensor", psyg[jj])
                p.do("tensor", lambda e, j=j, jj=jj: e.matmul(
                    ps_y[:, jj * 512:(jj + 1) * 512], sbm["onesbf"][0:1, :],
                    sbm["expR"][0:1, j * 512:(j + 1) * 512], start=True, stop=True), inc="pe")
                v_ev = p.ev("pe")
                p.wait("scalar", v_ev)
                p.do("scalar", lambda e, j=j, jj=jj: e.activation(
                    w["gmv"][:, j * 512:(j + 1) * 512], ps_y[:, jj * 512:(jj + 1) * 512],
                    AF.Identity), inc="act", chain=False)
                psyg[jj] = p.ev("act")
            ve_h = p.ev("act")
            p.wait("vector", ve_h)
            p.do("vector", lambda e, half=half: e.tensor_mul(
                w["gmv"][:, half * 2048:(half + 1) * 2048].rearrange("p (a b) -> p a b", a=16),
                w["gmv"][:, half * 2048:(half + 1) * 2048].rearrange("p (a b) -> p a b", a=16),
                convT[:, None, 17 * 128:18 * 128].broadcast_to([128, 16, 128])), inc="dve")
            gv_h[half] = p.ev("dve")
        st["gm_dve"] = p.ev("dve")
        gm_ev = st["gm_dve"]
        p.wait("vector", fev["vrd"])
        p.wait("vector", tr_ev)
        p.wait("vector", st.get("states_pe"))
        p.do("vector", lambda e: e.tensor_mul(
            w["xdec"][:].rearrange("p (a b) -> p a b", a=NH),
            w["xstok"][:].rearrange("p (a b) -> p a b", a=NH),
            bview(sb["dtds"][:], HD)), inc="dve")
        st["xdec_dve"] = p.ev("dve")
        xd_ev = st["xdec_dve"]
        p.do("vector", lambda e: e.tensor_mul(
            w["dx"][:].rearrange("p (a b) -> p a b", a=NH),
            w["xstok"][:].rearrange("p (a b) -> p a b", a=NH),
            bview(sbm[f"Dbc_{dd}"][:], HD)), inc="dve", chain=False)
        dx_ev = p.ev("dve")
        p.wait("tensor", tr_ev)
        scq_prev = st.get(("scan_q", sq)) or [None] * 4
        yq_ev = {}
        for q in range(4):
            p.wait("tensor", psyg[q])
            p.wait("tensor", cg_h[q // 2])
            p.wait("tensor", gv_h[q // 2])
            p.wait("tensor", scq_prev[q])
            for hh in range(8):
                h = q * 8 + hh
                p.do("tensor", lambda e, h=h, q=q, hh=hh: e.matmul(
                    ps_y[:, h * 64:(h + 1) * 64], w["gmm"][:, h * 128:(h + 1) * 128],
                    w["xstok"][:, h * 64:(h + 1) * 64], start=True, stop=False,
                    skip_group_check=True))
                p.do("tensor", lambda e, h=h, q=q, hh=hh, stilb=stilb: e.matmul(
                    ps_y[:, h * 64:(h + 1) * 64], w["gmv"][:, h * 128:(h + 1) * 128],
                    stilb[:, h * 64:(h + 1) * 64], start=False, stop=True,
                    skip_group_check=True), inc="pe")
            q_pe = p.ev("pe")
            p.wait("scalar", q_pe)
            if q == 0:
                p.wait("scalar", st.get("y0_free"))
                p.wait("scalar", st.get("yread_dve"))
            p.do("scalar", lambda e, q=q: e.activation(
                w["y0"][:, q * 512:(q + 1) * 512], ps_y[:, q * 512:(q + 1) * 512],
                AF.Identity), inc="act", chain=(q == 0))
            psyg[q] = p.ev("act")
            yq_ev[q] = psyg[q]
        st["ydyo_pe"] = p.ev("pe")
        ye_ev = p.ev("act")
        p.wait("tensor", xd_ev)
        stq = []
        for q in range(4):
            p.wait("tensor", psyg[q])
            p.do("tensor", lambda e, q=q: e.matmul(
                ps_y[:, q * 512:(q + 1) * 512], w["btok"][:],
                w["xdec"][:, q * 512:(q + 1) * 512], start=True, stop=True), inc="pe")
            stq.append(p.ev("pe"))
        st["states_pe"] = p.ev("pe")
        st_ev = st["states_pe"]
        p.wait("gpsimd", fev["exp"])
        saq, scq = [None] * 4, [None] * 4
        for q in range(4):
            p.wait("gpsimd", scq_prev[q])
            p.do("gpsimd", lambda e, q=q, Sb=Sb: e.tensor_mul(
                Sb[:, q * 512:(q + 1) * 512].rearrange("p (a b) -> p a b", a=8),
                Sb[:, q * 512:(q + 1) * 512].rearrange("p (a b) -> p a b", a=8),
                bview(sb["exptot"][:, q * 8:(q + 1) * 8], HD)), inc="pool")
            smq = p.ev("pool")
            p.wait("vector", smq)
            p.wait("vector", stq[q])
            p.do("vector", lambda e, q=q, Sb=Sb: e.tensor_add(
                Sb[:, q * 512:(q + 1) * 512], Sb[:, q * 512:(q + 1) * 512],
                ps_y[:, q * 512:(q + 1) * 512]), inc="dve")
            saq[q] = p.ev("dve")
            p.wait("scalar", saq[q])
            p.do("scalar", lambda e, q=q, Sb=Sb, stilb=stilb: e.activation(
                stilb[:, q * 512:(q + 1) * 512],
                Sb[:, q * 512:(q + 1) * 512], AF.Identity), inc="act", chain=False)
            scq[q] = p.ev("act")
            psyg[q] = saq[q]
        st[("sadd_q", sq)], st[("scan_q", sq)] = saq, scq
        p.wait("vector", ye_ev)
        p.wait("vector", dx_ev)
        for half in range(2):
            p.do("vector", lambda e, half=half: e.tensor_add(
                w["y0"][:, half * 1024:(half + 1) * 1024],
                w["y0"][:, half * 1024:(half + 1) * 1024],
                w["dx"][:, half * 1024:(half + 1) * 1024]), inc="dve",
                chain=(half == 0))
        p.wait("vector", fev["sz"])
        for half in range(2):
            p.do("vector", lambda e, half=half, szb=szb: e.tensor_mul(
                w["dx"][:, half * 1024:(half + 1) * 1024],
                w["y0"][:, half * 1024:(half + 1) * 1024],
                szb[:, half * 1024:(half + 1) * 1024]), inc="dve")
        y2_ev = p.ev("dve")
        p.wait("scalar", y2_ev)
        for half in range(2):
            p.do("scalar", lambda e, half=half: e.activation(
                sbm["scrA"][:, 0:1024], w["dx"][:, half * 1024:(half + 1) * 1024],
                AF.Square, accum_out=sbm["col1"][:, 8 + half:9 + half]), inc="act")
        sq_ev = p.ev("act")
        p.wait("vector", sq_ev)
        p.do("vector", lambda e: e.tensor_add(sbm["col1"][:, 10:11], sbm["col1"][:, 8:9],
                                              sbm["col1"][:, 9:10]), inc="dve")
        p.do("vector", lambda e: e.tensor_scalar(sbm["col1"][:, 11:12], sbm["col1"][:, 10:11],
                                                 1.0 / DI, None, ALU.mult), inc="dve")
        ms_ev = p.ev("dve")
        p.wait("scalar", ms_ev)
        p.do("scalar", lambda e: e.activation(sbm["col1"][:, 12:13], sbm["col1"][:, 11:12],
                                              AF.Ln, bias=EPS), inc="act")
        p.do("scalar", lambda e: e.activation(sbm["col1"][:, 13:14], sbm["col1"][:, 12:13],
                                              AF.Exp, scale=-0.5), inc="act")
        rs_ev = p.ev("act")
        p.wait("vector", rs_ev)
        p.do("vector", lambda e: e.scalar_tensor_tensor(
            w["y0"][:], w["dx"][:], sbm["col1"][:, 13:14], sbm[f"nwbc_{dd}"][:],
            ALU.mult, ALU.mult), inc="dve")
        y_ev = p.ev("dve")
        st["yread_dve"] = y_ev
        st["mask_pe"] = st["states_pe"]
        return y_ev

    def front_a(p, seq, c, par, st):
        sb = sml[par]
        fev = {}
        xbuf = wa["xA0"] if par == 0 else wa["xA1"]
        bend = st["bend"].get(par)
        for e_ in ("vector", "scalar", "sync", "tensor", "gpsimd"):
            p.wait_snap(e_, bend)
        p.do("sync", lambda e, seq=seq, c=c, xbuf=xbuf: e.dma_start(
            xbuf[:], x_in.ap()[seq, c * CK:(c + 1) * CK, :]), inc="dld", n=16)
        xld = p.ev("dld")
        p.wait("vector", xld)
        p.wait("scalar", xld)
        p.do("scalar", lambda e, xbuf=xbuf: e.activation(
            sbm["scrA"][:, 0:1024], xbuf[:], AF.Square,
            accum_out=sbm["col1"][:, 0:1]), inc="act")
        sq_ev = p.ev("act")
        p.do("vector", lambda e, xbuf=xbuf: e.reduce_sum(sbm["col1"][:, 1:2], xbuf[:], AX.X), inc="dve")
        p.wait("vector", sq_ev)
        p.do("vector", lambda e: e.tensor_scalar(sbm["col1"][:, 2:3], sbm["col1"][:, 0:1],
                                                 1.0 / D, None, ALU.mult), inc="dve")
        p.do("vector", lambda e: e.tensor_scalar(sbm["col1"][:, 3:4], sbm["col1"][:, 1:2],
                                                 1.0 / D, None, ALU.mult), inc="dve")
        p.do("vector", lambda e: e.tensor_mul(sbm["col1"][:, 4:5], sbm["col1"][:, 3:4],
                                              sbm["col1"][:, 3:4]), inc="dve")
        p.do("vector", lambda e: e.tensor_sub(sbm["col1"][:, 5:6], sbm["col1"][:, 2:3],
                                              sbm["col1"][:, 4:5]), inc="dve")
        var_ev = p.ev("dve")
        p.wait("scalar", var_ev)
        p.do("scalar", lambda e: e.activation(sbm["col1"][:, 6:7], sbm["col1"][:, 5:6],
                                              AF.Ln, bias=EPS), inc="act")
        p.do("scalar", lambda e: e.activation(sbm["col1"][:, 7:8], sbm["col1"][:, 6:7],
                                              AF.Exp, scale=-0.5), inc="act")
        rstd_ev = p.ev("act")
        p.wait("vector", rstd_ev)
        p.wait("vector", st.get("xn_free"))
        p.wait("vector", st.get("xn_free_sp"))
        p.do("vector", lambda e, xbuf=xbuf: e.scalar_tensor_tensor(
            wa["xn"][:], xbuf[:], sbm["col1"][:, 3:4],
            sbm["col1"][:, 7:8].broadcast_to([128, D]),
            ALU.subtract, ALU.mult), inc="dve")
        xn_ev = p.ev("dve")
        p.wait("scalar", xn_ev)
        p.do("scalar", lambda e, seq=seq, c=c: e.dma_start(
            sp_xn.ap()[seq, c], wa["xn"][:]), inc="dsp", n=16)
        xnsp_ev = p.ev("dsp")
        st["xn_free_sp"] = xnsp_ev
        psv = ps_s[:].bitcast(BF16)
        _, xnT_ev = pe_tp(p, [wa["xn"][:, k * 128:(k + 1) * 128] for k in range(8)],
                          psv, wa["xnT"][:],
                          pre_waits_pe=(xn_ev, st.get("ps_s_g"), st.get("inproj_pe")),
                          pre_waits_dve=(st.get("inproj_pe"),))
        st["ps_s_g"] = xnT_ev
        st["xn_free"] = p.ev("pe")
        szbuf = wa["siluz0"] if par == 0 else wa["siluz1"]
        p.wait("tensor", xnT_ev)
        jev = st.get("jev", {})
        halo_evs = []
        for j in range(9):
            psl = ps_a[:, (j % 2) * 512:(j % 2) * 512 + 512]
            ncols = 512 if j < 8 else 288
            p.wait("tensor", jev.get(j % 2))
            for k in range(8):
                p.do("tensor", lambda e, j=j, k=k, psl=psl, ncols=ncols: e.matmul(
                    psl[:, 0:ncols], wa["xnT"][:, k * 128:(k + 1) * 128],
                    wa["w_inT"][:, k * DIP + j * 512:k * DIP + j * 512 + ncols],
                    start=(k == 0), stop=(k == 7)), inc=("pe" if k == 7 else None))
            mm_ev = p.ev("pe")
            if j < 4:
                p.wait("scalar", mm_ev)
                if j == 0:
                    p.wait("scalar", st.get("gate_dve"))
                p.do("scalar", lambda e, j=j, psl=psl, szbuf=szbuf: e.activation(
                    szbuf[:, j * 512:(j + 1) * 512], psl[:], AF.Identity), inc="act",
                    chain=(j == 0))
                jev[j % 2] = p.ev("act")
            else:
                p.wait("scalar", mm_ev)
                nxb = 512 if j < 8 else 256
                stg = wa["stage"][:, (j % 4) * 512:(j % 4) * 512 + 512]
                p.wait("scalar", st.get(("stg", j % 4)))
                p.do("scalar", lambda e, psl=psl, nxb=nxb, stg=stg: e.activation(
                    stg[:, 0:nxb], psl[:, 0:nxb], AF.Identity), inc="act", chain=False)
                if j == 8:
                    p.do("scalar", lambda e, psl=psl, sb=sb: e.activation(
                        sb["dtraw"][:], psl[:, 256:288], AF.Identity), inc="act", chain=False)
                st_ev = p.ev("act")
                jev[j % 2] = st_ev
                pw = [st_ev, st.get("ps_s_g")]
                dw = []
                if j == 4:
                    dw = [st.get("halo_free"), st.get("halo_free_cv"),
                          st.get("halo_free_tl")]
                nt = nxb // 128
                i0 = (j - 4) * 4
                hdst = wa["halo"][:].rearrange("p (i s) -> p i s", i=18)[
                    :, i0:i0 + nt, HDAT:HDAT + 128]
                _, hev = pe_tp(p, [stg[:, bb * 128:(bb + 1) * 128] for bb in range(nt)],
                               ps_s[:].bitcast(BF16), hdst,
                               pre_waits_pe=pw, pre_waits_dve=dw, eng="scalar")
                st["ps_s_g"] = hev
                st[("stg", j % 4)] = p.ev("pe")
                halo_evs.append(hev)
        st["jev"] = jev
        st["inproj_pe"] = p.ev("pe")
        xbcT_ev = halo_evs[-1]
        convT = wa["convT0"] if par == 0 else wa["convT1"]
        p.wait("vector", xbcT_ev)
        p.wait("vector", p.ev("act"))
        tl = wa["tail0"] if seq % 2 == 0 else wa["tail1"]
        if c == 0:
            p.do("vector", lambda e, tl=tl: e.memset(tl[:], 0.0), inc="dve")
            p.do("vector", lambda e: e.memset(
                wa["halo"][:].rearrange("p (i s) -> p i s", i=18)[:, :, 0:16], 0.0), inc="dve")
            p.do("vector", lambda e: e.memset(
                wa["halo"][:].rearrange("p (i s) -> p i s", i=18)[:, :, 144:160], 0.0), inc="dve")
        p.do("vector", lambda e, tl=tl: e.tensor_copy(
            wa["halo"][:].rearrange("p (i s) -> p i s", i=18)[:, :, 13:16],
            tl[:].rearrange("p (i s) -> p i s", i=18)[:, :, 1:4]), inc="dve")
        batches, cfin = conv_batches(p, "f", wa["halo"], convT)
        vrd = dt_chain(p, "f", par, st, fill=batches)
        fev["vrd"] = vrd
        fev["vrows"] = vrd
        fev["exp"] = p.ev("act")
        fev["conv"] = cfin()
        p.do("scalar", lambda e, szbuf=szbuf: e.activation(
            szbuf[:], szbuf[:], AF.Silu), inc="act")
        fev["sz"] = sz_ev = p.ev("act")
        p.wait("scalar", sz_ev)
        p.do("scalar", lambda e, seq=seq, c=c, szbuf=szbuf: e.dma_start(
            sp_sz.ap()[seq, c], szbuf[:]), inc="dsp", n=16)
        p.do("vector", lambda e, tl=tl: e.tensor_copy(
            tl[:].rearrange("p (i s) -> p i s", i=18)[:, :, 1:4],
            wa["halo"][:].rearrange("p (i s) -> p i s", i=18)[:, :, 141:144]), inc="dve")
        tail_ev = p.ev("dve")
        p.wait("scalar", vrd)
        p.do("scalar", lambda e, seq=seq, c=c, sb=sb: e.dma_start(
            sp_flat.ap()[seq, c], sb["flatpk"][0:96, :]), inc="dsp", n=16)
        fev["flat"] = p.ev("dsp")
        p.wait("scalar", xbcT_ev)
        p.do("scalar", lambda e, seq=seq, c=c: e.dma_start(
            sp_xbc.ap()[seq, c], wa["halo"][:]), inc="dsp", n=16)
        hsp_ev = p.ev("dsp")
        p.do("scalar", lambda e, seq=seq, c=c, sb=sb: e.dma_start(
            sp_dt.ap()[seq, c], sb["dtraw"][:]), inc="dsp", n=16)
        st["halo_free"] = hsp_ev
        st["halo_free_cv"] = fev["conv"][1]
        st["halo_free_tl"] = tail_ev
        fev["szbuf"] = szbuf
        return fev

    def back_a(p, seq, c, par, st, fev):
        y_ev = back_ssd(p, "f", seq, c, par, wa, st, fev, c == 0)
        p.wait("scalar", y_ev)
        p.do("scalar", lambda e, seq=seq, c=c: e.dma_start(
            sp_yf.ap()[seq, c], wa["y0"][:]), inc="dsp", n=16)
        st["y0_free"] = p.ev("dsp")
        st["bend"][par] = p.snap()

    def front_b1(p, seq, c, par, st):
        sb = sml[par]
        fev = {}
        bend = st["bend"].get(par)
        for e_ in ENGINES:
            p.wait_snap(e_, bend)
        halo = wb["haloB0"] if par == 0 else wb["haloB1"]
        szb = wb["szB0"] if par == 0 else wb["szB1"]
        yfb = wb["yfB0"] if par == 0 else wb["yfB1"]
        p.do("sync", lambda e, seq=seq, c=c, halo=halo: e.dma_start(
            halo[:], sp_xbc.ap()[seq, c]), inc="dld", n=16)
        p.do("sync", lambda e, seq=seq, c=c, sb=sb: e.dma_start(
            sb["dtraw"][:], sp_dt.ap()[seq, c]), inc="dld", n=16)
        p.do("sync", lambda e, seq=seq, c=c, szb=szb: e.dma_start(
            szb[:], sp_sz.ap()[seq, c]), inc="dld", n=16)
        p.do("sync", lambda e, seq=seq, c=c, yfb=yfb: e.dma_start(
            yfb[:], sp_yf.ap()[seq, c]), inc="dld", n=16)
        ld_ev = p.ev("dld")
        fev["ld"] = ld_ev
        p.wait("vector", ld_ev)
        tlr = wb["tailR0"] if seq % 2 == 0 else wb["tailR1"]
        if c == NCK - 1:
            p.do("vector", lambda e, tlr=tlr: e.memset(tlr[:], 0.0), inc="dve")
        p.do("vector", lambda e, halo=halo, tlr=tlr: e.tensor_copy(
            halo[:].rearrange("p (i s) -> p i s", i=18)[:, :, 144:147],
            tlr[:].rearrange("p (i s) -> p i s", i=18)[:, :, 1:4]), inc="dve")
        p.do("vector", lambda e, halo=halo, tlr=tlr: e.tensor_copy(
            tlr[:].rearrange("p (i s) -> p i s", i=18)[:, :, 1:4],
            halo[:].rearrange("p (i s) -> p i s", i=18)[:, :, 16:19]), inc="dve")
        convT = wb["convT0"] if par == 0 else wb["convT1"]
        batches, cfin = conv_batches(p, "r", halo, convT)
        vrd = dt_chain(p, "r", par, st, fill=batches)
        fev["vrd"] = vrd
        fev["vrows"] = vrd
        fev["exp"] = p.ev("act")
        fev["conv"] = cfin()
        p.wait("scalar", vrd)
        p.do("scalar", lambda e, seq=seq, c=c, sb=sb: e.dma_start(
            sp_flat.ap()[seq, c], sb["flatpk"][0:96, :]), inc="dsp", n=16)
        fev["flat"] = p.ev("dsp")
        fev["sz"] = ld_ev
        fev["szbuf"] = szb
        fev["yfbuf"] = yfb
        return fev

    def back_b1(p, seq, c, par, st, fev):
        y_ev = back_ssd(p, "r", seq, c, par, wb, st, fev, c == NCK - 1)
        p.wait("vector", fev["ld"])
        p.do("vector", lambda e, yfb=fev["yfbuf"]: e.tensor_add(
            wb["y0"][:], wb["y0"][:], yfb[:]), inc="dve")
        yc_ev = p.ev("dve")
        st["yread_dve"] = yc_ev
        yes = []
        for q in range(2):
            pw = [yc_ev, st.get("sevac_dve")] if q == 0 else [yc_ev]
            _, ye_ = pe_tp(p, [wb["y0"][:, (q * 8 + i) * 128:(q * 8 + i + 1) * 128]
                               for i in range(8)],
                           ps_a[:, q * 512:(q + 1) * 512].bitcast(BF16),
                           wb["ycombT"][:, q * 1024:(q + 1) * 1024],
                           pre_waits_pe=pw, eng="scalar")
            yes.append(ye_)
        st["y0_free"] = p.ev("pe")
        p.wait("tensor", yes[0])
        p.wait("tensor", yes[1])
        for j in range(2):
            for k in range(16):
                p.do("tensor", lambda e, j=j, k=k: e.matmul(
                    ps_a[:, j * 512:(j + 1) * 512], wb["ycombT"][:, k * 128:(k + 1) * 128],
                    wb["w_outT"][:, k * D + j * 512:k * D + j * 512 + 512],
                    start=(k == 0), stop=(k == 15)), inc=("pe" if k == 15 else None))
        st["op_pe"] = p.ev("pe")
        p.wait("vector", st["op_pe"])
        p.wait("vector", st.get("s_spill"))
        p.do("vector", lambda e: e.tensor_scalar(wb["sevac"][:], ps_a[:], 0.5, None,
                                                 ALU.mult), inc="dve")
        st["sevac_dve"] = p.ev("dve")
        p.wait("scalar", st["sevac_dve"])
        p.do("scalar", lambda e, seq=seq, c=c: e.dma_start(
            sp_s.ap()[seq, c], wb["sevac"][:]), inc="dsp", n=16)
        st["s_spill"] = p.ev("dsp")
        st["bend"][par] = p.snap()

    def front_b2(p, seq, c, par, st):
        fev = {}
        bend = st["bend"].get(par)
        for e_ in ENGINES:
            p.wait_snap(e_, bend)
        xb = wc["xC0"] if par == 0 else wc["xC1"]
        sb_ = wc["sC0"] if par == 0 else wc["sC1"]
        xnT = wc["xnTC0"] if par == 0 else wc["xnTC1"]
        p.wait("sync", st.get("xnC_free"))
        p.do("sync", lambda e, seq=seq, c=c: e.dma_start(
            wc["xnC"][:], sp_xn.ap()[seq, c]), inc="dld", n=16)
        ld1 = p.ev("dld")
        p.do("sync", lambda e, seq=seq, c=c, xb=xb: e.dma_start(
            xb[:], x_in.ap()[seq, c * CK:(c + 1) * CK, :]), inc="dld", n=16)
        p.do("sync", lambda e, seq=seq, c=c, sb_=sb_: e.dma_start(
            sb_[:], sp_s.ap()[seq, c]), inc="dld", n=16)
        ld_ev = p.ev("dld")
        psv = ps_s[:].bitcast(BF16)
        _, xnT_ev2 = pe_tp(p, [wc["xnC"][:, k * 128:(k + 1) * 128] for k in range(8)],
                           psv, xnT[:],
                           pre_waits_pe=(ld1, st.get("ps_s_g2")))
        st["ps_s_g2"] = xnT_ev2
        st["xnC_free"] = p.ev("pe")
        fev["xnT_ev"] = xnT_ev2
        fev["ld"] = ld_ev
        fev["bufs"] = (xb, sb_, xnT)
        return fev

    def back_b2(p, seq, c, par, st, fev):
        xb, sb_, xnT = fev["bufs"]
        p.wait("tensor", fev["xnT_ev"])
        jev = st.get("jev2", {})
        for j in range(8):
            s3 = j % 3
            psl = (ps_a[:, 0:512], ps_a[:, 512:1024], ps_x[:])[s3]
            p.wait("tensor", jev.get(s3))
            for k in range(8):
                p.do("tensor", lambda e, j=j, k=k, psl=psl, xnT=xnT: e.matmul(
                    psl[:], xnT[:, k * 128:(k + 1) * 128],
                    wc["w1T"][:, k * 4 * D + j * 512:k * 4 * D + j * 512 + 512],
                    start=(k == 0), stop=False))
            p.do("tensor", lambda e, j=j, psl=psl: e.matmul(
                psl[:], sbm["onesbf"][0:1, :], wc["b1row"][0:1, j * 512:(j + 1) * 512],
                start=False, stop=True), inc="pe")
            mm_ev = p.ev("pe")
            p.wait("scalar", mm_ev)
            if j == 0:
                p.wait("scalar", st.get("hT_dtr"))
            p.do("scalar", lambda e, j=j, psl=psl: e.activation(
                wc["hbuf"][:, j * 512:(j + 1) * 512], psl[:], AF.Gelu), inc="act")
            jev[s3] = p.ev("act")
        st["jev2"] = jev
        h_ev = p.ev("act")
        hq = st.get("hq", {})
        for g in range(4):
            qq = 2 + (g % 2)
            pw = [h_ev, hq.get(qq)]
            _, he_ = pe_tp(p, [wc["hbuf"][:, (g * 8 + i) * 128:(g * 8 + i + 1) * 128]
                               for i in range(8)],
                           ps_y[:, qq * 512:(qq + 1) * 512].bitcast(BF16),
                           wc["hT"][:, g * 1024:(g + 1) * 1024],
                           pre_waits_pe=pw)
            hq[qq] = he_
        st["hq"] = hq
        hT_ev = p.ev("dve")
        st["hT_dtr"] = p.ev("pe")
        p.wait("tensor", hT_ev)
        p.wait("tensor", st.get("g_dve"))
        for j in range(2):
            for k in range(32):
                p.do("tensor", lambda e, j=j, k=k: e.matmul(
                    ps_y[:, j * 512:(j + 1) * 512], wc["hT"][:, k * 128:(k + 1) * 128],
                    wc["w2T"][:, k * D + j * 512:k * D + j * 512 + 512],
                    start=(k == 0), stop=False))
            p.do("tensor", lambda e, j=j: e.matmul(
                ps_y[:, j * 512:(j + 1) * 512], sbm["onesbf"][0:1, :],
                wc["b2row"][0:1, j * 512:(j + 1) * 512], start=False, stop=True), inc="pe")
        st["ffn2_pe"] = p.ev("pe")
        p.wait("scalar", st["ffn2_pe"])
        p.do("scalar", lambda e: e.activation(wc["thC"][:], ps_y[:, 0:1024], AF.Tanh,
                                              scale=0.5), inc="act")
        th_ev = p.ev("act")
        p.wait("vector", th_ev)
        p.wait("vector", fev["ld"])
        p.do("vector", lambda e: e.scalar_tensor_tensor(
            wc["gC"][:], wc["thC"][:], 1.0, ps_y[:, 0:1024], ALU.add, ALU.mult), inc="dve")
        st["g_dve"] = p.ev("dve")
        p.do("vector", lambda e, sb_=sb_: e.tensor_mul(sb_[:], sb_[:], wc["gC"][:]), inc="dve")
        p.wait("vector", st.get("out_spill"))
        p.do("vector", lambda e, xb=xb, sb_=sb_: e.tensor_add(wc["outC"][:], xb[:], sb_[:]), inc="dve")
        o_ev = p.ev("dve")
        p.wait("scalar", o_ev)
        p.do("scalar", lambda e, seq=seq, c=c: e.dma_start(
            out_t.ap()[seq, c * CK:(c + 1) * CK, :], wc["outC"][:]), inc="dsp", n=16)
        st["out_spill"] = p.ev("dsp")
        st["bend"][par] = p.snap()

    def prep(p: P):
        sb = sbm
        for dd in ("f", "r"):
            p.do("sync", lambda e, dd=dd: e.dma_start(
                sb[f"convw_{dd}"][:].rearrange("p (i k) -> p i k", k=4),
                pr[f"convw_{dd}"].ap().rearrange("(i p) k -> p i k", p=128)), inc="dld", n=16)
        p.do("sync", lambda e: e.dma_start(sb["triu"][:], c_triu.ap()), inc="dld", n=16)
        p.do("sync", lambda e: e.dma_start(sb["tril"][:], c_tril.ap()), inc="dld", n=16)
        p.do("sync", lambda e: e.dma_start(sb["identf"][:], c_identf.ap()), inc="dld", n=16)
        p.do("sync", lambda e: e.dma_start(sb["identb"][:], c_identb.ap()), inc="dld", n=16)
        p.do("sync", lambda e: e.dma_start(sb["rhs_mask"][0:64, :], c_E2.ap()), inc="dld", n=16)
        p.do("sync", lambda e: e.dma_start(sb["cbias"][:], c_bias.ap()), inc="dld", n=16)
        for di, dd in enumerate(("f", "r")):
            p.do("sync", lambda e, dd=dd: e.dma_start(
                sb["cbst"][0:18, :], pr[f"convb_{dd}"].ap().rearrange("(i p) -> i p", p=128)),
                inc="dld", n=16)
            cb_ld = p.ev("dld")
            p.wait("tensor", cb_ld)
            p.do("tensor", lambda e, di=di: e.transpose(
                ps_s[:, 384 + di * 32:384 + di * 32 + 32], sb["cbst"][0:32, :],
                sb["identf"][0:32, 0:32]), inc="pe")
            cb_tr = p.ev("pe")
            p.wait("vector", cb_tr)
            p.do("vector", lambda e, dd=dd, di=di: e.tensor_copy(
                sb[f"convb_{dd}"][:], ps_s[:, 384 + di * 32:384 + di * 32 + 18]), inc="dve")
            cb_cp = p.ev("dve")
            p.wait("sync", cb_cp)
        p.do("gpsimd", lambda e: e.dma_start(sb["rowf"][0:1, 0:32], pr["dtb_f"].ap()[None, :]), inc="dgp", n=16)
        p.do("gpsimd", lambda e: e.dma_start(sb["rowf"][0:1, 32:64], pr["dtb_r"].ap()[None, :]), inc="dgp", n=16)
        p.do("gpsimd", lambda e: e.dma_start(sb["rowf"][0:1, 64:96], pr["alog_f"].ap()[None, :]), inc="dgp", n=16)
        p.do("gpsimd", lambda e: e.dma_start(sb["rowf"][0:1, 96:128], pr["alog_r"].ap()[None, :]), inc="dgp", n=16)
        p.do("gpsimd", lambda e: e.dma_start(sb["rowD"][0:1, 0:32], pr["dsk_f"].ap()[None, :]), inc="dgp", n=16)
        p.do("gpsimd", lambda e: e.dma_start(sb["rowD"][0:1, 32:64], pr["dsk_r"].ap()[None, :]), inc="dgp", n=16)
        p.do("gpsimd", lambda e: e.dma_start(wp["wtmp0"][0:1, 0:DI], pr["nrm_f"].ap()[None, :]), inc="dgp", n=16)
        p.do("gpsimd", lambda e: e.dma_start(wp["wtmp1"][0:1, 0:DI], pr["nrm_r"].ap()[None, :]), inc="dgp", n=16)
        ld0 = p.ev("dgp")
        p.do("vector", lambda e: e.memset(sb["onesbf"][:], 1.0), inc="dve")
        p.do("vector", lambda e: e.memset(sb["onescol"][:], 1.0), inc="dve")
        for par in range(2):
            p.do("vector", lambda e, par=par: e.memset(sml[par]["lhsT_mask"][64:66, :], 1.0), inc="dve")
        p.wait("scalar", ld0)
        p.wait("scalar", p.ev("dld"))
        p.do("scalar", lambda e: e.activation(sb["rowf"][0:1, 64:128], sb["rowf"][0:1, 64:128],
                                              AF.Exp), inc="act")
        p.do("scalar", lambda e: e.mul(sb["rowf"][0:1, 64:128], sb["rowf"][0:1, 64:128], -1.0), inc="act")
        a_ev = p.ev("act")
        p.wait("vector", a_ev)
        emit_hilo(p, sb["rowf"][0:1, 0:128], sb["rowt"][0:1, 0:128], sb["rowf"][0:1, 128:256],
                  sb["rowt"][0:1, 128:256])
        hl_ev = p.ev("dve")
        p.wait("tensor", hl_ev)
        p.do("tensor", lambda e: e.matmul(ps_y[:, 0:128], sb["onesbf"][0:1, :],
                                          sb["rowt"][0:1, 0:128], start=True, stop=False))
        p.do("tensor", lambda e: e.matmul(ps_y[:, 0:128], sb["onesbf"][0:1, :],
                                          sb["rowt"][0:1, 128:256], start=False, stop=True), inc="pe")
        p.do("tensor", lambda e: e.matmul(ps_y[:, 128:192], sb["onesbf"][0:1, :],
                                          sb["rowD"][0:1, 0:64], start=True, stop=True), inc="pe")
        pb_ev = p.ev("pe")
        p.wait("vector", pb_ev)
        p.do("vector", lambda e: e.tensor_copy(sb["dtb_f"][:], ps_y[:, 0:32]), inc="dve")
        p.do("vector", lambda e: e.tensor_copy(sb["dtb_r"][:], ps_y[:, 32:64]), inc="dve")
        p.do("vector", lambda e: e.tensor_copy(sb["negA_f"][:], ps_y[:, 64:96]), inc="dve")
        p.do("vector", lambda e: e.tensor_copy(sb["negA_r"][:], ps_y[:, 96:128]), inc="dve")
        p.do("vector", lambda e: e.tensor_copy(sb["Dbc_f"][:], ps_y[:, 128:160]), inc="dve")
        p.do("vector", lambda e: e.tensor_copy(sb["Dbc_r"][:], ps_y[:, 160:192]), inc="dve")
        dcp_ev = p.ev("dve")
        p.wait("tensor", dcp_ev)
        for j in range(8):
            dd = "f" if j < 4 else "r"
            jj = j % 4
            wsrc_t = wp["wtmp0"] if j < 4 else wp["wtmp1"]
            p.do("tensor", lambda e, j=j, jj=jj, wsrc_t=wsrc_t: e.matmul(
                ps_y[:, 512 + (j % 2) * 512: 512 + (j % 2) * 512 + 512],
                sb["onesbf"][0:1, :],
                wsrc_t[0:1, jj * 512:(jj + 1) * 512],
                start=True, stop=True), inc="pe")
            nw_ev = p.ev("pe")
            p.wait("vector", nw_ev)
            p.do("vector", lambda e, dd=dd, jj=jj, j=j: e.tensor_copy(
                sb[f"nwbc_{dd}"][:, jj * 512:(jj + 1) * 512],
                ps_y[:, 512 + (j % 2) * 512:512 + (j % 2) * 512 + 512]), inc="dve")
            p.wait("tensor", p.ev("dve"))
        trx_prev = [Z]
        def tr8(wt, col0, ntile, rows, dst3, lev):
            psb = ps_x[:].bitcast(BF16)
            p.wait("tensor", lev)
            p.wait("tensor", trx_prev[0])
            for k in range(ntile):
                p.do("tensor", lambda e, k=k: e.transpose(
                    psb[:, k * 128:k * 128 + rows],
                    wt[0:rows, col0 + k * 128:col0 + (k + 1) * 128],
                    sbm["identb"][0:rows, 0:rows]),
                    inc=("pe" if k == ntile - 1 else None))
            tev = p.ev("pe")
            p.wait("vector", tev)
            p.do("vector", lambda e: e.tensor_copy(
                dst3, psb[:, 0:ntile * 128].rearrange(
                    "p (k c) -> p k c", k=ntile)[:, :, 0:rows]), inc="dve")
            trx_prev[0] = p.ev("dve")
            return tev, trx_prev[0]
        wfree = {}
        for jt in range(35):
            rows = 128 if jt < 34 else 32
            par = jt % 2
            wt = wp["wtmp0"] if par == 0 else wp["wtmp1"]
            p.wait("gpsimd", wfree.get(par, p.ev("pe") if jt < 2 else None))
            p.do("gpsimd", lambda e, jt=jt, wt=wt, rows=rows: e.dma_start(
                wt[0:rows, 0:1024], w_in.ap()[jt * 128:jt * 128 + rows, :]), inc="dgp", n=16)
            lev = p.ev("dgp")
            dst3 = wa["w_inT"][:].rearrange("p (k c) -> p k c", c=DIP)[
                :, :, jt * 128:jt * 128 + rows]
            tev, _ = tr8(wt, 0, 8, rows, dst3, lev)
            wfree[par] = tev
        for rt in range(8):
            par = rt % 2
            wt = wp["wtmp0"] if par == 0 else wp["wtmp1"]
            p.wait("gpsimd", wfree.get(par))
            p.wait("gpsimd", p.ev("dsp"))
            p.do("gpsimd", lambda e, rt=rt, wt=wt: e.dma_start(
                wt[:, 0:2048], w_out.ap()[rt * 128:(rt + 1) * 128, :]), inc="dgp", n=16)
            lev = p.ev("dgp")
            evs = []
            for hb in range(2):
                d3 = wp["trst"][:, hb * 1024:(hb + 1) * 1024].rearrange(
                    "p (k c) -> p k c", k=8)
                tev, dev = tr8(wt, hb * 1024, 8, 128, d3, lev)
                evs.append(dev)
            wfree[par] = tev
            p.wait("scalar", evs[1])
            for ct in range(16):
                p.do("scalar", lambda e, rt=rt, ct=ct: e.dma_start(
                    sp_wot.ap()[ct, :, rt * 128:(rt + 1) * 128],
                    wp["trst"][:, ct * 128:(ct + 1) * 128]), inc="dsp", n=16)
        def ternary(wsrc, nrtile, ncol, spill):
            nhalf = (ncol + 2047) // 2048
            for eng in ("gpsimd", "scalar", "sync", "vector"):
                for s_ in ("dtr", "dsp", "dve", "act"):
                    p.wait(eng, p.ev(s_))
            p.do("vector", lambda e: e.memset(wp["acc"][:], 0.0), inc="dve")
            absev = {}
            for it in range(nrtile):
                for hf in range(nhalf):
                    cw = min(2048, ncol - hf * 2048)
                    par = (it * nhalf + hf) % 2
                    wf = wp["wf0"] if par == 0 else wp["wf1"]
                    if par in absev:
                        p.wait("gpsimd", absev[par])
                    p.do("gpsimd", lambda e, it=it, hf=hf, cw=cw, wf=wf: e.dma_start(
                        wf[:, 0:cw], wsrc.ap()[it * 128:(it + 1) * 128,
                                               hf * 2048:hf * 2048 + cw]), inc="dgp", n=16)
                    lev = p.ev("dgp")
                    p.wait("scalar", lev)
                    p.wait("scalar", p.ev("dve"))
                    p.do("scalar", lambda e, wf=wf, cw=cw: e.activation(
                        wp["wi0"][:].bitcast(F32)[:, 0:cw],
                        wf[:, 0:cw], AF.Abs, accum_out=wp["acc"][:, 1:2]), inc="act")
                    absev[par] = p.ev("act")
                    p.wait("vector", absev[par])
                    p.do("vector", lambda e: e.tensor_add(wp["acc"][:, 0:1], wp["acc"][:, 0:1],
                                                          wp["acc"][:, 1:2]), inc="dve")
            emit_hilo(p, wp["acc"][:, 0:1], sb["colbf"][:, 0:1],
                      wp["acc"][:, 3:4], sb["colbf"][:, 1:2])
            hl2 = p.ev("dve")
            p.wait("tensor", hl2)
            p.do("tensor", lambda e: e.matmul(ps_s[0:1, 0:1], sb["colbf"][:, 0:1],
                                              sb["onescol"][:, 0:1], start=True, stop=False))
            p.do("tensor", lambda e: e.matmul(ps_s[0:1, 0:1], sb["colbf"][:, 1:2],
                                              sb["onescol"][:, 0:1], start=False, stop=True), inc="pe")
            ps_ev = p.ev("pe")
            p.wait("vector", ps_ev)
            p.do("vector", lambda e: e.tensor_scalar(wp["acc"][0:1, 4:5], ps_s[0:1, 0:1],
                                                     1.0 / (nrtile * 128 * ncol), 1e-8,
                                                     ALU.mult, ALU.max), inc="dve")
            p.do("vector", lambda e: e.reciprocal(wp["acc"][0:1, 5:6], wp["acc"][0:1, 4:5]), inc="dve")
            emit_hilo(p, wp["acc"][0:1, 5:6], sb["rowbf"][0:1, 0:1], wp["acc"][0:1, 6:7],
                      sb["rowbf"][0:1, 1:2])
            iv_ev = p.ev("dve")
            p.wait("tensor", iv_ev)
            p.do("tensor", lambda e: e.matmul(ps_s[:, 4:5], sb["onesbf"][0:1, :],
                                              sb["rowbf"][0:1, 0:1], start=True, stop=False))
            p.do("tensor", lambda e: e.matmul(ps_s[:, 4:5], sb["onesbf"][0:1, :],
                                              sb["rowbf"][0:1, 1:2], start=False, stop=True), inc="pe")
            ib_ev = p.ev("pe")
            p.wait("vector", ib_ev)
            p.do("vector", lambda e: e.tensor_copy(wp["acc"][:, 7:8], ps_s[:, 4:5]), inc="dve")
            trev = {}
            for it in range(nrtile):
                for hf in range(nhalf):
                    cw = min(2048, ncol - hf * 2048)
                    par = (it * nhalf + hf) % 2
                    wf = wp["wf0"] if par == 0 else wp["wf1"]
                    wt = wp["wtmp0"] if par == 0 else wp["wtmp1"]
                    if par in trev:
                        p.wait("gpsimd", trev[par])
                    p.do("gpsimd", lambda e, it=it, hf=hf, cw=cw, wf=wf: e.dma_start(
                        wf[:, 0:cw], wsrc.ap()[it * 128:(it + 1) * 128,
                                               hf * 2048:hf * 2048 + cw]), inc="dgp", n=16)
                    lev = p.ev("dgp")
                    p.wait("vector", lev)
                    p.do("vector", lambda e, wf=wf, cw=cw: e.tensor_scalar(
                        wf[:, 0:cw], wf[:, 0:cw], wp["acc"][:, 7:8], None, ALU.mult), inc="dve")
                    q1_ = p.ev("dve")
                    p.wait("scalar", q1_)
                    if par in trev:
                        p.wait("scalar", trev[par])
                    av = wp["wi0"][:].bitcast(F32)
                    p.do("scalar", lambda e, wf=wf, cw=cw, av=av: e.activation(
                        av[:, 0:cw], wf[:, 0:cw], AF.Abs), inc="act")
                    p.do("scalar", lambda e, wf=wf, wt=wt, cw=cw: e.activation(
                        wt[:, 0:cw], wf[:, 0:cw], AF.Sign), inc="act")
                    q2_ = p.ev("act")
                    p.wait("vector", q2_)
                    p.do("vector", lambda e, cw=cw, av=av: e.tensor_scalar(
                        av[:, 0:cw], av[:, 0:cw], 0.5, None, ALU.is_ge), inc="dve")
                    p.do("vector", lambda e, wt=wt, cw=cw, av=av: e.tensor_mul(
                        wt[:, 0:cw], wt[:, 0:cw], av[:, 0:cw]), inc="dve")
                    qev = p.ev("dve")
                    p.wait("vector", p.ev("dsp"))
                    devs = []
                    for hb in range(cw // 1024):
                        d3 = wp["trst"][:, hb * 1024:(hb + 1) * 1024].rearrange(
                            "p (k c) -> p k c", k=8)
                        tev, dev = tr8(wt, hb * 1024, 8, 128, d3, qev)
                        devs.append(dev)
                    trev[par] = tev
                    p.wait("scalar", devs[-1])
                    for ct in range(cw // 128):
                        ctg = hf * 16 + ct
                        p.do("scalar", lambda e, it=it, ctg=ctg, ct=ct: e.dma_start(
                            spill.ap()[ctg, :, it * 128:(it + 1) * 128],
                            wp["trst"][:, ct * 128:(ct + 1) * 128]), inc="dsp", n=16)
        ternary(w1, 32, 1024, sp_w1t)
        ternary(w2, 8, 4096, sp_w2t)

    def run_pass(p, front, back, order):
        st = {"bend": {}, "psyg": [None] * 4}
        fevs = {}
        items = list(order)
        for i in range(len(items) + 1):
            if i >= 1:
                seq, c = items[i - 1]
                back(p, seq, c, (i - 1) % 2, st, fevs.pop(i - 1))
            if i < len(items):
                seq, c = items[i]
                fevs[i] = front(p, seq, c, i % 2, st)

    def emit(p: P):
        prep(p)
        p.barrier()

        def fa(p_, seq, c, par, st):
            if c == 0:
                sq = seq % 2
                p_.do("vector", lambda e, sq=sq: e.memset(
                    wa["S0" if sq == 0 else "S1"][:], 0.0), inc="dve")
                p_.do("vector", lambda e, sq=sq: e.memset(
                    wa["stil0" if sq == 0 else "stil1"][:], 0.0), inc="dve")
                ev = p_.ev("dve")
                st[("sadd_q", sq)] = [ev] * 4
                st[("scan_q", sq)] = [ev] * 4
            return front_a(p_, seq, c, par, st)
        orderA = [(s, c) for s in range(NSEQ) for c in range(NCK)]
        run_pass(p, fa, back_a, orderA)
        p.barrier()
        for kt in range(16):
            p.do("sync", lambda e, kt=kt: e.dma_start(
                wb["w_outT"][:, kt * D:(kt + 1) * D], sp_wot.ap()[kt]), inc="dld", n=16)
        wo_ev = p.ev("dld")

        def fb1(p_, seq, c, par, st):
            p_.wait("tensor", wo_ev)
            if c == NCK - 1:
                sq = seq % 2
                p_.do("vector", lambda e, sq=sq: e.memset(
                    wb["S0" if sq == 0 else "S1"][:], 0.0), inc="dve")
                p_.do("vector", lambda e, sq=sq: e.memset(
                    wb["stil0" if sq == 0 else "stil1"][:], 0.0), inc="dve")
                ev = p_.ev("dve")
                st[("sadd_q", sq)] = [ev] * 4
                st[("scan_q", sq)] = [ev] * 4
            return front_b1(p_, seq, c, par, st)
        orderB1 = [(s, NCK - 1 - ci) for s in range(NSEQ) for ci in range(NCK)]
        run_pass(p, fb1, back_b1, orderB1)
        p.barrier()
        for kt in range(8):
            p.do("sync", lambda e, kt=kt: e.dma_start(
                wc["w1T"][:, kt * 4 * D:(kt + 1) * 4 * D], sp_w1t.ap()[kt]), inc="dld", n=16)
        for kt in range(32):
            p.do("sync", lambda e, kt=kt: e.dma_start(
                wc["w2T"][:, kt * D:(kt + 1) * D], sp_w2t.ap()[kt]), inc="dld", n=16)
        p.do("gpsimd", lambda e: e.dma_start(wc["b1row"][:], b1.ap()[None, :]), inc="dgp", n=16)
        p.do("gpsimd", lambda e: e.dma_start(wc["b2row"][:], b2.ap()[None, :]), inc="dgp", n=16)
        wl_ev = p.ev("dld")
        bl_ev = p.ev("dgp")

        def fb2(p_, seq, c, par, st):
            p_.wait("tensor", wl_ev)
            p_.wait("tensor", bl_ev)
            p_.wait("sync", wl_ev)
            return front_b2(p_, seq, c, par, st)
        orderB2 = [(s, c) for s in range(NSEQ) for c in range(NCK)]
        run_pass(p, fb2, back_b2, orderB2)
        for e in ENGINES:
            for s in p.sems:
                if p.c[s] > 0:
                    p.wait(e, (s, p.c[s]))

    sems = {s: nc.alloc_semaphore(s) for s in sems_proto}
    with nc.Block() as blk:
        def mk(role):
            def body(eng):
                emit(P(nc, role, eng, sems))
            return body
        blk.sync(mk("sync"))
        blk.tensor(mk("tensor"))
        blk.vector(mk("vector"))
        blk.scalar(mk("scalar"))
        blk.gpsimd(mk("gpsimd"))
    return nc


def make_in_maps(inputs, xs, nseq):
    base = {
        "W_in": np.asarray(inputs["L0_attn_W_in"], np.float32),
        "W_out": np.asarray(inputs["L0_attn_W_out"], np.float32),
        "W1": np.asarray(inputs["L0_ffn_W1"], np.float32),
        "b1": np.asarray(inputs["L0_ffn_b1"], np.float32),
        "W2": np.asarray(inputs["L0_ffn_W2"], np.float32),
        "b2": np.asarray(inputs["L0_ffn_b2"], np.float32),
    }
    for dd in ("f", "r"):
        base[f"conv_w_{dd}"] = np.asarray(inputs[f"L0_attn_conv_w_{dd}"], np.float32)
        base[f"conv_b_{dd}"] = np.asarray(inputs[f"L0_attn_conv_b_{dd}"], np.float32)
        base[f"dt_bias_{dd}"] = np.asarray(inputs[f"L0_attn_dt_bias_{dd}"], np.float32)
        base[f"A_log_{dd}"] = np.asarray(inputs[f"L0_attn_A_log_{dd}"], np.float32)
        base[f"D_{dd}"] = np.asarray(inputs[f"L0_attn_D_{dd}"], np.float32)
        base[f"norm_w_{dd}"] = np.asarray(inputs[f"L0_attn_norm_w_{dd}"], np.float32)
    in_maps = []
    for core in range(NCORE):
        m = dict(base)
        m["x"] = np.ascontiguousarray(xs[nseq * core:nseq * core + nseq])
        in_maps.append(m)
    return in_maps


def kernel(**inputs):
    xs = np.asarray(inputs["L0_attn_x"], np.float32)
    nc = build_nc(NSEQ=2, NCK=32)
    in_maps = make_in_maps(inputs, xs, 2)
    res = run_bass_kernel_spmd(nc, in_maps, core_ids=list(range(NCORE)))
    return np.concatenate([res.results[i]["out"] for i in range(NCORE)], axis=0)


# version 19
# speedup vs baseline: 1.2135x; 1.0052x over previous
import numpy as np
import ml_dtypes
import concourse.bass as bass
import concourse.mybir as mybir
from concourse.bass_utils import run_bass_kernel_spmd

F32 = mybir.dt.float32
BF16 = mybir.dt.bfloat16
I32 = mybir.dt.int32
AF = mybir.ActivationFunctionType
ALU = mybir.AluOpType
AX = mybir.AxisListType

B, L, D = 16, 4096, 1024
DI, DSTATE, NH, HD = 2048, 128, 32, 64
CONVD = 2304
DIP = 4384
CK = 128
CAP = 1e4
EPS = 1e-5
NCORE = 8
HSTR = 160
HDAT = 16

ENGINES = ("sync", "tensor", "vector", "scalar", "gpsimd")
CSEMS = ("pe", "dve", "act", "pool", "dtr", "dld", "dsp", "dgp")
Z = ("pe", 0)


def bfq(x):
    return np.asarray(x, dtype=np.float32).astype(ml_dtypes.bfloat16)


class P:
    def __init__(self, nc, role, eng, sems):
        self.nc, self.role, self.eng, self.sems = nc, role, eng, sems
        self.c = {k: 0 for k in sems}
        self.tq = 0

    OWN = {"vector": "dve", "scalar": "act", "gpsimd": "pool"}

    def do(self, engine, emit, inc=None, n=1, chain=True):
        own = self.OWN.get(engine)
        serialize = own is not None and (inc is None or inc == own)
        if serialize:
            prev = self.c[own]
            inc, n = own, 1
        if inc is not None:
            self.c[inc] += n
        if self.role == engine:
            if serialize and chain and prev > 0:
                self.eng.wait_ge(self.sems[own], prev)
            inst = emit(self.eng)
            if inc is not None:
                inst.then_inc(self.sems[inc], n)

    def tqueue(self):
        return "sync"

    def ev(self, sem):
        return (sem, self.c[sem])

    def wait(self, engine, ev):
        if ev is None or ev[1] <= 0:
            return
        if self.role == engine:
            self.eng.wait_ge(self.sems[ev[0]], ev[1])

    def snap(self):
        return {s: self.c[s] for s in CSEMS}

    def wait_snap(self, engine, snap):
        if snap is None:
            return
        for s in CSEMS:
            self.wait(engine, (s, snap[s]))

    def barrier(self):
        own = {"sync": "b_sy", "tensor": "b_pe", "vector": "b_dv",
               "scalar": "b_ac", "gpsimd": "b_gp"}
        for e in ENGINES:
            self.do(e, lambda en: en.nop(), inc=own[e])
        for e in ENGINES:
            for s in self.sems:
                if self.c[s] > 0:
                    self.wait(e, (s, self.c[s]))


def build_nc(NSEQ=2, NCK=32, detect_races=True, dbg=False):
    LL = NCK * CK
    nc = bass.Bass("TRN2", target_bir_lowering=False,
                   detect_race_conditions=detect_races)

    x_in = nc.dram_tensor("x", [NSEQ, LL, D], F32, kind="ExternalInput")
    out_t = nc.dram_tensor("out", [NSEQ, LL, D], F32, kind="ExternalOutput")
    w_in = nc.dram_tensor("W_in", [DIP, D], F32, kind="ExternalInput")
    w_out = nc.dram_tensor("W_out", [D, DI], F32, kind="ExternalInput")
    w1 = nc.dram_tensor("W1", [4 * D, D], F32, kind="ExternalInput")
    b1 = nc.dram_tensor("b1", [4 * D], F32, kind="ExternalInput")
    w2 = nc.dram_tensor("W2", [D, 4 * D], F32, kind="ExternalInput")
    b2 = nc.dram_tensor("b2", [D], F32, kind="ExternalInput")
    pr = {}
    for dd in ("f", "r"):
        pr[f"convw_{dd}"] = nc.dram_tensor(f"conv_w_{dd}", [CONVD, 4], F32, kind="ExternalInput")
        pr[f"convb_{dd}"] = nc.dram_tensor(f"conv_b_{dd}", [CONVD], F32, kind="ExternalInput")
        pr[f"dtb_{dd}"] = nc.dram_tensor(f"dt_bias_{dd}", [NH], F32, kind="ExternalInput")
        pr[f"alog_{dd}"] = nc.dram_tensor(f"A_log_{dd}", [NH], F32, kind="ExternalInput")
        pr[f"dsk_{dd}"] = nc.dram_tensor(f"D_{dd}", [NH], F32, kind="ExternalInput")
        pr[f"nrm_{dd}"] = nc.dram_tensor(f"norm_w_{dd}", [DI], F32, kind="ExternalInput")

    E1 = np.repeat(np.eye(NH, dtype=np.float32), CK, axis=1)
    c_E2 = nc.inline_tensor(bfq(np.vstack([E1, E1])), "cE2")
    c_triu = nc.inline_tensor(bfq(np.triu(np.ones((CK, CK), np.float32))), "ctriu")
    c_tril = nc.inline_tensor(bfq(np.tril(np.ones((CK, CK), np.float32))), "ctril")
    c_identf = nc.inline_tensor(np.eye(CK, dtype=np.float32), "cidentf")
    c_identb = nc.inline_tensor(np.eye(CK).astype(ml_dtypes.bfloat16), "cidentb")
    c_bias = nc.inline_tensor(np.tile(np.array([[1.0, EPS, 0.0, 0.0]], dtype=np.float32), (128, 1)), "cbias")

    KD = "ExternalOutput" if dbg else "Internal"
    sp_yf = nc.dram_tensor("sp_yf", [NSEQ, NCK, CK, DI], BF16, kind=KD)
    sp_sz = nc.dram_tensor("sp_sz", [NSEQ, NCK, CK, DI], BF16, kind=KD)
    sp_xbc = nc.dram_tensor("sp_xbc", [NSEQ, NCK, CK, 18 * HSTR], BF16, kind=KD)
    sp_dt = nc.dram_tensor("sp_dt", [NSEQ, NCK, CK, NH], F32, kind=KD)
    sp_xn = nc.dram_tensor("sp_xn", [NSEQ, NCK, CK, D], BF16, kind=KD)
    sp_s = nc.dram_tensor("sp_s", [NSEQ, NCK, CK, D], F32, kind=KD)
    sp_flat = nc.dram_tensor("sp_flat", [NSEQ, NCK, 96, CK], BF16, kind=KD)
    sp_w1t = nc.dram_tensor("sp_w1t", [8, CK, 4 * D], BF16, kind=KD)
    sp_w2t = nc.dram_tensor("sp_w2t", [32, CK, D], BF16, kind=KD)
    sp_wot = nc.dram_tensor("sp_wot", [16, CK, D], BF16, kind=KD)

    SB_BASE = 16512
    SB_LIMIT = 229000
    cur = [0]

    def salloc(name, shape, dtype):
        esz = 2 if dtype == BF16 else 4
        nbytes = int(np.prod(shape[1:])) * esz
        o = (cur[0] + 31) // 32 * 32
        cur[0] = o + nbytes
        assert cur[0] <= SB_LIMIT, f"SBUF overflow at {name}: {cur[0]}"
        return nc.alloc_sbuf_tensor_at(name, list(shape), dtype, offset=o)

    def region(base, specs, pref=""):
        cur[0] = base + (SB_BASE if base < SB_BASE else 0)
        out = {}
        for nm, shape, dt_ in specs:
            out[nm] = salloc(pref + nm, shape, dt_)
        return out, cur[0]

    sb0, P0E = region(0, [
        ("onesbf", [2, 128], BF16),
        ("onescol", [128, 2], BF16),
        ("col1", [128, 16], F32),
        ("rowbf", [1, 16], BF16),
        ("rowf", [1, 256], F32),
        ("rowt", [1, 256], BF16),
        ("rowD", [1, 64], BF16),
        ("colbf", [128, 4], BF16),
        ("cbias", [128, 4], F32),
        ("identb", [128, 128], BF16),
        ("scrA", [128, 2048], BF16),
    ])
    SML = [
        ("dtraw", [128, NH], F32),
        ("dt", [128, NH], F32),
        ("sm1", [128, NH], F32),
        ("sm2", [128, NH], F32),
        ("acum", [128, 2 * NH], F32),
        ("totbc", [128, NH], F32),
        ("dstate", [128, NH], F32),
        ("exptot", [128, NH], F32),
        ("dtds", [128, NH], BF16),
        ("pk96", [128, 96], F32),
        ("pk64", [128, 64], F32),
        ("vtok", [128, 32], F32),
        ("bstg", [128, 32], BF16),
        ("flatpk", [96, 128], BF16),
        ("hilo", [128, 2 * NH], BF16),
        ("tot_hl", [1, 2 * NH], BF16),
        ("trow", [1, NH], F32),
        ("lhsT_mask", [128, 128], BF16),
    ]
    sb1, P1E0 = region(P0E, [
        ("rhs_mask", [128, 4096], BF16),
        ("expR", [1, 4096], BF16),
        ("triu", [128, 128], BF16),
        ("tril", [128, 128], BF16),
        ("identf", [128, 128], F32),
        ("convw_f", [128, 72], F32),
        ("convw_r", [128, 72], F32),
        ("convb_f", [128, 18], F32),
        ("convb_r", [128, 18], F32),
        ("dtb_f", [128, NH], F32),
        ("dtb_r", [128, NH], F32),
        ("negA_f", [128, NH], F32),
        ("negA_r", [128, NH], F32),
        ("Dbc_f", [128, NH], BF16),
        ("Dbc_r", [128, NH], BF16),
        ("nwbc_f", [128, DI], BF16),
        ("nwbc_r", [128, DI], BF16),
        ("cbst", [32, 128], F32),
    ])
    sml0, P1E1 = region(P1E0, SML, pref="s0_")
    sml1, P1E = region(P1E1, SML, pref="s1_")
    sml = [sml0, sml1]
    sbm = {}
    sbm.update(sb0)
    sbm.update(sb1)

    wa, WAE = region(P1E, [
        ("w_inT", [128, 8 * DIP], BF16),
        ("xA0", [128, D], F32),
        ("xA1", [128, D], F32),
        ("xn", [128, D], BF16),
        ("xnT", [128, 8 * 128], BF16),
        ("siluz0", [128, DI], BF16),
        ("siluz1", [128, DI], BF16),
        ("stage", [128, 4 * 512], BF16),
        ("halo", [128, 18 * HSTR], BF16),
        ("tail0", [128, 18 * 4], BF16),
        ("tail1", [128, 18 * 4], BF16),
        ("convT0", [128, CONVD], BF16),
        ("convT1", [128, CONVD], BF16),
        ("xstok", [128, DI], BF16),
        ("btok", [128, 128], BF16),
        ("gt", [128, 128], BF16),
        ("gmm", [128, 4096], BF16),
        ("gmv", [128, 4096], BF16),
        ("y0", [128, DI], BF16),
        ("dx", [128, DI], BF16),
        ("S0", [128, DI], F32),
        ("S1", [128, DI], F32),
        ("stil0", [128, DI], BF16),
        ("stil1", [128, DI], BF16),
        ("xdec", [128, DI], BF16),
    ], pref="a_")

    WPB = P1E + ((8 * DIP * 2 + 31) // 32 * 32)
    wp, WPE = region(WPB, [
        ("wf0", [128, 2048], F32),
        ("wf1", [128, 2048], F32),
        ("wi0", [128, 2048], I32),
        ("wtmp0", [128, 2048], BF16),
        ("wtmp1", [128, 2048], BF16),
        ("trst", [128, 2048], BF16),
        ("acc", [128, 8], F32),
    ], pref="p_")

    wb, WBE = region(P1E, [
        ("w_outT", [128, 16 * D], BF16),
        ("szB0", [128, DI], BF16),
        ("szB1", [128, DI], BF16),
        ("yfB0", [128, DI], BF16),
        ("yfB1", [128, DI], BF16),
        ("haloB0", [128, 18 * HSTR], BF16),
        ("haloB1", [128, 18 * HSTR], BF16),
        ("convT0", [128, CONVD], BF16),
        ("convT1", [128, CONVD], BF16),
        ("tailR0", [128, 18 * 4], BF16),
        ("tailR1", [128, 18 * 4], BF16),
        ("xstok", [128, DI], BF16),
        ("btok", [128, 128], BF16),
        ("gt", [128, 128], BF16),
        ("gmm", [128, 4096], BF16),
        ("gmv", [128, 4096], BF16),
        ("y0", [128, DI], BF16),
        ("dx", [128, DI], BF16),
        ("S0", [128, DI], F32),
        ("S1", [128, DI], F32),
        ("stil0", [128, DI], BF16),
        ("stil1", [128, DI], BF16),
        ("xdec", [128, DI], BF16),
        ("ycombT", [128, DI], BF16),
        ("sevac", [128, D], F32),
    ], pref="b_")

    wc, WCE = region(P0E, [
        ("w1T", [128, 8 * 4 * D], BF16),
        ("w2T", [128, 32 * D], BF16),
        ("xC0", [128, D], F32),
        ("xC1", [128, D], F32),
        ("xnC", [128, D], BF16),
        ("xnTC0", [128, 8 * 128], BF16),
        ("xnTC1", [128, 8 * 128], BF16),
        ("hbuf", [128, 4 * D], BF16),
        ("hT", [128, 32 * 128], BF16),
        ("sC0", [128, D], F32),
        ("sC1", [128, D], F32),
        ("thC", [128, D], BF16),
        ("gC", [128, D], F32),
        ("outC", [128, D], F32),
        ("b1row", [1, 4 * D], BF16),
        ("b2row", [1, D], BF16),
    ], pref="c_")

    nc.const_aps.register(F32, 1.0, sb0["cbias"][:, 0:1])
    nc.const_aps.register(F32, EPS, sb0["cbias"][:, 1:2])
    nc.const_aps.register(F32, 0.0, sb0["cbias"][:, 2:3])
    nc.const_aps.register(BF16, 0.0, sb0["cbias"][:, 3:4].bitcast(BF16)[:, 0:1])

    ps_a = nc.alloc_psum_tensor("ps_a", [128, 1024], F32)
    ps_s = nc.alloc_psum_tensor("ps_s", [128, 512], F32)
    ps_x = nc.alloc_psum_tensor("ps_x", [128, 512], F32)
    ps_y = nc.alloc_psum_tensor("ps_y", [128, 2048], F32)

    sems_proto = list(CSEMS) + ["b_sy", "b_pe", "b_dv", "b_ac", "b_gp"]

    def emit_hilo(p, src32, hi_bf, tmp32, lo_bf):
        p.do("vector", lambda e: e.tensor_copy(hi_bf, src32), inc="dve")
        p.do("vector", lambda e: e.tensor_copy(tmp32, hi_bf), inc="dve")
        p.do("vector", lambda e: e.tensor_sub(tmp32, src32, tmp32), inc="dve")
        p.do("vector", lambda e: e.tensor_copy(lo_bf, tmp32), inc="dve")

    def bview(t, nrep):
        return t[:, :, None].broadcast_to([t.shape[0], t.shape[1], nrep])

    def pe_tp(p, srcs, psview, dst, pre_waits_pe=(), pre_waits_dve=(), eng="vector"):
        for ev in pre_waits_pe:
            p.wait("tensor", ev)
        n = len(srcs)
        for i, s_ in enumerate(srcs):
            p.do("tensor", lambda e, i=i, s_=s_: e.transpose(
                psview[:, i * 128:(i + 1) * 128], s_, sbm["identb"][:]),
                inc=("pe" if i == n - 1 else None))
        pe_ev = p.ev("pe")
        p.wait(eng, pe_ev)
        for ev in pre_waits_dve:
            p.wait(eng, ev)
        srcv = psview[:, 0:n * 128]
        if len(dst.shape) == 3:
            srcv = srcv.rearrange("p (a b) -> p a b", b=128)
        if eng == "vector":
            p.do("vector", lambda e: e.tensor_copy(dst, srcv), inc="dve")
            return pe_ev, p.ev("dve")
        p.do("scalar", lambda e: e.activation(dst, srcv, AF.Identity), inc="act")
        return pe_ev, p.ev("act")

    def dt_chain(p, dd, par, st, fill=None):
        sb = sml[par]
        tri = sbm["triu"] if dd == "f" else sbm["tril"]
        def drain(n):
            if fill:
                for _ in range(min(n, len(fill))):
                    fill.pop(0)()
        p.do("vector", lambda e: e.tensor_add(sb["sm1"][:], sb["dtraw"][:],
                                              sbm[f"dtb_{dd}"][:]), inc="dve")
        t1_ev = p.ev("dve")
        p.wait("scalar", t1_ev)
        p.do("scalar", lambda e: e.activation(sb["sm2"][:], sb["sm1"][:], AF.Abs), inc="act")
        p.do("scalar", lambda e: e.activation(sb["sm2"][:], sb["sm2"][:], AF.Exp, scale=-1.0), inc="act")
        p.do("scalar", lambda e: e.activation(sb["sm2"][:], sb["sm2"][:], AF.Ln, bias=1.0), inc="act")
        p.do("scalar", lambda e: e.activation(sb["dt"][:], sb["sm1"][:], AF.Relu), inc="act")
        sp_ev = p.ev("act")
        drain(2)
        p.wait("vector", sp_ev)
        p.do("vector", lambda e: e.tensor_add(sb["dt"][:], sb["dt"][:], sb["sm2"][:]), inc="dve")
        p.do("vector", lambda e: e.tensor_scalar_max(sb["dt"][:], sb["dt"][:], 1e-30), inc="dve")
        dt_ev = p.ev("dve")
        p.wait("scalar", dt_ev)
        p.do("scalar", lambda e: e.activation(sb["acum"][:, NH:2 * NH], sb["dt"][:], AF.Ln), inc="act")
        lnd_ev = p.ev("act")
        p.do("vector", lambda e: e.tensor_mul(sb["sm1"][:], sb["dt"][:], sbm[f"negA_{dd}"][:]), inc="dve")
        emit_hilo(p, sb["sm1"][:], sb["hilo"][:, 0:NH], sb["sm2"][:], sb["hilo"][:, NH:2 * NH])
        hl_ev = p.ev("dve")
        p.wait("tensor", hl_ev)
        p.wait("tensor", st.get("ps_s_g"))
        p.do("tensor", lambda e: e.matmul(ps_s[:, 0:NH], tri[:], sb["hilo"][:, 0:NH],
                                          start=True, stop=False))
        p.do("tensor", lambda e: e.matmul(ps_s[:, 0:NH], tri[:], sb["hilo"][:, NH:2 * NH],
                                          start=False, stop=True), inc="pe")
        p.do("tensor", lambda e: e.matmul(ps_s[0:1, 32:32 + NH], sbm["onescol"][:, 0:1],
                                          sb["hilo"][:, 0:NH], start=True, stop=False))
        p.do("tensor", lambda e: e.matmul(ps_s[0:1, 32:32 + NH], sbm["onescol"][:, 0:1],
                                          sb["hilo"][:, NH:2 * NH], start=False, stop=True), inc="pe")
        cs_ev = p.ev("pe")
        drain(2)
        p.wait("vector", cs_ev)
        p.do("vector", lambda e: e.tensor_copy(sb["acum"][:, 0:NH], ps_s[:, 0:NH]), inc="dve")
        p.do("vector", lambda e: e.tensor_copy(sb["trow"][:], ps_s[0:1, 32:32 + NH]), inc="dve")
        emit_hilo(p, sb["trow"][:], sb["tot_hl"][0:1, 0:NH], sb["sm2"][0:1, 0:NH],
                  sb["tot_hl"][0:1, NH:2 * NH])
        th_ev = p.ev("dve")
        p.wait("tensor", th_ev)
        p.do("tensor", lambda e: e.matmul(ps_s[:, 64:64 + NH], sbm["onesbf"][0:1, :],
                                          sb["tot_hl"][0:1, 0:NH], start=True, stop=False))
        p.do("tensor", lambda e: e.matmul(ps_s[:, 64:64 + NH], sbm["onesbf"][0:1, :],
                                          sb["tot_hl"][0:1, NH:2 * NH], start=False, stop=True), inc="pe")
        tb_ev = p.ev("pe")
        drain(2)
        p.wait("vector", tb_ev)
        p.wait("vector", lnd_ev)
        p.do("vector", lambda e: e.tensor_copy(sb["totbc"][:], ps_s[:, 64:64 + NH]), inc="dve")
        p.do("vector", lambda e: e.tensor_sub(sb["dstate"][:], sb["totbc"][:], sb["acum"][:, 0:NH]), inc="dve")
        ds_ev = p.ev("dve")
        p.wait("scalar", ds_ev)
        p.do("scalar", lambda e: e.activation(sb["dstate"][:], sb["dstate"][:], AF.Exp), inc="act")
        p.do("scalar", lambda e: e.activation(sb["exptot"][:], sb["totbc"][:], AF.Exp), inc="act")
        p.do("scalar", lambda e: e.activation(sb["pk96"][:, 64:96], sb["acum"][:, 0:NH], AF.Exp), inc="act")
        ex_ev = p.ev("act")
        drain(1)
        p.wait("vector", ex_ev)
        p.do("vector", lambda e: e.tensor_mul(sb["dtds"][:], sb["dt"][:], sb["dstate"][:]), inc="dve")
        p.do("vector", lambda e: e.tensor_copy(sb["bstg"][:], sb["acum"][:, 0:NH]), inc="dve")
        p.do("vector", lambda e: e.tensor_copy(sb["pk96"][:, 0:32], sb["bstg"][:]), inc="dve")
        p.do("vector", lambda e: e.tensor_sub(sb["pk96"][:, 32:64], sb["acum"][:, 0:NH],
                                              sb["pk96"][:, 0:32]), inc="dve")
        p.do("vector", lambda e: e.tensor_sub(sb["vtok"][:], sb["acum"][:, NH:2 * NH],
                                              sb["acum"][:, 0:NH]), inc="dve")
        p.do("vector", lambda e: e.tensor_copy(sb["bstg"][:], sb["vtok"][:]), inc="dve")
        p.do("vector", lambda e: e.tensor_copy(sb["pk64"][:, 0:32], sb["bstg"][:]), inc="dve")
        p.do("vector", lambda e: e.tensor_sub(sb["pk64"][:, 32:64], sb["vtok"][:],
                                              sb["pk64"][:, 0:32]), inc="dve")
        pkd_ev = p.ev("dve")
        p.wait("tensor", pkd_ev)
        p.do("tensor", lambda e: e.transpose(ps_s[0:96, 128:256], sb["pk96"][:, 0:96],
                                             sbm["identf"][:]), inc="pe")
        p.do("tensor", lambda e: e.transpose(ps_s[0:64, 256:384], sb["pk64"][:, 0:64],
                                             sbm["identf"][:]), inc="pe")
        tr_ev = p.ev("pe")
        drain(1)
        p.wait("vector", tr_ev)
        p.do("vector", lambda e: e.tensor_copy(sb["flatpk"][0:96, :], ps_s[0:96, 128:256]), inc="dve")
        p.do("vector", lambda e: e.tensor_copy(sb["lhsT_mask"][0:64, :], ps_s[0:64, 256:384]), inc="dve")
        st["ps_s_g"] = p.ev("dve")
        drain(8)
        return p.ev("dve")

    def conv_batches(p, dd, halo, out_buf):
        wv = sbm[f"convw_{dd}"]
        offs = [13, 14, 15, 16] if dd == "f" else [19, 18, 17, 16]
        def hs(i, k):
            return halo[:, i * HSTR + offs[k]:i * HSTR + offs[k] + 128]
        evs = {}
        def tap(i0, i1, k, first):
            def go():
                for i in range(i0, i1):
                    if k == 0:
                        p.do("vector", lambda e, i=i: e.tensor_scalar(
                            out_buf[:, i * 128:(i + 1) * 128], hs(i, 0),
                            wv[:, i * 4:i * 4 + 1], sbm[f"convb_{dd}"][:, i:i + 1],
                            ALU.mult, ALU.add), inc="dve", chain=(first and i == i0))
                    else:
                        p.do("vector", lambda e, i=i, k=k: e.scalar_tensor_tensor(
                            out_buf[:, i * 128:(i + 1) * 128], hs(i, k),
                            wv[:, i * 4 + k:i * 4 + k + 1],
                            out_buf[:, i * 128:(i + 1) * 128],
                            ALU.mult, ALU.add), inc="dve", chain=False)
                if k == 3:
                    evs[i0] = p.ev("dve")
            return go
        batches = [tap(0, 8, k, True) for k in range(4)] +                   [tap(8, 18, k, False) for k in range(4)]
        def finish():
            out = []
            for i0, i1 in ((0, 8), (8, 18)):
                p.wait("scalar", evs[i0])
                p.do("scalar", lambda e, i0=i0, i1=i1: e.activation(
                    out_buf[:, i0 * 128:i1 * 128], out_buf[:, i0 * 128:i1 * 128],
                    AF.Silu), inc="act")
                out.append(p.ev("act"))
            return out
        return batches, finish

    def back_ssd(p, dd, seq, c, par, w, st, fev, first):
        sb = sml[par]
        tri = sbm["triu"] if dd == "f" else sbm["tril"]
        convT = w["convT0"] if par == 0 else w["convT1"]
        sq = seq % 2
        Sb = w["S0"] if sq == 0 else w["S1"]
        stilb = w["stil0"] if sq == 0 else w["stil1"]
        szb = fev["szbuf"]
        psyg = st["psyg"]
        p.wait("sync", fev["flat"])
        p.wait("sync", st.get("mask_pe"))
        flview = sp_flat.ap()[seq, c].rearrange("(a b) t -> a (b t)", a=3)
        p.do("sync", lambda e: e.dma_start(sbm["rhs_mask"][64:66, :], flview[0:2, :]), inc="dld", n=16)
        p.do("sync", lambda e: e.dma_start(sbm["expR"][:], flview[2:3, :]), inc="dld", n=16)
        fl_ev = p.ev("dld")
        for q in range(2):
            pw = [fev["conv"][q], psyg[q * 2], psyg[q * 2 + 1]]
            dw = [st.get("states_pe"), st.get("xdec_dve")] if q == 0 else []
            _, xe = pe_tp(p, [convT[:, (q * 8 + i) * 128:(q * 8 + i + 1) * 128]
                              for i in range(8)],
                          ps_y[:, q * 1024:(q + 1) * 1024].bitcast(BF16),
                          w["xstok"][:, q * 1024:(q + 1) * 1024],
                          pre_waits_pe=pw, pre_waits_dve=dw)
            psyg[q * 2] = xe
            psyg[q * 2 + 1] = xe
        _, be = pe_tp(p, [convT[:, 16 * 128:17 * 128]],
                      ps_x[:].bitcast(BF16)[:, 0:128], w["btok"][:],
                      pre_waits_pe=(st.get("gt_dve"), fev["conv"][1]))
        tr_ev = p.ev("dve")
        p.wait("tensor", fev["conv"][1])
        p.wait("tensor", tr_ev)
        p.do("tensor", lambda e: e.matmul(ps_x[:, 0:128], convT[:, 16 * 128:17 * 128],
                                          convT[:, 17 * 128:18 * 128], start=True, stop=True), inc="pe")
        g_ev = p.ev("pe")
        p.wait("vector", g_ev)
        p.do("vector", lambda e: e.tensor_mul(w["gt"][:], ps_x[:, 0:128], tri[:]), inc="dve")
        st["gt_dve"] = p.ev("dve")
        gt_ev = st["gt_dve"]
        first_exp = True
        cg_h = [None, None]
        for half in range(2):
            for jj in range(4):
                j = half * 4 + jj
                if jj == 0:
                    p.wait("tensor", fl_ev)
                    p.wait("tensor", fev["vrows"])
                p.wait("tensor", psyg[jj])
                p.do("tensor", lambda e, j=j, jj=jj, par=par: e.matmul(
                    ps_y[:, jj * 512:(jj + 1) * 512], sml[par]["lhsT_mask"][0:66, :],
                    sbm["rhs_mask"][0:66, j * 512:(j + 1) * 512],
                    start=True, stop=True), inc="pe")
                m_ev = p.ev("pe")
                p.wait("scalar", m_ev)
                if first_exp:
                    p.wait("scalar", st.get("ydyo_pe"))
                    p.wait("scalar", st.get("gm_dve"))
                    first_exp = False
                p.do("scalar", lambda e, j=j, jj=jj: e.activation(
                    w["gmm"][:, j * 512:(j + 1) * 512], ps_y[:, jj * 512:(jj + 1) * 512],
                    AF.Exp), inc="act", chain=(j == 0))
                psyg[jj] = p.ev("act")
            me_h = p.ev("act")
            p.wait("vector", me_h)
            p.do("vector", lambda e, half=half: e.scalar_tensor_tensor(
                w["gmm"][:, half * 2048:(half + 1) * 2048].rearrange("p (a b) -> p a b", a=16),
                w["gmm"][:, half * 2048:(half + 1) * 2048].rearrange("p (a b) -> p a b", a=16),
                CAP, w["gt"][:, None, :].broadcast_to([128, 16, 128]),
                ALU.min, ALU.mult), inc="dve")
            cg_h[half] = p.ev("dve")
        gv_h = [None, None]
        for half in range(2):
            for jj in range(4):
                j = half * 4 + jj
                p.wait("tensor", psyg[jj])
                p.do("tensor", lambda e, j=j, jj=jj: e.matmul(
                    ps_y[:, jj * 512:(jj + 1) * 512], sbm["onesbf"][0:1, :],
                    sbm["expR"][0:1, j * 512:(j + 1) * 512], start=True, stop=True), inc="pe")
                v_ev = p.ev("pe")
                p.wait("scalar", v_ev)
                p.do("scalar", lambda e, j=j, jj=jj: e.activation(
                    w["gmv"][:, j * 512:(j + 1) * 512], ps_y[:, jj * 512:(jj + 1) * 512],
                    AF.Identity), inc="act", chain=False)
                psyg[jj] = p.ev("act")
            ve_h = p.ev("act")
            p.wait("vector", ve_h)
            p.do("vector", lambda e, half=half: e.tensor_mul(
                w["gmv"][:, half * 2048:(half + 1) * 2048].rearrange("p (a b) -> p a b", a=16),
                w["gmv"][:, half * 2048:(half + 1) * 2048].rearrange("p (a b) -> p a b", a=16),
                convT[:, None, 17 * 128:18 * 128].broadcast_to([128, 16, 128])), inc="dve")
            gv_h[half] = p.ev("dve")
        st["gm_dve"] = p.ev("dve")
        gm_ev = st["gm_dve"]
        p.wait("vector", fev["vrd"])
        p.wait("vector", tr_ev)
        p.wait("vector", st.get("states_pe"))
        p.do("vector", lambda e: e.tensor_mul(
            w["xdec"][:].rearrange("p (a b) -> p a b", a=NH),
            w["xstok"][:].rearrange("p (a b) -> p a b", a=NH),
            bview(sb["dtds"][:], HD)), inc="dve")
        st["xdec_dve"] = p.ev("dve")
        xd_ev = st["xdec_dve"]
        p.do("vector", lambda e: e.tensor_mul(
            w["dx"][:].rearrange("p (a b) -> p a b", a=NH),
            w["xstok"][:].rearrange("p (a b) -> p a b", a=NH),
            bview(sbm[f"Dbc_{dd}"][:], HD)), inc="dve", chain=False)
        dx_ev = p.ev("dve")
        p.wait("tensor", tr_ev)
        scq_prev = st.get(("scan_q", sq)) or [None] * 4
        yq_ev = {}
        for q in range(4):
            p.wait("tensor", psyg[q])
            p.wait("tensor", cg_h[q // 2])
            p.wait("tensor", gv_h[q // 2])
            p.wait("tensor", scq_prev[q])
            for hh in range(8):
                h = q * 8 + hh
                p.do("tensor", lambda e, h=h, q=q, hh=hh: e.matmul(
                    ps_y[:, h * 64:(h + 1) * 64], w["gmm"][:, h * 128:(h + 1) * 128],
                    w["xstok"][:, h * 64:(h + 1) * 64], start=True, stop=False,
                    skip_group_check=True))
                p.do("tensor", lambda e, h=h, q=q, hh=hh, stilb=stilb: e.matmul(
                    ps_y[:, h * 64:(h + 1) * 64], w["gmv"][:, h * 128:(h + 1) * 128],
                    stilb[:, h * 64:(h + 1) * 64], start=False, stop=True,
                    skip_group_check=True), inc="pe")
            q_pe = p.ev("pe")
            p.wait("scalar", q_pe)
            if q == 0:
                p.wait("scalar", st.get("y0_free"))
                p.wait("scalar", st.get("yread_dve"))
            p.do("scalar", lambda e, q=q: e.activation(
                w["y0"][:, q * 512:(q + 1) * 512], ps_y[:, q * 512:(q + 1) * 512],
                AF.Identity), inc="act", chain=(q == 0))
            psyg[q] = p.ev("act")
            yq_ev[q] = psyg[q]
        st["ydyo_pe"] = p.ev("pe")
        ye_ev = p.ev("act")
        p.wait("tensor", xd_ev)
        stq = []
        for q in range(4):
            p.wait("tensor", psyg[q])
            p.do("tensor", lambda e, q=q: e.matmul(
                ps_y[:, q * 512:(q + 1) * 512], w["btok"][:],
                w["xdec"][:, q * 512:(q + 1) * 512], start=True, stop=True), inc="pe")
            stq.append(p.ev("pe"))
        st["states_pe"] = p.ev("pe")
        st_ev = st["states_pe"]
        p.wait("gpsimd", fev["exp"])
        saq, scq = [None] * 4, [None] * 4
        for q in range(4):
            p.wait("gpsimd", scq_prev[q])
            p.do("gpsimd", lambda e, q=q, Sb=Sb: e.tensor_mul(
                Sb[:, q * 512:(q + 1) * 512].rearrange("p (a b) -> p a b", a=8),
                Sb[:, q * 512:(q + 1) * 512].rearrange("p (a b) -> p a b", a=8),
                bview(sb["exptot"][:, q * 8:(q + 1) * 8], HD)), inc="pool")
            smq = p.ev("pool")
            p.wait("vector", smq)
            p.wait("vector", stq[q])
            p.do("vector", lambda e, q=q, Sb=Sb: e.tensor_add(
                Sb[:, q * 512:(q + 1) * 512], Sb[:, q * 512:(q + 1) * 512],
                ps_y[:, q * 512:(q + 1) * 512]), inc="dve")
            saq[q] = p.ev("dve")
            p.wait("scalar", saq[q])
            p.do("scalar", lambda e, q=q, Sb=Sb, stilb=stilb: e.activation(
                stilb[:, q * 512:(q + 1) * 512],
                Sb[:, q * 512:(q + 1) * 512], AF.Identity), inc="act", chain=False)
            scq[q] = p.ev("act")
            psyg[q] = saq[q]
        st[("sadd_q", sq)], st[("scan_q", sq)] = saq, scq
        p.wait("vector", ye_ev)
        p.wait("vector", dx_ev)
        for half in range(2):
            p.do("vector", lambda e, half=half: e.tensor_add(
                w["y0"][:, half * 1024:(half + 1) * 1024],
                w["y0"][:, half * 1024:(half + 1) * 1024],
                w["dx"][:, half * 1024:(half + 1) * 1024]), inc="dve",
                chain=(half == 0))
        p.wait("vector", fev["sz"])
        for half in range(2):
            p.do("vector", lambda e, half=half, szb=szb: e.tensor_mul(
                w["dx"][:, half * 1024:(half + 1) * 1024],
                w["y0"][:, half * 1024:(half + 1) * 1024],
                szb[:, half * 1024:(half + 1) * 1024]), inc="dve")
        y2_ev = p.ev("dve")
        p.wait("scalar", y2_ev)
        p.do("scalar", lambda e: e.activation(
            sbm["scrA"][:, 0:2048], w["dx"][:],
            AF.Square, accum_out=sbm["col1"][:, 10:11]), inc="act")
        sq_ev = p.ev("act")
        p.wait("vector", sq_ev)
        p.do("vector", lambda e: e.tensor_scalar(sbm["col1"][:, 11:12], sbm["col1"][:, 10:11],
                                                 1.0 / DI, None, ALU.mult), inc="dve")
        ms_ev = p.ev("dve")
        p.wait("scalar", ms_ev)
        p.do("scalar", lambda e: e.activation(sbm["col1"][:, 12:13], sbm["col1"][:, 11:12],
                                              AF.Ln, bias=EPS), inc="act")
        p.do("scalar", lambda e: e.activation(sbm["col1"][:, 13:14], sbm["col1"][:, 12:13],
                                              AF.Exp, scale=-0.5), inc="act")
        rs_ev = p.ev("act")
        p.wait("vector", rs_ev)
        p.do("vector", lambda e: e.scalar_tensor_tensor(
            w["y0"][:], w["dx"][:], sbm["col1"][:, 13:14], sbm[f"nwbc_{dd}"][:],
            ALU.mult, ALU.mult), inc="dve")
        y_ev = p.ev("dve")
        st["yread_dve"] = y_ev
        st["mask_pe"] = st["states_pe"]
        return y_ev

    def front_a(p, seq, c, par, st):
        sb = sml[par]
        fev = {}
        xbuf = wa["xA0"] if par == 0 else wa["xA1"]
        bend = st["bend"].get(par)
        for e_ in ("vector", "scalar", "sync", "tensor", "gpsimd"):
            p.wait_snap(e_, bend)
        p.do("sync", lambda e, seq=seq, c=c, xbuf=xbuf: e.dma_start(
            xbuf[:], x_in.ap()[seq, c * CK:(c + 1) * CK, :]), inc="dld", n=16)
        xld = p.ev("dld")
        p.wait("vector", xld)
        p.wait("scalar", xld)
        p.do("scalar", lambda e, xbuf=xbuf: e.activation(
            sbm["scrA"][:, 0:1024], xbuf[:], AF.Square,
            accum_out=sbm["col1"][:, 0:1]), inc="act")
        sq_ev = p.ev("act")
        p.do("vector", lambda e, xbuf=xbuf: e.reduce_sum(sbm["col1"][:, 1:2], xbuf[:], AX.X), inc="dve")
        p.wait("vector", sq_ev)
        p.do("vector", lambda e: e.tensor_scalar(sbm["col1"][:, 2:3], sbm["col1"][:, 0:1],
                                                 1.0 / D, None, ALU.mult), inc="dve")
        p.do("vector", lambda e: e.tensor_scalar(sbm["col1"][:, 3:4], sbm["col1"][:, 1:2],
                                                 1.0 / D, None, ALU.mult), inc="dve")
        p.do("vector", lambda e: e.tensor_mul(sbm["col1"][:, 4:5], sbm["col1"][:, 3:4],
                                              sbm["col1"][:, 3:4]), inc="dve")
        p.do("vector", lambda e: e.tensor_sub(sbm["col1"][:, 5:6], sbm["col1"][:, 2:3],
                                              sbm["col1"][:, 4:5]), inc="dve")
        var_ev = p.ev("dve")
        p.wait("scalar", var_ev)
        p.do("scalar", lambda e: e.activation(sbm["col1"][:, 6:7], sbm["col1"][:, 5:6],
                                              AF.Ln, bias=EPS), inc="act")
        p.do("scalar", lambda e: e.activation(sbm["col1"][:, 7:8], sbm["col1"][:, 6:7],
                                              AF.Exp, scale=-0.5), inc="act")
        rstd_ev = p.ev("act")
        p.wait("vector", rstd_ev)
        p.wait("vector", st.get("xn_free"))
        p.wait("vector", st.get("xn_free_sp"))
        p.do("vector", lambda e, xbuf=xbuf: e.scalar_tensor_tensor(
            wa["xn"][:], xbuf[:], sbm["col1"][:, 3:4],
            sbm["col1"][:, 7:8].broadcast_to([128, D]),
            ALU.subtract, ALU.mult), inc="dve")
        xn_ev = p.ev("dve")
        p.wait("scalar", xn_ev)
        p.do("scalar", lambda e, seq=seq, c=c: e.dma_start(
            sp_xn.ap()[seq, c], wa["xn"][:]), inc="dsp", n=16)
        xnsp_ev = p.ev("dsp")
        st["xn_free_sp"] = xnsp_ev
        psv = ps_s[:].bitcast(BF16)
        _, xnT_ev = pe_tp(p, [wa["xn"][:, k * 128:(k + 1) * 128] for k in range(8)],
                          psv, wa["xnT"][:],
                          pre_waits_pe=(xn_ev, st.get("ps_s_g"), st.get("inproj_pe")),
                          pre_waits_dve=(st.get("inproj_pe"),))
        st["ps_s_g"] = xnT_ev
        st["xn_free"] = p.ev("pe")
        szbuf = wa["siluz0"] if par == 0 else wa["siluz1"]
        p.wait("tensor", xnT_ev)
        jev = st.get("jev", {})
        halo_evs = []
        for j in range(9):
            psl = ps_a[:, (j % 2) * 512:(j % 2) * 512 + 512]
            ncols = 512 if j < 8 else 288
            p.wait("tensor", jev.get(j % 2))
            for k in range(8):
                p.do("tensor", lambda e, j=j, k=k, psl=psl, ncols=ncols: e.matmul(
                    psl[:, 0:ncols], wa["xnT"][:, k * 128:(k + 1) * 128],
                    wa["w_inT"][:, k * DIP + j * 512:k * DIP + j * 512 + ncols],
                    start=(k == 0), stop=(k == 7)), inc=("pe" if k == 7 else None))
            mm_ev = p.ev("pe")
            if j < 4:
                p.wait("scalar", mm_ev)
                if j == 0:
                    p.wait("scalar", st.get("gate_dve"))
                p.do("scalar", lambda e, j=j, psl=psl, szbuf=szbuf: e.activation(
                    szbuf[:, j * 512:(j + 1) * 512], psl[:], AF.Identity), inc="act",
                    chain=(j == 0))
                jev[j % 2] = p.ev("act")
            else:
                p.wait("scalar", mm_ev)
                nxb = 512 if j < 8 else 256
                stg = wa["stage"][:, (j % 4) * 512:(j % 4) * 512 + 512]
                p.wait("scalar", st.get(("stg", j % 4)))
                p.do("scalar", lambda e, psl=psl, nxb=nxb, stg=stg: e.activation(
                    stg[:, 0:nxb], psl[:, 0:nxb], AF.Identity), inc="act", chain=False)
                if j == 8:
                    p.do("scalar", lambda e, psl=psl, sb=sb: e.activation(
                        sb["dtraw"][:], psl[:, 256:288], AF.Identity), inc="act", chain=False)
                st_ev = p.ev("act")
                jev[j % 2] = st_ev
                pw = [st_ev, st.get("ps_s_g")]
                dw = []
                if j == 4:
                    dw = [st.get("halo_free"), st.get("halo_free_cv"),
                          st.get("halo_free_tl")]
                nt = nxb // 128
                i0 = (j - 4) * 4
                hdst = wa["halo"][:].rearrange("p (i s) -> p i s", i=18)[
                    :, i0:i0 + nt, HDAT:HDAT + 128]
                _, hev = pe_tp(p, [stg[:, bb * 128:(bb + 1) * 128] for bb in range(nt)],
                               ps_s[:].bitcast(BF16), hdst,
                               pre_waits_pe=pw, pre_waits_dve=dw, eng="scalar")
                st["ps_s_g"] = hev
                st[("stg", j % 4)] = p.ev("pe")
                halo_evs.append(hev)
        st["jev"] = jev
        st["inproj_pe"] = p.ev("pe")
        xbcT_ev = halo_evs[-1]
        convT = wa["convT0"] if par == 0 else wa["convT1"]
        p.wait("vector", xbcT_ev)
        p.wait("vector", p.ev("act"))
        tl = wa["tail0"] if seq % 2 == 0 else wa["tail1"]
        if c == 0:
            p.do("vector", lambda e, tl=tl: e.memset(tl[:], 0.0), inc="dve")
            p.do("vector", lambda e: e.memset(
                wa["halo"][:].rearrange("p (i s) -> p i s", i=18)[:, :, 0:16], 0.0), inc="dve")
            p.do("vector", lambda e: e.memset(
                wa["halo"][:].rearrange("p (i s) -> p i s", i=18)[:, :, 144:160], 0.0), inc="dve")
        p.do("vector", lambda e, tl=tl: e.tensor_copy(
            wa["halo"][:].rearrange("p (i s) -> p i s", i=18)[:, :, 13:16],
            tl[:].rearrange("p (i s) -> p i s", i=18)[:, :, 1:4]), inc="dve")
        batches, cfin = conv_batches(p, "f", wa["halo"], convT)
        vrd = dt_chain(p, "f", par, st, fill=batches)
        fev["vrd"] = vrd
        fev["vrows"] = vrd
        fev["exp"] = p.ev("act")
        fev["conv"] = cfin()
        p.do("scalar", lambda e, szbuf=szbuf: e.activation(
            szbuf[:], szbuf[:], AF.Silu), inc="act")
        fev["sz"] = sz_ev = p.ev("act")
        p.wait("scalar", sz_ev)
        p.do("scalar", lambda e, seq=seq, c=c, szbuf=szbuf: e.dma_start(
            sp_sz.ap()[seq, c], szbuf[:]), inc="dsp", n=16)
        p.do("vector", lambda e, tl=tl: e.tensor_copy(
            tl[:].rearrange("p (i s) -> p i s", i=18)[:, :, 1:4],
            wa["halo"][:].rearrange("p (i s) -> p i s", i=18)[:, :, 141:144]), inc="dve")
        tail_ev = p.ev("dve")
        p.wait("scalar", vrd)
        p.do("scalar", lambda e, seq=seq, c=c, sb=sb: e.dma_start(
            sp_flat.ap()[seq, c], sb["flatpk"][0:96, :]), inc="dsp", n=16)
        fev["flat"] = p.ev("dsp")
        p.wait("scalar", xbcT_ev)
        p.do("scalar", lambda e, seq=seq, c=c: e.dma_start(
            sp_xbc.ap()[seq, c], wa["halo"][:]), inc="dsp", n=16)
        hsp_ev = p.ev("dsp")
        p.do("scalar", lambda e, seq=seq, c=c, sb=sb: e.dma_start(
            sp_dt.ap()[seq, c], sb["dtraw"][:]), inc="dsp", n=16)
        st["halo_free"] = hsp_ev
        st["halo_free_cv"] = fev["conv"][1]
        st["halo_free_tl"] = tail_ev
        fev["szbuf"] = szbuf
        return fev

    def back_a(p, seq, c, par, st, fev):
        y_ev = back_ssd(p, "f", seq, c, par, wa, st, fev, c == 0)
        p.wait("scalar", y_ev)
        p.do("scalar", lambda e, seq=seq, c=c: e.dma_start(
            sp_yf.ap()[seq, c], wa["y0"][:]), inc="dsp", n=16)
        st["y0_free"] = p.ev("dsp")
        st["bend"][par] = p.snap()

    def front_b1(p, seq, c, par, st):
        sb = sml[par]
        fev = {}
        bend = st["bend"].get(par)
        for e_ in ENGINES:
            p.wait_snap(e_, bend)
        halo = wb["haloB0"] if par == 0 else wb["haloB1"]
        szb = wb["szB0"] if par == 0 else wb["szB1"]
        yfb = wb["yfB0"] if par == 0 else wb["yfB1"]
        p.do("sync", lambda e, seq=seq, c=c, halo=halo: e.dma_start(
            halo[:], sp_xbc.ap()[seq, c]), inc="dld", n=16)
        p.do("sync", lambda e, seq=seq, c=c, sb=sb: e.dma_start(
            sb["dtraw"][:], sp_dt.ap()[seq, c]), inc="dld", n=16)
        p.do("sync", lambda e, seq=seq, c=c, szb=szb: e.dma_start(
            szb[:], sp_sz.ap()[seq, c]), inc="dld", n=16)
        p.do("sync", lambda e, seq=seq, c=c, yfb=yfb: e.dma_start(
            yfb[:], sp_yf.ap()[seq, c]), inc="dld", n=16)
        ld_ev = p.ev("dld")
        fev["ld"] = ld_ev
        p.wait("vector", ld_ev)
        tlr = wb["tailR0"] if seq % 2 == 0 else wb["tailR1"]
        if c == NCK - 1:
            p.do("vector", lambda e, tlr=tlr: e.memset(tlr[:], 0.0), inc="dve")
        p.do("vector", lambda e, halo=halo, tlr=tlr: e.tensor_copy(
            halo[:].rearrange("p (i s) -> p i s", i=18)[:, :, 144:147],
            tlr[:].rearrange("p (i s) -> p i s", i=18)[:, :, 1:4]), inc="dve")
        p.do("vector", lambda e, halo=halo, tlr=tlr: e.tensor_copy(
            tlr[:].rearrange("p (i s) -> p i s", i=18)[:, :, 1:4],
            halo[:].rearrange("p (i s) -> p i s", i=18)[:, :, 16:19]), inc="dve")
        convT = wb["convT0"] if par == 0 else wb["convT1"]
        batches, cfin = conv_batches(p, "r", halo, convT)
        vrd = dt_chain(p, "r", par, st, fill=batches)
        fev["vrd"] = vrd
        fev["vrows"] = vrd
        fev["exp"] = p.ev("act")
        fev["conv"] = cfin()
        p.wait("scalar", vrd)
        p.do("scalar", lambda e, seq=seq, c=c, sb=sb: e.dma_start(
            sp_flat.ap()[seq, c], sb["flatpk"][0:96, :]), inc="dsp", n=16)
        fev["flat"] = p.ev("dsp")
        fev["sz"] = ld_ev
        fev["szbuf"] = szb
        fev["yfbuf"] = yfb
        return fev

    def back_b1(p, seq, c, par, st, fev):
        y_ev = back_ssd(p, "r", seq, c, par, wb, st, fev, c == NCK - 1)
        p.wait("vector", fev["ld"])
        p.do("vector", lambda e, yfb=fev["yfbuf"]: e.tensor_add(
            wb["y0"][:], wb["y0"][:], yfb[:]), inc="dve")
        yc_ev = p.ev("dve")
        st["yread_dve"] = yc_ev
        yes = []
        for q in range(2):
            pw = [yc_ev, st.get("sevac_dve")] if q == 0 else [yc_ev]
            _, ye_ = pe_tp(p, [wb["y0"][:, (q * 8 + i) * 128:(q * 8 + i + 1) * 128]
                               for i in range(8)],
                           ps_a[:, q * 512:(q + 1) * 512].bitcast(BF16),
                           wb["ycombT"][:, q * 1024:(q + 1) * 1024],
                           pre_waits_pe=pw, eng="scalar")
            yes.append(ye_)
        st["y0_free"] = p.ev("pe")
        p.wait("tensor", yes[0])
        p.wait("tensor", yes[1])
        for j in range(2):
            for k in range(16):
                p.do("tensor", lambda e, j=j, k=k: e.matmul(
                    ps_a[:, j * 512:(j + 1) * 512], wb["ycombT"][:, k * 128:(k + 1) * 128],
                    wb["w_outT"][:, k * D + j * 512:k * D + j * 512 + 512],
                    start=(k == 0), stop=(k == 15)), inc=("pe" if k == 15 else None))
        st["op_pe"] = p.ev("pe")
        p.wait("vector", st["op_pe"])
        p.wait("vector", st.get("s_spill"))
        p.do("vector", lambda e: e.tensor_scalar(wb["sevac"][:], ps_a[:], 0.5, None,
                                                 ALU.mult), inc="dve")
        st["sevac_dve"] = p.ev("dve")
        p.wait("scalar", st["sevac_dve"])
        p.do("scalar", lambda e, seq=seq, c=c: e.dma_start(
            sp_s.ap()[seq, c], wb["sevac"][:]), inc="dsp", n=16)
        st["s_spill"] = p.ev("dsp")
        st["bend"][par] = p.snap()

    def front_b2(p, seq, c, par, st):
        fev = {}
        bend = st["bend"].get(par)
        for e_ in ENGINES:
            p.wait_snap(e_, bend)
        xb = wc["xC0"] if par == 0 else wc["xC1"]
        sb_ = wc["sC0"] if par == 0 else wc["sC1"]
        xnT = wc["xnTC0"] if par == 0 else wc["xnTC1"]
        p.wait("sync", st.get("xnC_free"))
        p.do("sync", lambda e, seq=seq, c=c: e.dma_start(
            wc["xnC"][:], sp_xn.ap()[seq, c]), inc="dld", n=16)
        ld1 = p.ev("dld")
        p.do("sync", lambda e, seq=seq, c=c, xb=xb: e.dma_start(
            xb[:], x_in.ap()[seq, c * CK:(c + 1) * CK, :]), inc="dld", n=16)
        p.do("sync", lambda e, seq=seq, c=c, sb_=sb_: e.dma_start(
            sb_[:], sp_s.ap()[seq, c]), inc="dld", n=16)
        ld_ev = p.ev("dld")
        psv = ps_s[:].bitcast(BF16)
        _, xnT_ev2 = pe_tp(p, [wc["xnC"][:, k * 128:(k + 1) * 128] for k in range(8)],
                           psv, xnT[:],
                           pre_waits_pe=(ld1, st.get("ps_s_g2")))
        st["ps_s_g2"] = xnT_ev2
        st["xnC_free"] = p.ev("pe")
        fev["xnT_ev"] = xnT_ev2
        fev["ld"] = ld_ev
        fev["bufs"] = (xb, sb_, xnT)
        return fev

    def back_b2(p, seq, c, par, st, fev):
        xb, sb_, xnT = fev["bufs"]
        p.wait("tensor", fev["xnT_ev"])
        jev = st.get("jev2", {})
        for j in range(8):
            s3 = j % 3
            psl = (ps_a[:, 0:512], ps_a[:, 512:1024], ps_x[:])[s3]
            p.wait("tensor", jev.get(s3))
            for k in range(8):
                p.do("tensor", lambda e, j=j, k=k, psl=psl, xnT=xnT: e.matmul(
                    psl[:], xnT[:, k * 128:(k + 1) * 128],
                    wc["w1T"][:, k * 4 * D + j * 512:k * 4 * D + j * 512 + 512],
                    start=(k == 0), stop=False))
            p.do("tensor", lambda e, j=j, psl=psl: e.matmul(
                psl[:], sbm["onesbf"][0:1, :], wc["b1row"][0:1, j * 512:(j + 1) * 512],
                start=False, stop=True), inc="pe")
            mm_ev = p.ev("pe")
            p.wait("scalar", mm_ev)
            if j == 0:
                p.wait("scalar", st.get("hT_dtr"))
            p.do("scalar", lambda e, j=j, psl=psl: e.activation(
                wc["hbuf"][:, j * 512:(j + 1) * 512], psl[:], AF.Gelu), inc="act")
            jev[s3] = p.ev("act")
        st["jev2"] = jev
        h_ev = p.ev("act")
        hq = st.get("hq", {})
        for g in range(4):
            qq = 2 + (g % 2)
            pw = [h_ev, hq.get(qq)]
            _, he_ = pe_tp(p, [wc["hbuf"][:, (g * 8 + i) * 128:(g * 8 + i + 1) * 128]
                               for i in range(8)],
                           ps_y[:, qq * 512:(qq + 1) * 512].bitcast(BF16),
                           wc["hT"][:, g * 1024:(g + 1) * 1024],
                           pre_waits_pe=pw)
            hq[qq] = he_
        st["hq"] = hq
        hT_ev = p.ev("dve")
        st["hT_dtr"] = p.ev("pe")
        p.wait("tensor", hT_ev)
        p.wait("tensor", st.get("g_dve"))
        for j in range(2):
            for k in range(32):
                p.do("tensor", lambda e, j=j, k=k: e.matmul(
                    ps_y[:, j * 512:(j + 1) * 512], wc["hT"][:, k * 128:(k + 1) * 128],
                    wc["w2T"][:, k * D + j * 512:k * D + j * 512 + 512],
                    start=(k == 0), stop=False))
            p.do("tensor", lambda e, j=j: e.matmul(
                ps_y[:, j * 512:(j + 1) * 512], sbm["onesbf"][0:1, :],
                wc["b2row"][0:1, j * 512:(j + 1) * 512], start=False, stop=True), inc="pe")
        st["ffn2_pe"] = p.ev("pe")
        p.wait("scalar", st["ffn2_pe"])
        p.do("scalar", lambda e: e.activation(wc["thC"][:], ps_y[:, 0:1024], AF.Tanh,
                                              scale=0.5), inc="act")
        th_ev = p.ev("act")
        p.wait("vector", th_ev)
        p.wait("vector", fev["ld"])
        p.do("vector", lambda e: e.scalar_tensor_tensor(
            wc["gC"][:], wc["thC"][:], 1.0, ps_y[:, 0:1024], ALU.add, ALU.mult), inc="dve")
        st["g_dve"] = p.ev("dve")
        p.do("vector", lambda e, sb_=sb_: e.tensor_mul(sb_[:], sb_[:], wc["gC"][:]), inc="dve")
        p.wait("vector", st.get("out_spill"))
        p.do("vector", lambda e, xb=xb, sb_=sb_: e.tensor_add(wc["outC"][:], xb[:], sb_[:]), inc="dve")
        o_ev = p.ev("dve")
        p.wait("scalar", o_ev)
        p.do("scalar", lambda e, seq=seq, c=c: e.dma_start(
            out_t.ap()[seq, c * CK:(c + 1) * CK, :], wc["outC"][:]), inc="dsp", n=16)
        st["out_spill"] = p.ev("dsp")
        st["bend"][par] = p.snap()

    def prep(p: P):
        sb = sbm
        for dd in ("f", "r"):
            p.do("sync", lambda e, dd=dd: e.dma_start(
                sb[f"convw_{dd}"][:].rearrange("p (i k) -> p i k", k=4),
                pr[f"convw_{dd}"].ap().rearrange("(i p) k -> p i k", p=128)), inc="dld", n=16)
        p.do("sync", lambda e: e.dma_start(sb["triu"][:], c_triu.ap()), inc="dld", n=16)
        p.do("sync", lambda e: e.dma_start(sb["tril"][:], c_tril.ap()), inc="dld", n=16)
        p.do("sync", lambda e: e.dma_start(sb["identf"][:], c_identf.ap()), inc="dld", n=16)
        p.do("sync", lambda e: e.dma_start(sb["identb"][:], c_identb.ap()), inc="dld", n=16)
        p.do("sync", lambda e: e.dma_start(sb["rhs_mask"][0:64, :], c_E2.ap()), inc="dld", n=16)
        p.do("sync", lambda e: e.dma_start(sb["cbias"][:], c_bias.ap()), inc="dld", n=16)
        for di, dd in enumerate(("f", "r")):
            p.do("sync", lambda e, dd=dd: e.dma_start(
                sb["cbst"][0:18, :], pr[f"convb_{dd}"].ap().rearrange("(i p) -> i p", p=128)),
                inc="dld", n=16)
            cb_ld = p.ev("dld")
            p.wait("tensor", cb_ld)
            p.do("tensor", lambda e, di=di: e.transpose(
                ps_s[:, 384 + di * 32:384 + di * 32 + 32], sb["cbst"][0:32, :],
                sb["identf"][0:32, 0:32]), inc="pe")
            cb_tr = p.ev("pe")
            p.wait("vector", cb_tr)
            p.do("vector", lambda e, dd=dd, di=di: e.tensor_copy(
                sb[f"convb_{dd}"][:], ps_s[:, 384 + di * 32:384 + di * 32 + 18]), inc="dve")
            cb_cp = p.ev("dve")
            p.wait("sync", cb_cp)
        p.do("gpsimd", lambda e: e.dma_start(sb["rowf"][0:1, 0:32], pr["dtb_f"].ap()[None, :]), inc="dgp", n=16)
        p.do("gpsimd", lambda e: e.dma_start(sb["rowf"][0:1, 32:64], pr["dtb_r"].ap()[None, :]), inc="dgp", n=16)
        p.do("gpsimd", lambda e: e.dma_start(sb["rowf"][0:1, 64:96], pr["alog_f"].ap()[None, :]), inc="dgp", n=16)
        p.do("gpsimd", lambda e: e.dma_start(sb["rowf"][0:1, 96:128], pr["alog_r"].ap()[None, :]), inc="dgp", n=16)
        p.do("gpsimd", lambda e: e.dma_start(sb["rowD"][0:1, 0:32], pr["dsk_f"].ap()[None, :]), inc="dgp", n=16)
        p.do("gpsimd", lambda e: e.dma_start(sb["rowD"][0:1, 32:64], pr["dsk_r"].ap()[None, :]), inc="dgp", n=16)
        p.do("gpsimd", lambda e: e.dma_start(wp["wtmp0"][0:1, 0:DI], pr["nrm_f"].ap()[None, :]), inc="dgp", n=16)
        p.do("gpsimd", lambda e: e.dma_start(wp["wtmp1"][0:1, 0:DI], pr["nrm_r"].ap()[None, :]), inc="dgp", n=16)
        ld0 = p.ev("dgp")
        p.do("vector", lambda e: e.memset(sb["onesbf"][:], 1.0), inc="dve")
        p.do("vector", lambda e: e.memset(sb["onescol"][:], 1.0), inc="dve")
        for par in range(2):
            p.do("vector", lambda e, par=par: e.memset(sml[par]["lhsT_mask"][64:66, :], 1.0), inc="dve")
        p.wait("scalar", ld0)
        p.wait("scalar", p.ev("dld"))
        p.do("scalar", lambda e: e.activation(sb["rowf"][0:1, 64:128], sb["rowf"][0:1, 64:128],
                                              AF.Exp), inc="act")
        p.do("scalar", lambda e: e.mul(sb["rowf"][0:1, 64:128], sb["rowf"][0:1, 64:128], -1.0), inc="act")
        a_ev = p.ev("act")
        p.wait("vector", a_ev)
        emit_hilo(p, sb["rowf"][0:1, 0:128], sb["rowt"][0:1, 0:128], sb["rowf"][0:1, 128:256],
                  sb["rowt"][0:1, 128:256])
        hl_ev = p.ev("dve")
        p.wait("tensor", hl_ev)
        p.do("tensor", lambda e: e.matmul(ps_y[:, 0:128], sb["onesbf"][0:1, :],
                                          sb["rowt"][0:1, 0:128], start=True, stop=False))
        p.do("tensor", lambda e: e.matmul(ps_y[:, 0:128], sb["onesbf"][0:1, :],
                                          sb["rowt"][0:1, 128:256], start=False, stop=True), inc="pe")
        p.do("tensor", lambda e: e.matmul(ps_y[:, 128:192], sb["onesbf"][0:1, :],
                                          sb["rowD"][0:1, 0:64], start=True, stop=True), inc="pe")
        pb_ev = p.ev("pe")
        p.wait("vector", pb_ev)
        p.do("vector", lambda e: e.tensor_copy(sb["dtb_f"][:], ps_y[:, 0:32]), inc="dve")
        p.do("vector", lambda e: e.tensor_copy(sb["dtb_r"][:], ps_y[:, 32:64]), inc="dve")
        p.do("vector", lambda e: e.tensor_copy(sb["negA_f"][:], ps_y[:, 64:96]), inc="dve")
        p.do("vector", lambda e: e.tensor_copy(sb["negA_r"][:], ps_y[:, 96:128]), inc="dve")
        p.do("vector", lambda e: e.tensor_copy(sb["Dbc_f"][:], ps_y[:, 128:160]), inc="dve")
        p.do("vector", lambda e: e.tensor_copy(sb["Dbc_r"][:], ps_y[:, 160:192]), inc="dve")
        dcp_ev = p.ev("dve")
        p.wait("tensor", dcp_ev)
        for j in range(8):
            dd = "f" if j < 4 else "r"
            jj = j % 4
            wsrc_t = wp["wtmp0"] if j < 4 else wp["wtmp1"]
            p.do("tensor", lambda e, j=j, jj=jj, wsrc_t=wsrc_t: e.matmul(
                ps_y[:, 512 + (j % 2) * 512: 512 + (j % 2) * 512 + 512],
                sb["onesbf"][0:1, :],
                wsrc_t[0:1, jj * 512:(jj + 1) * 512],
                start=True, stop=True), inc="pe")
            nw_ev = p.ev("pe")
            p.wait("vector", nw_ev)
            p.do("vector", lambda e, dd=dd, jj=jj, j=j: e.tensor_copy(
                sb[f"nwbc_{dd}"][:, jj * 512:(jj + 1) * 512],
                ps_y[:, 512 + (j % 2) * 512:512 + (j % 2) * 512 + 512]), inc="dve")
            p.wait("tensor", p.ev("dve"))
        trx_prev = [Z]
        def tr8(wt, col0, ntile, rows, dst3, lev):
            psb = ps_x[:].bitcast(BF16)
            p.wait("tensor", lev)
            p.wait("tensor", trx_prev[0])
            for k in range(ntile):
                p.do("tensor", lambda e, k=k: e.transpose(
                    psb[:, k * 128:k * 128 + rows],
                    wt[0:rows, col0 + k * 128:col0 + (k + 1) * 128],
                    sbm["identb"][0:rows, 0:rows]),
                    inc=("pe" if k == ntile - 1 else None))
            tev = p.ev("pe")
            p.wait("vector", tev)
            p.do("vector", lambda e: e.tensor_copy(
                dst3, psb[:, 0:ntile * 128].rearrange(
                    "p (k c) -> p k c", k=ntile)[:, :, 0:rows]), inc="dve")
            trx_prev[0] = p.ev("dve")
            return tev, trx_prev[0]
        wfree = {}
        for jt in range(35):
            rows = 128 if jt < 34 else 32
            par = jt % 2
            wt = wp["wtmp0"] if par == 0 else wp["wtmp1"]
            p.wait("gpsimd", wfree.get(par, p.ev("pe") if jt < 2 else None))
            p.do("gpsimd", lambda e, jt=jt, wt=wt, rows=rows: e.dma_start(
                wt[0:rows, 0:1024], w_in.ap()[jt * 128:jt * 128 + rows, :]), inc="dgp", n=16)
            lev = p.ev("dgp")
            dst3 = wa["w_inT"][:].rearrange("p (k c) -> p k c", c=DIP)[
                :, :, jt * 128:jt * 128 + rows]
            tev, _ = tr8(wt, 0, 8, rows, dst3, lev)
            wfree[par] = tev
        for rt in range(8):
            par = rt % 2
            wt = wp["wtmp0"] if par == 0 else wp["wtmp1"]
            p.wait("gpsimd", wfree.get(par))
            p.wait("gpsimd", p.ev("dsp"))
            p.do("gpsimd", lambda e, rt=rt, wt=wt: e.dma_start(
                wt[:, 0:2048], w_out.ap()[rt * 128:(rt + 1) * 128, :]), inc="dgp", n=16)
            lev = p.ev("dgp")
            evs = []
            for hb in range(2):
                d3 = wp["trst"][:, hb * 1024:(hb + 1) * 1024].rearrange(
                    "p (k c) -> p k c", k=8)
                tev, dev = tr8(wt, hb * 1024, 8, 128, d3, lev)
                evs.append(dev)
            wfree[par] = tev
            p.wait("scalar", evs[1])
            for ct in range(16):
                p.do("scalar", lambda e, rt=rt, ct=ct: e.dma_start(
                    sp_wot.ap()[ct, :, rt * 128:(rt + 1) * 128],
                    wp["trst"][:, ct * 128:(ct + 1) * 128]), inc="dsp", n=16)
        def ternary(wsrc, nrtile, ncol, spill):
            nhalf = (ncol + 2047) // 2048
            for eng in ("gpsimd", "scalar", "sync", "vector"):
                for s_ in ("dtr", "dsp", "dve", "act"):
                    p.wait(eng, p.ev(s_))
            p.do("vector", lambda e: e.memset(wp["acc"][:], 0.0), inc="dve")
            absev = {}
            for it in range(nrtile):
                for hf in range(nhalf):
                    cw = min(2048, ncol - hf * 2048)
                    par = (it * nhalf + hf) % 2
                    wf = wp["wf0"] if par == 0 else wp["wf1"]
                    if par in absev:
                        p.wait("gpsimd", absev[par])
                    p.do("gpsimd", lambda e, it=it, hf=hf, cw=cw, wf=wf: e.dma_start(
                        wf[:, 0:cw], wsrc.ap()[it * 128:(it + 1) * 128,
                                               hf * 2048:hf * 2048 + cw]), inc="dgp", n=16)
                    lev = p.ev("dgp")
                    p.wait("scalar", lev)
                    p.wait("scalar", p.ev("dve"))
                    p.do("scalar", lambda e, wf=wf, cw=cw: e.activation(
                        wp["wi0"][:].bitcast(F32)[:, 0:cw],
                        wf[:, 0:cw], AF.Abs, accum_out=wp["acc"][:, 1:2]), inc="act")
                    absev[par] = p.ev("act")
                    p.wait("vector", absev[par])
                    p.do("vector", lambda e: e.tensor_add(wp["acc"][:, 0:1], wp["acc"][:, 0:1],
                                                          wp["acc"][:, 1:2]), inc="dve")
            emit_hilo(p, wp["acc"][:, 0:1], sb["colbf"][:, 0:1],
                      wp["acc"][:, 3:4], sb["colbf"][:, 1:2])
            hl2 = p.ev("dve")
            p.wait("tensor", hl2)
            p.do("tensor", lambda e: e.matmul(ps_s[0:1, 0:1], sb["colbf"][:, 0:1],
                                              sb["onescol"][:, 0:1], start=True, stop=False))
            p.do("tensor", lambda e: e.matmul(ps_s[0:1, 0:1], sb["colbf"][:, 1:2],
                                              sb["onescol"][:, 0:1], start=False, stop=True), inc="pe")
            ps_ev = p.ev("pe")
            p.wait("vector", ps_ev)
            p.do("vector", lambda e: e.tensor_scalar(wp["acc"][0:1, 4:5], ps_s[0:1, 0:1],
                                                     1.0 / (nrtile * 128 * ncol), 1e-8,
                                                     ALU.mult, ALU.max), inc="dve")
            p.do("vector", lambda e: e.reciprocal(wp["acc"][0:1, 5:6], wp["acc"][0:1, 4:5]), inc="dve")
            emit_hilo(p, wp["acc"][0:1, 5:6], sb["rowbf"][0:1, 0:1], wp["acc"][0:1, 6:7],
                      sb["rowbf"][0:1, 1:2])
            iv_ev = p.ev("dve")
            p.wait("tensor", iv_ev)
            p.do("tensor", lambda e: e.matmul(ps_s[:, 4:5], sb["onesbf"][0:1, :],
                                              sb["rowbf"][0:1, 0:1], start=True, stop=False))
            p.do("tensor", lambda e: e.matmul(ps_s[:, 4:5], sb["onesbf"][0:1, :],
                                              sb["rowbf"][0:1, 1:2], start=False, stop=True), inc="pe")
            ib_ev = p.ev("pe")
            p.wait("vector", ib_ev)
            p.do("vector", lambda e: e.tensor_copy(wp["acc"][:, 7:8], ps_s[:, 4:5]), inc="dve")
            trev = {}
            for it in range(nrtile):
                for hf in range(nhalf):
                    cw = min(2048, ncol - hf * 2048)
                    par = (it * nhalf + hf) % 2
                    wf = wp["wf0"] if par == 0 else wp["wf1"]
                    wt = wp["wtmp0"] if par == 0 else wp["wtmp1"]
                    if par in trev:
                        p.wait("gpsimd", trev[par])
                    p.do("gpsimd", lambda e, it=it, hf=hf, cw=cw, wf=wf: e.dma_start(
                        wf[:, 0:cw], wsrc.ap()[it * 128:(it + 1) * 128,
                                               hf * 2048:hf * 2048 + cw]), inc="dgp", n=16)
                    lev = p.ev("dgp")
                    p.wait("vector", lev)
                    p.do("vector", lambda e, wf=wf, cw=cw: e.tensor_scalar(
                        wf[:, 0:cw], wf[:, 0:cw], wp["acc"][:, 7:8], None, ALU.mult), inc="dve")
                    q1_ = p.ev("dve")
                    p.wait("scalar", q1_)
                    if par in trev:
                        p.wait("scalar", trev[par])
                    av = wp["wi0"][:].bitcast(F32)
                    p.do("scalar", lambda e, wf=wf, cw=cw, av=av: e.activation(
                        av[:, 0:cw], wf[:, 0:cw], AF.Abs), inc="act")
                    p.do("scalar", lambda e, wf=wf, wt=wt, cw=cw: e.activation(
                        wt[:, 0:cw], wf[:, 0:cw], AF.Sign), inc="act")
                    q2_ = p.ev("act")
                    p.wait("vector", q2_)
                    p.do("vector", lambda e, cw=cw, av=av: e.tensor_scalar(
                        av[:, 0:cw], av[:, 0:cw], 0.5, None, ALU.is_ge), inc="dve")
                    p.do("vector", lambda e, wt=wt, cw=cw, av=av: e.tensor_mul(
                        wt[:, 0:cw], wt[:, 0:cw], av[:, 0:cw]), inc="dve")
                    qev = p.ev("dve")
                    p.wait("vector", p.ev("dsp"))
                    devs = []
                    for hb in range(cw // 1024):
                        d3 = wp["trst"][:, hb * 1024:(hb + 1) * 1024].rearrange(
                            "p (k c) -> p k c", k=8)
                        tev, dev = tr8(wt, hb * 1024, 8, 128, d3, qev)
                        devs.append(dev)
                    trev[par] = tev
                    p.wait("scalar", devs[-1])
                    for ct in range(cw // 128):
                        ctg = hf * 16 + ct
                        p.do("scalar", lambda e, it=it, ctg=ctg, ct=ct: e.dma_start(
                            spill.ap()[ctg, :, it * 128:(it + 1) * 128],
                            wp["trst"][:, ct * 128:(ct + 1) * 128]), inc="dsp", n=16)
        ternary(w1, 32, 1024, sp_w1t)
        ternary(w2, 8, 4096, sp_w2t)

    def run_pass(p, front, back, order):
        st = {"bend": {}, "psyg": [None] * 4}
        fevs = {}
        items = list(order)
        for i in range(len(items) + 1):
            if i >= 1:
                seq, c = items[i - 1]
                back(p, seq, c, (i - 1) % 2, st, fevs.pop(i - 1))
            if i < len(items):
                seq, c = items[i]
                fevs[i] = front(p, seq, c, i % 2, st)

    def emit(p: P):
        prep(p)
        p.barrier()

        def fa(p_, seq, c, par, st):
            if c == 0:
                sq = seq % 2
                p_.do("vector", lambda e, sq=sq: e.memset(
                    wa["S0" if sq == 0 else "S1"][:], 0.0), inc="dve")
                p_.do("vector", lambda e, sq=sq: e.memset(
                    wa["stil0" if sq == 0 else "stil1"][:], 0.0), inc="dve")
                ev = p_.ev("dve")
                st[("sadd_q", sq)] = [ev] * 4
                st[("scan_q", sq)] = [ev] * 4
            return front_a(p_, seq, c, par, st)
        orderA = [(s, c) for s in range(NSEQ) for c in range(NCK)]
        run_pass(p, fa, back_a, orderA)
        p.barrier()
        for kt in range(16):
            p.do("sync", lambda e, kt=kt: e.dma_start(
                wb["w_outT"][:, kt * D:(kt + 1) * D], sp_wot.ap()[kt]), inc="dld", n=16)
        wo_ev = p.ev("dld")

        def fb1(p_, seq, c, par, st):
            p_.wait("tensor", wo_ev)
            if c == NCK - 1:
                sq = seq % 2
                p_.do("vector", lambda e, sq=sq: e.memset(
                    wb["S0" if sq == 0 else "S1"][:], 0.0), inc="dve")
                p_.do("vector", lambda e, sq=sq: e.memset(
                    wb["stil0" if sq == 0 else "stil1"][:], 0.0), inc="dve")
                ev = p_.ev("dve")
                st[("sadd_q", sq)] = [ev] * 4
                st[("scan_q", sq)] = [ev] * 4
            return front_b1(p_, seq, c, par, st)
        orderB1 = [(s, NCK - 1 - ci) for s in range(NSEQ) for ci in range(NCK)]
        run_pass(p, fb1, back_b1, orderB1)
        p.barrier()
        for kt in range(8):
            p.do("sync", lambda e, kt=kt: e.dma_start(
                wc["w1T"][:, kt * 4 * D:(kt + 1) * 4 * D], sp_w1t.ap()[kt]), inc="dld", n=16)
        for kt in range(32):
            p.do("sync", lambda e, kt=kt: e.dma_start(
                wc["w2T"][:, kt * D:(kt + 1) * D], sp_w2t.ap()[kt]), inc="dld", n=16)
        p.do("gpsimd", lambda e: e.dma_start(wc["b1row"][:], b1.ap()[None, :]), inc="dgp", n=16)
        p.do("gpsimd", lambda e: e.dma_start(wc["b2row"][:], b2.ap()[None, :]), inc="dgp", n=16)
        wl_ev = p.ev("dld")
        bl_ev = p.ev("dgp")

        def fb2(p_, seq, c, par, st):
            p_.wait("tensor", wl_ev)
            p_.wait("tensor", bl_ev)
            p_.wait("sync", wl_ev)
            return front_b2(p_, seq, c, par, st)
        orderB2 = [(s, c) for s in range(NSEQ) for c in range(NCK)]
        run_pass(p, fb2, back_b2, orderB2)
        for e in ENGINES:
            for s in p.sems:
                if p.c[s] > 0:
                    p.wait(e, (s, p.c[s]))

    sems = {s: nc.alloc_semaphore(s) for s in sems_proto}
    with nc.Block() as blk:
        def mk(role):
            def body(eng):
                emit(P(nc, role, eng, sems))
            return body
        blk.sync(mk("sync"))
        blk.tensor(mk("tensor"))
        blk.vector(mk("vector"))
        blk.scalar(mk("scalar"))
        blk.gpsimd(mk("gpsimd"))
    return nc


def make_in_maps(inputs, xs, nseq):
    base = {
        "W_in": np.asarray(inputs["L0_attn_W_in"], np.float32),
        "W_out": np.asarray(inputs["L0_attn_W_out"], np.float32),
        "W1": np.asarray(inputs["L0_ffn_W1"], np.float32),
        "b1": np.asarray(inputs["L0_ffn_b1"], np.float32),
        "W2": np.asarray(inputs["L0_ffn_W2"], np.float32),
        "b2": np.asarray(inputs["L0_ffn_b2"], np.float32),
    }
    for dd in ("f", "r"):
        base[f"conv_w_{dd}"] = np.asarray(inputs[f"L0_attn_conv_w_{dd}"], np.float32)
        base[f"conv_b_{dd}"] = np.asarray(inputs[f"L0_attn_conv_b_{dd}"], np.float32)
        base[f"dt_bias_{dd}"] = np.asarray(inputs[f"L0_attn_dt_bias_{dd}"], np.float32)
        base[f"A_log_{dd}"] = np.asarray(inputs[f"L0_attn_A_log_{dd}"], np.float32)
        base[f"D_{dd}"] = np.asarray(inputs[f"L0_attn_D_{dd}"], np.float32)
        base[f"norm_w_{dd}"] = np.asarray(inputs[f"L0_attn_norm_w_{dd}"], np.float32)
    in_maps = []
    for core in range(NCORE):
        m = dict(base)
        m["x"] = np.ascontiguousarray(xs[nseq * core:nseq * core + nseq])
        in_maps.append(m)
    return in_maps


def kernel(**inputs):
    xs = np.asarray(inputs["L0_attn_x"], np.float32)
    nc = build_nc(NSEQ=2, NCK=32)
    in_maps = make_in_maps(inputs, xs, 2)
    res = run_bass_kernel_spmd(nc, in_maps, core_ids=list(range(NCORE)))
    return np.concatenate([res.results[i]["out"] for i in range(NCORE)], axis=0)


# version 20
# speedup vs baseline: 1.2489x; 1.0292x over previous
import numpy as np
import ml_dtypes
import concourse.bass as bass
import concourse.mybir as mybir
from concourse.bass_utils import run_bass_kernel_spmd

F32 = mybir.dt.float32
BF16 = mybir.dt.bfloat16
I32 = mybir.dt.int32
AF = mybir.ActivationFunctionType
ALU = mybir.AluOpType
AX = mybir.AxisListType

B, L, D = 16, 4096, 1024
DI, DSTATE, NH, HD = 2048, 128, 32, 64
CONVD = 2304
DIP = 4384
CK = 128
CAP = 1e4
EPS = 1e-5
NCORE = 8
HSTR = 160
HDAT = 16

ENGINES = ("sync", "tensor", "vector", "scalar", "gpsimd")
CSEMS = ("pe", "dve", "act", "pool", "dtr", "dld", "dsp", "dgp")
Z = ("pe", 0)


def bfq(x):
    return np.asarray(x, dtype=np.float32).astype(ml_dtypes.bfloat16)


class P:
    def __init__(self, nc, role, eng, sems):
        self.nc, self.role, self.eng, self.sems = nc, role, eng, sems
        self.c = {k: 0 for k in sems}
        self.tq = 0

    OWN = {"vector": "dve", "scalar": "act", "gpsimd": "pool"}

    def do(self, engine, emit, inc=None, n=1, chain=True):
        own = self.OWN.get(engine)
        serialize = own is not None and (inc is None or inc == own)
        if serialize:
            prev = self.c[own]
            inc, n = own, 1
        if inc is not None:
            self.c[inc] += n
        if self.role == engine:
            if serialize and chain and prev > 0:
                self.eng.wait_ge(self.sems[own], prev)
            inst = emit(self.eng)
            if inc is not None:
                inst.then_inc(self.sems[inc], n)

    def tqueue(self):
        return "sync"

    def ev(self, sem):
        return (sem, self.c[sem])

    def wait(self, engine, ev):
        if ev is None or ev[1] <= 0:
            return
        if self.role == engine:
            self.eng.wait_ge(self.sems[ev[0]], ev[1])

    def snap(self):
        return {s: self.c[s] for s in CSEMS}

    def wait_snap(self, engine, snap):
        if snap is None:
            return
        for s in CSEMS:
            self.wait(engine, (s, snap[s]))

    def barrier(self):
        own = {"sync": "b_sy", "tensor": "b_pe", "vector": "b_dv",
               "scalar": "b_ac", "gpsimd": "b_gp"}
        for e in ENGINES:
            self.do(e, lambda en: en.nop(), inc=own[e])
        for e in ENGINES:
            for s in self.sems:
                if self.c[s] > 0:
                    self.wait(e, (s, self.c[s]))


def build_nc(NSEQ=2, NCK=32, detect_races=True, dbg=False):
    LL = NCK * CK
    nc = bass.Bass("TRN2", target_bir_lowering=False,
                   detect_race_conditions=detect_races)

    x_in = nc.dram_tensor("x", [NSEQ, LL, D], F32, kind="ExternalInput")
    out_t = nc.dram_tensor("out", [NSEQ, LL, D], F32, kind="ExternalOutput")
    w_in = nc.dram_tensor("W_in", [DIP, D], F32, kind="ExternalInput")
    w_out = nc.dram_tensor("W_out", [D, DI], F32, kind="ExternalInput")
    w1 = nc.dram_tensor("W1", [4 * D, D], F32, kind="ExternalInput")
    b1 = nc.dram_tensor("b1", [4 * D], F32, kind="ExternalInput")
    w2 = nc.dram_tensor("W2", [D, 4 * D], F32, kind="ExternalInput")
    b2 = nc.dram_tensor("b2", [D], F32, kind="ExternalInput")
    pr = {}
    for dd in ("f", "r"):
        pr[f"convw_{dd}"] = nc.dram_tensor(f"conv_w_{dd}", [CONVD, 4], F32, kind="ExternalInput")
        pr[f"convb_{dd}"] = nc.dram_tensor(f"conv_b_{dd}", [CONVD], F32, kind="ExternalInput")
        pr[f"dtb_{dd}"] = nc.dram_tensor(f"dt_bias_{dd}", [NH], F32, kind="ExternalInput")
        pr[f"alog_{dd}"] = nc.dram_tensor(f"A_log_{dd}", [NH], F32, kind="ExternalInput")
        pr[f"dsk_{dd}"] = nc.dram_tensor(f"D_{dd}", [NH], F32, kind="ExternalInput")
        pr[f"nrm_{dd}"] = nc.dram_tensor(f"norm_w_{dd}", [DI], F32, kind="ExternalInput")

    E1 = np.repeat(np.eye(NH, dtype=np.float32), CK, axis=1)
    c_E2 = nc.inline_tensor(bfq(np.vstack([E1, E1])), "cE2")
    c_triu = nc.inline_tensor(bfq(np.triu(np.ones((CK, CK), np.float32))), "ctriu")
    c_tril = nc.inline_tensor(bfq(np.tril(np.ones((CK, CK), np.float32))), "ctril")
    c_identf = nc.inline_tensor(np.eye(CK, dtype=np.float32), "cidentf")
    c_identb = nc.inline_tensor(np.eye(CK).astype(ml_dtypes.bfloat16), "cidentb")
    c_bias = nc.inline_tensor(np.tile(np.array([[1.0, EPS, 0.0, 0.0]], dtype=np.float32), (128, 1)), "cbias")

    KD = "ExternalOutput" if dbg else "Internal"
    sp_yf = nc.dram_tensor("sp_yf", [NSEQ, NCK, CK, DI], BF16, kind=KD)
    sp_sz = nc.dram_tensor("sp_sz", [NSEQ, NCK, CK, DI], BF16, kind=KD)
    sp_xbc = nc.dram_tensor("sp_xbc", [NSEQ, NCK, CK, 18 * HSTR], BF16, kind=KD)
    sp_dt = nc.dram_tensor("sp_dt", [NSEQ, NCK, CK, NH], F32, kind=KD)
    sp_xn = nc.dram_tensor("sp_xn", [NSEQ, NCK, CK, D], BF16, kind=KD)
    sp_s = nc.dram_tensor("sp_s", [NSEQ, NCK, CK, D], F32, kind=KD)
    sp_flat = nc.dram_tensor("sp_flat", [NSEQ, NCK, 96, CK], BF16, kind=KD)
    sp_w1t = nc.dram_tensor("sp_w1t", [8, CK, 4 * D], BF16, kind=KD)
    sp_w2t = nc.dram_tensor("sp_w2t", [32, CK, D], BF16, kind=KD)
    sp_wot = nc.dram_tensor("sp_wot", [16, CK, D], BF16, kind=KD)

    SB_BASE = 16512
    SB_LIMIT = 229000
    cur = [0]

    def salloc(name, shape, dtype):
        esz = 2 if dtype == BF16 else 4
        nbytes = int(np.prod(shape[1:])) * esz
        o = (cur[0] + 31) // 32 * 32
        cur[0] = o + nbytes
        assert cur[0] <= SB_LIMIT, f"SBUF overflow at {name}: {cur[0]}"
        return nc.alloc_sbuf_tensor_at(name, list(shape), dtype, offset=o)

    def region(base, specs, pref=""):
        cur[0] = base + (SB_BASE if base < SB_BASE else 0)
        out = {}
        for nm, shape, dt_ in specs:
            out[nm] = salloc(pref + nm, shape, dt_)
        return out, cur[0]

    sb0, P0E = region(0, [
        ("onesbf", [2, 128], BF16),
        ("onescol", [128, 2], BF16),
        ("col1", [128, 16], F32),
        ("rowbf", [1, 16], BF16),
        ("rowf", [1, 256], F32),
        ("rowt", [1, 256], BF16),
        ("rowD", [1, 64], BF16),
        ("colbf", [128, 4], BF16),
        ("cbias", [128, 4], F32),
        ("identb", [128, 128], BF16),
        ("scrA", [128, 2048], BF16),
    ])
    SML = [
        ("dtraw", [128, NH], F32),
        ("dt", [128, NH], F32),
        ("sm1", [128, NH], F32),
        ("sm2", [128, NH], F32),
        ("acum", [128, 2 * NH], F32),
        ("totbc", [128, NH], F32),
        ("dstate", [128, NH], F32),
        ("exptot", [128, NH], F32),
        ("dtds", [128, NH], BF16),
        ("pk96", [128, 96], F32),
        ("pk64", [128, 64], F32),
        ("vtok", [128, 32], F32),
        ("bstg", [128, 32], BF16),
        ("flatpk", [96, 128], BF16),
        ("hilo", [128, 2 * NH], BF16),
        ("tot_hl", [1, 2 * NH], BF16),
        ("trow", [1, NH], F32),
        ("lhsT_mask", [128, 128], BF16),
    ]
    sb1, P1E0 = region(P0E, [
        ("rhs_mask", [128, 4096], BF16),
        ("expR", [1, 4096], BF16),
        ("triu", [128, 128], BF16),
        ("tril", [128, 128], BF16),
        ("identf", [128, 128], F32),
        ("convw_f", [128, 72], F32),
        ("convw_r", [128, 72], F32),
        ("convb_f", [128, 18], F32),
        ("convb_r", [128, 18], F32),
        ("dtb_f", [128, NH], F32),
        ("dtb_r", [128, NH], F32),
        ("negA_f", [128, NH], F32),
        ("negA_r", [128, NH], F32),
        ("Dbc_f", [128, NH], BF16),
        ("Dbc_r", [128, NH], BF16),
        ("nwbc_f", [128, DI], BF16),
        ("nwbc_r", [128, DI], BF16),
        ("cbst", [32, 128], F32),
    ])
    sml0, P1E1 = region(P1E0, SML, pref="s0_")
    sml1, P1E = region(P1E1, SML, pref="s1_")
    sml = [sml0, sml1]
    sbm = {}
    sbm.update(sb0)
    sbm.update(sb1)

    wa, WAE = region(P1E, [
        ("w_inT", [128, 8 * DIP], BF16),
        ("xA0", [128, D], F32),
        ("xA1", [128, D], F32),
        ("xn", [128, D], BF16),
        ("xnT", [128, 8 * 128], BF16),
        ("siluz0", [128, DI], BF16),
        ("siluz1", [128, DI], BF16),
        ("halo", [128, 18 * HSTR], BF16),
        ("tail0", [128, 18 * 4], BF16),
        ("tail1", [128, 18 * 4], BF16),
        ("convT0", [128, CONVD], BF16),
        ("convT1", [128, CONVD], BF16),
        ("xstok", [128, DI], BF16),
        ("btok", [128, 128], BF16),
        ("gt", [128, 128], BF16),
        ("gmm", [128, 4096], BF16),
        ("gmv", [128, 4096], BF16),
        ("y0", [128, DI], BF16),
        ("dx", [128, DI], BF16),
        ("S0", [128, DI], F32),
        ("S1", [128, DI], F32),
        ("stil0", [128, DI], BF16),
        ("stil1", [128, DI], BF16),
        ("xdec", [128, DI], BF16),
    ], pref="a_")

    WPB = P1E + ((8 * DIP * 2 + 31) // 32 * 32)
    wp, WPE = region(WPB, [
        ("wf0", [128, 2048], F32),
        ("wf1", [128, 2048], F32),
        ("wi0", [128, 2048], I32),
        ("wtmp0", [128, 2048], BF16),
        ("wtmp1", [128, 2048], BF16),
        ("trst", [128, 2048], BF16),
        ("acc", [128, 8], F32),
    ], pref="p_")

    wb, WBE = region(P1E, [
        ("w_outT", [128, 16 * D], BF16),
        ("szB0", [128, DI], BF16),
        ("szB1", [128, DI], BF16),
        ("yfB0", [128, DI], BF16),
        ("yfB1", [128, DI], BF16),
        ("haloB0", [128, 18 * HSTR], BF16),
        ("haloB1", [128, 18 * HSTR], BF16),
        ("convT0", [128, CONVD], BF16),
        ("convT1", [128, CONVD], BF16),
        ("tailR0", [128, 18 * 4], BF16),
        ("tailR1", [128, 18 * 4], BF16),
        ("xstok", [128, DI], BF16),
        ("btok", [128, 128], BF16),
        ("gt", [128, 128], BF16),
        ("gmm", [128, 4096], BF16),
        ("gmv", [128, 4096], BF16),
        ("y0", [128, DI], BF16),
        ("dx", [128, DI], BF16),
        ("S0", [128, DI], F32),
        ("S1", [128, DI], F32),
        ("stil0", [128, DI], BF16),
        ("stil1", [128, DI], BF16),
        ("xdec", [128, DI], BF16),
        ("ycombT", [128, DI], BF16),
        ("sevac", [128, D], F32),
    ], pref="b_")

    wc, WCE = region(P0E, [
        ("w1T", [128, 8 * 4 * D], BF16),
        ("w2T", [128, 32 * D], BF16),
        ("xC0", [128, D], F32),
        ("xC1", [128, D], F32),
        ("xnC", [128, D], BF16),
        ("xnTC0", [128, 8 * 128], BF16),
        ("xnTC1", [128, 8 * 128], BF16),
        ("hbuf", [128, 4 * D], BF16),
        ("hT", [128, 32 * 128], BF16),
        ("sC0", [128, D], F32),
        ("sC1", [128, D], F32),
        ("thC", [128, D], BF16),
        ("gC", [128, D], F32),
        ("outC", [128, D], F32),
        ("b1row", [1, 4 * D], BF16),
        ("b2row", [1, D], BF16),
    ], pref="c_")

    nc.const_aps.register(F32, 1.0, sb0["cbias"][:, 0:1])
    nc.const_aps.register(F32, EPS, sb0["cbias"][:, 1:2])
    nc.const_aps.register(F32, 0.0, sb0["cbias"][:, 2:3])
    nc.const_aps.register(BF16, 0.0, sb0["cbias"][:, 3:4].bitcast(BF16)[:, 0:1])

    ps_a = nc.alloc_psum_tensor("ps_a", [128, 1024], F32)
    ps_s = nc.alloc_psum_tensor("ps_s", [128, 512], F32)
    ps_x = nc.alloc_psum_tensor("ps_x", [128, 512], F32)
    ps_y = nc.alloc_psum_tensor("ps_y", [128, 2048], F32)

    sems_proto = list(CSEMS) + ["b_sy", "b_pe", "b_dv", "b_ac", "b_gp"]

    def emit_hilo(p, src32, hi_bf, tmp32, lo_bf):
        p.do("vector", lambda e: e.tensor_copy(hi_bf, src32), inc="dve")
        p.do("vector", lambda e: e.tensor_copy(tmp32, hi_bf), inc="dve")
        p.do("vector", lambda e: e.tensor_sub(tmp32, src32, tmp32), inc="dve")
        p.do("vector", lambda e: e.tensor_copy(lo_bf, tmp32), inc="dve")

    def bview(t, nrep):
        return t[:, :, None].broadcast_to([t.shape[0], t.shape[1], nrep])

    def pe_tp(p, srcs, psview, dst, pre_waits_pe=(), pre_waits_dve=(), eng="vector"):
        for ev in pre_waits_pe:
            p.wait("tensor", ev)
        n = len(srcs)
        for i, s_ in enumerate(srcs):
            p.do("tensor", lambda e, i=i, s_=s_: e.transpose(
                psview[:, i * 128:(i + 1) * 128], s_, sbm["identb"][:]),
                inc=("pe" if i == n - 1 else None))
        pe_ev = p.ev("pe")
        p.wait(eng, pe_ev)
        for ev in pre_waits_dve:
            p.wait(eng, ev)
        srcv = psview[:, 0:n * 128]
        if len(dst.shape) == 3:
            srcv = srcv.rearrange("p (a b) -> p a b", b=128)
        if eng == "vector":
            p.do("vector", lambda e: e.tensor_copy(dst, srcv), inc="dve")
            return pe_ev, p.ev("dve")
        p.do("scalar", lambda e: e.activation(dst, srcv, AF.Identity), inc="act")
        return pe_ev, p.ev("act")

    def dt_chain(p, dd, par, st, fill=None):
        sb = sml[par]
        tri = sbm["triu"] if dd == "f" else sbm["tril"]
        def drain(n):
            if fill:
                for _ in range(min(n, len(fill))):
                    fill.pop(0)()
        p.do("vector", lambda e: e.tensor_add(sb["sm1"][:], sb["dtraw"][:],
                                              sbm[f"dtb_{dd}"][:]), inc="dve")
        t1_ev = p.ev("dve")
        p.wait("scalar", t1_ev)
        p.do("scalar", lambda e: e.activation(sb["sm2"][:], sb["sm1"][:], AF.Abs), inc="act")
        p.do("scalar", lambda e: e.activation(sb["sm2"][:], sb["sm2"][:], AF.Exp, scale=-1.0), inc="act")
        p.do("scalar", lambda e: e.activation(sb["sm2"][:], sb["sm2"][:], AF.Ln, bias=1.0), inc="act")
        p.do("scalar", lambda e: e.activation(sb["dt"][:], sb["sm1"][:], AF.Relu), inc="act")
        sp_ev = p.ev("act")
        drain(2)
        p.wait("vector", sp_ev)
        p.do("vector", lambda e: e.tensor_add(sb["dt"][:], sb["dt"][:], sb["sm2"][:]), inc="dve")
        p.do("vector", lambda e: e.tensor_scalar_max(sb["dt"][:], sb["dt"][:], 1e-30), inc="dve")
        dt_ev = p.ev("dve")
        p.wait("scalar", dt_ev)
        p.do("scalar", lambda e: e.activation(sb["acum"][:, NH:2 * NH], sb["dt"][:], AF.Ln), inc="act")
        lnd_ev = p.ev("act")
        p.do("vector", lambda e: e.tensor_mul(sb["sm1"][:], sb["dt"][:], sbm[f"negA_{dd}"][:]), inc="dve")
        emit_hilo(p, sb["sm1"][:], sb["hilo"][:, 0:NH], sb["sm2"][:], sb["hilo"][:, NH:2 * NH])
        hl_ev = p.ev("dve")
        p.wait("tensor", hl_ev)
        p.wait("tensor", st.get("ps_s_g"))
        p.do("tensor", lambda e: e.matmul(ps_s[:, 0:NH], tri[:], sb["hilo"][:, 0:NH],
                                          start=True, stop=False))
        p.do("tensor", lambda e: e.matmul(ps_s[:, 0:NH], tri[:], sb["hilo"][:, NH:2 * NH],
                                          start=False, stop=True), inc="pe")
        p.do("tensor", lambda e: e.matmul(ps_s[0:1, 32:32 + NH], sbm["onescol"][:, 0:1],
                                          sb["hilo"][:, 0:NH], start=True, stop=False))
        p.do("tensor", lambda e: e.matmul(ps_s[0:1, 32:32 + NH], sbm["onescol"][:, 0:1],
                                          sb["hilo"][:, NH:2 * NH], start=False, stop=True), inc="pe")
        cs_ev = p.ev("pe")
        drain(2)
        p.wait("vector", cs_ev)
        p.do("vector", lambda e: e.tensor_copy(sb["acum"][:, 0:NH], ps_s[:, 0:NH]), inc="dve")
        p.do("vector", lambda e: e.tensor_copy(sb["trow"][:], ps_s[0:1, 32:32 + NH]), inc="dve")
        emit_hilo(p, sb["trow"][:], sb["tot_hl"][0:1, 0:NH], sb["sm2"][0:1, 0:NH],
                  sb["tot_hl"][0:1, NH:2 * NH])
        th_ev = p.ev("dve")
        p.wait("tensor", th_ev)
        p.do("tensor", lambda e: e.matmul(ps_s[:, 64:64 + NH], sbm["onesbf"][0:1, :],
                                          sb["tot_hl"][0:1, 0:NH], start=True, stop=False))
        p.do("tensor", lambda e: e.matmul(ps_s[:, 64:64 + NH], sbm["onesbf"][0:1, :],
                                          sb["tot_hl"][0:1, NH:2 * NH], start=False, stop=True), inc="pe")
        tb_ev = p.ev("pe")
        drain(2)
        p.wait("vector", tb_ev)
        p.wait("vector", lnd_ev)
        p.do("vector", lambda e: e.tensor_copy(sb["totbc"][:], ps_s[:, 64:64 + NH]), inc="dve")
        p.do("vector", lambda e: e.tensor_sub(sb["dstate"][:], sb["totbc"][:], sb["acum"][:, 0:NH]), inc="dve")
        ds_ev = p.ev("dve")
        p.wait("scalar", ds_ev)
        p.do("scalar", lambda e: e.activation(sb["dstate"][:], sb["dstate"][:], AF.Exp), inc="act")
        p.do("scalar", lambda e: e.activation(sb["exptot"][:], sb["totbc"][:], AF.Exp), inc="act")
        p.do("scalar", lambda e: e.activation(sb["pk96"][:, 64:96], sb["acum"][:, 0:NH], AF.Exp), inc="act")
        ex_ev = p.ev("act")
        drain(1)
        p.wait("vector", ex_ev)
        p.do("vector", lambda e: e.tensor_mul(sb["dtds"][:], sb["dt"][:], sb["dstate"][:]), inc="dve")
        p.do("vector", lambda e: e.tensor_copy(sb["bstg"][:], sb["acum"][:, 0:NH]), inc="dve")
        p.do("vector", lambda e: e.tensor_copy(sb["pk96"][:, 0:32], sb["bstg"][:]), inc="dve")
        p.do("vector", lambda e: e.tensor_sub(sb["pk96"][:, 32:64], sb["acum"][:, 0:NH],
                                              sb["pk96"][:, 0:32]), inc="dve")
        p.do("vector", lambda e: e.tensor_sub(sb["vtok"][:], sb["acum"][:, NH:2 * NH],
                                              sb["acum"][:, 0:NH]), inc="dve")
        p.do("vector", lambda e: e.tensor_copy(sb["bstg"][:], sb["vtok"][:]), inc="dve")
        p.do("vector", lambda e: e.tensor_copy(sb["pk64"][:, 0:32], sb["bstg"][:]), inc="dve")
        p.do("vector", lambda e: e.tensor_sub(sb["pk64"][:, 32:64], sb["vtok"][:],
                                              sb["pk64"][:, 0:32]), inc="dve")
        pkd_ev = p.ev("dve")
        p.wait("tensor", pkd_ev)
        p.do("tensor", lambda e: e.transpose(ps_s[0:96, 128:256], sb["pk96"][:, 0:96],
                                             sbm["identf"][:]), inc="pe")
        p.do("tensor", lambda e: e.transpose(ps_s[0:64, 256:384], sb["pk64"][:, 0:64],
                                             sbm["identf"][:]), inc="pe")
        tr_ev = p.ev("pe")
        drain(1)
        p.wait("vector", tr_ev)
        p.do("vector", lambda e: e.tensor_copy(sb["flatpk"][0:96, :], ps_s[0:96, 128:256]), inc="dve")
        p.do("vector", lambda e: e.tensor_copy(sb["lhsT_mask"][0:64, :], ps_s[0:64, 256:384]), inc="dve")
        st["ps_s_g"] = p.ev("dve")
        drain(8)
        return p.ev("dve")

    def conv_batches(p, dd, halo, out_buf):
        wv = sbm[f"convw_{dd}"]
        offs = [13, 14, 15, 16] if dd == "f" else [19, 18, 17, 16]
        def hs(i, k):
            return halo[:, i * HSTR + offs[k]:i * HSTR + offs[k] + 128]
        evs = {}
        def tap(i0, i1, k, first):
            def go():
                for i in range(i0, i1):
                    if k == 0:
                        p.do("vector", lambda e, i=i: e.tensor_scalar(
                            out_buf[:, i * 128:(i + 1) * 128], hs(i, 0),
                            wv[:, i * 4:i * 4 + 1], sbm[f"convb_{dd}"][:, i:i + 1],
                            ALU.mult, ALU.add), inc="dve", chain=(first and i == i0))
                    else:
                        p.do("vector", lambda e, i=i, k=k: e.scalar_tensor_tensor(
                            out_buf[:, i * 128:(i + 1) * 128], hs(i, k),
                            wv[:, i * 4 + k:i * 4 + k + 1],
                            out_buf[:, i * 128:(i + 1) * 128],
                            ALU.mult, ALU.add), inc="dve", chain=False)
                if k == 3:
                    evs[i0] = p.ev("dve")
            return go
        batches = [tap(0, 8, k, True) for k in range(4)] +                   [tap(8, 18, k, False) for k in range(4)]
        def finish():
            out = []
            for i0, i1 in ((0, 8), (8, 18)):
                p.wait("scalar", evs[i0])
                p.do("scalar", lambda e, i0=i0, i1=i1: e.activation(
                    out_buf[:, i0 * 128:i1 * 128], out_buf[:, i0 * 128:i1 * 128],
                    AF.Silu), inc="act")
                out.append(p.ev("act"))
            return out
        return batches, finish

    def back_ssd(p, dd, seq, c, par, w, st, fev, first):
        sb = sml[par]
        tri = sbm["triu"] if dd == "f" else sbm["tril"]
        convT = w["convT0"] if par == 0 else w["convT1"]
        sq = seq % 2
        Sb = w["S0"] if sq == 0 else w["S1"]
        stilb = w["stil0"] if sq == 0 else w["stil1"]
        szb = fev["szbuf"]
        psyg = st["psyg"]
        p.wait("sync", fev["flat"])
        p.wait("sync", st.get("mask_pe"))
        flview = sp_flat.ap()[seq, c].rearrange("(a b) t -> a (b t)", a=3)
        p.do("sync", lambda e: e.dma_start(sbm["rhs_mask"][64:66, :], flview[0:2, :]), inc="dld", n=16)
        p.do("sync", lambda e: e.dma_start(sbm["expR"][:], flview[2:3, :]), inc="dld", n=16)
        fl_ev = p.ev("dld")
        for q in range(2):
            pw = [fev["conv"][q], psyg[q * 2], psyg[q * 2 + 1]]
            dw = [st.get("states_pe"), st.get("xdec_dve")] if q == 0 else []
            _, xe = pe_tp(p, [convT[:, (q * 8 + i) * 128:(q * 8 + i + 1) * 128]
                              for i in range(8)],
                          ps_y[:, q * 1024:(q + 1) * 1024].bitcast(BF16),
                          w["xstok"][:, q * 1024:(q + 1) * 1024],
                          pre_waits_pe=pw, pre_waits_dve=dw)
            psyg[q * 2] = xe
            psyg[q * 2 + 1] = xe
        _, be = pe_tp(p, [convT[:, 16 * 128:17 * 128]],
                      ps_x[:].bitcast(BF16)[:, 0:128], w["btok"][:],
                      pre_waits_pe=(st.get("gt_dve"), fev["conv"][1]))
        tr_ev = p.ev("dve")
        p.wait("tensor", fev["conv"][1])
        p.wait("tensor", tr_ev)
        p.do("tensor", lambda e: e.matmul(ps_x[:, 0:128], convT[:, 16 * 128:17 * 128],
                                          convT[:, 17 * 128:18 * 128], start=True, stop=True), inc="pe")
        g_ev = p.ev("pe")
        p.wait("vector", g_ev)
        p.do("vector", lambda e: e.tensor_mul(w["gt"][:], ps_x[:, 0:128], tri[:]), inc="dve")
        st["gt_dve"] = p.ev("dve")
        gt_ev = st["gt_dve"]
        first_exp = True
        cg_h = [None, None]
        for half in range(2):
            for jj in range(4):
                j = half * 4 + jj
                if jj == 0:
                    p.wait("tensor", fl_ev)
                    p.wait("tensor", fev["vrows"])
                p.wait("tensor", psyg[jj])
                p.do("tensor", lambda e, j=j, jj=jj, par=par: e.matmul(
                    ps_y[:, jj * 512:(jj + 1) * 512], sml[par]["lhsT_mask"][0:66, :],
                    sbm["rhs_mask"][0:66, j * 512:(j + 1) * 512],
                    start=True, stop=True), inc="pe")
                m_ev = p.ev("pe")
                p.wait("scalar", m_ev)
                if first_exp:
                    p.wait("scalar", st.get("ydyo_pe"))
                    p.wait("scalar", st.get("gm_dve"))
                    first_exp = False
                p.do("scalar", lambda e, j=j, jj=jj: e.activation(
                    w["gmm"][:, j * 512:(j + 1) * 512], ps_y[:, jj * 512:(jj + 1) * 512],
                    AF.Exp), inc="act", chain=(j == 0))
                psyg[jj] = p.ev("act")
            me_h = p.ev("act")
            p.wait("vector", me_h)
            p.do("vector", lambda e, half=half: e.scalar_tensor_tensor(
                w["gmm"][:, half * 2048:(half + 1) * 2048].rearrange("p (a b) -> p a b", a=16),
                w["gmm"][:, half * 2048:(half + 1) * 2048].rearrange("p (a b) -> p a b", a=16),
                CAP, w["gt"][:, None, :].broadcast_to([128, 16, 128]),
                ALU.min, ALU.mult), inc="dve")
            cg_h[half] = p.ev("dve")
        gv_h = [None, None]
        for half in range(2):
            for jj in range(4):
                j = half * 4 + jj
                p.wait("tensor", psyg[jj])
                p.do("tensor", lambda e, j=j, jj=jj: e.matmul(
                    ps_y[:, jj * 512:(jj + 1) * 512], sbm["onesbf"][0:1, :],
                    sbm["expR"][0:1, j * 512:(j + 1) * 512], start=True, stop=True), inc="pe")
                v_ev = p.ev("pe")
                p.wait("scalar", v_ev)
                p.do("scalar", lambda e, j=j, jj=jj: e.activation(
                    w["gmv"][:, j * 512:(j + 1) * 512], ps_y[:, jj * 512:(jj + 1) * 512],
                    AF.Identity), inc="act", chain=False)
                psyg[jj] = p.ev("act")
            ve_h = p.ev("act")
            p.wait("vector", ve_h)
            p.do("vector", lambda e, half=half: e.tensor_mul(
                w["gmv"][:, half * 2048:(half + 1) * 2048].rearrange("p (a b) -> p a b", a=16),
                w["gmv"][:, half * 2048:(half + 1) * 2048].rearrange("p (a b) -> p a b", a=16),
                convT[:, None, 17 * 128:18 * 128].broadcast_to([128, 16, 128])), inc="dve")
            gv_h[half] = p.ev("dve")
        st["gm_dve"] = p.ev("dve")
        gm_ev = st["gm_dve"]
        p.wait("vector", fev["vrd"])
        p.wait("vector", tr_ev)
        p.wait("vector", st.get("states_pe"))
        p.do("vector", lambda e: e.tensor_mul(
            w["xdec"][:].rearrange("p (a b) -> p a b", a=NH),
            w["xstok"][:].rearrange("p (a b) -> p a b", a=NH),
            bview(sb["dtds"][:], HD)), inc="dve")
        st["xdec_dve"] = p.ev("dve")
        xd_ev = st["xdec_dve"]
        p.do("vector", lambda e: e.tensor_mul(
            w["dx"][:].rearrange("p (a b) -> p a b", a=NH),
            w["xstok"][:].rearrange("p (a b) -> p a b", a=NH),
            bview(sbm[f"Dbc_{dd}"][:], HD)), inc="dve", chain=False)
        dx_ev = p.ev("dve")
        p.wait("tensor", tr_ev)
        scq_prev = st.get(("scan_q", sq)) or [None] * 4
        yq_ev = {}
        for q in range(4):
            p.wait("tensor", psyg[q])
            p.wait("tensor", cg_h[q // 2])
            p.wait("tensor", gv_h[q // 2])
            p.wait("tensor", scq_prev[q])
            for hh in range(8):
                h = q * 8 + hh
                p.do("tensor", lambda e, h=h, q=q, hh=hh: e.matmul(
                    ps_y[:, h * 64:(h + 1) * 64], w["gmm"][:, h * 128:(h + 1) * 128],
                    w["xstok"][:, h * 64:(h + 1) * 64], start=True, stop=False,
                    skip_group_check=True))
                p.do("tensor", lambda e, h=h, q=q, hh=hh, stilb=stilb: e.matmul(
                    ps_y[:, h * 64:(h + 1) * 64], w["gmv"][:, h * 128:(h + 1) * 128],
                    stilb[:, h * 64:(h + 1) * 64], start=False, stop=True,
                    skip_group_check=True), inc="pe")
            q_pe = p.ev("pe")
            p.wait("scalar", q_pe)
            if q == 0:
                p.wait("scalar", st.get("y0_free"))
                p.wait("scalar", st.get("yread_dve"))
            p.do("scalar", lambda e, q=q: e.activation(
                w["y0"][:, q * 512:(q + 1) * 512], ps_y[:, q * 512:(q + 1) * 512],
                AF.Identity), inc="act", chain=(q == 0))
            psyg[q] = p.ev("act")
            yq_ev[q] = psyg[q]
        st["ydyo_pe"] = p.ev("pe")
        ye_ev = p.ev("act")
        p.wait("tensor", xd_ev)
        stq = []
        for q in range(4):
            p.wait("tensor", psyg[q])
            p.do("tensor", lambda e, q=q: e.matmul(
                ps_y[:, q * 512:(q + 1) * 512], w["btok"][:],
                w["xdec"][:, q * 512:(q + 1) * 512], start=True, stop=True), inc="pe")
            stq.append(p.ev("pe"))
        st["states_pe"] = p.ev("pe")
        st_ev = st["states_pe"]
        p.wait("gpsimd", fev["exp"])
        saq, scq = [None] * 4, [None] * 4
        for q in range(4):
            p.wait("gpsimd", scq_prev[q])
            p.do("gpsimd", lambda e, q=q, Sb=Sb: e.tensor_mul(
                Sb[:, q * 512:(q + 1) * 512].rearrange("p (a b) -> p a b", a=8),
                Sb[:, q * 512:(q + 1) * 512].rearrange("p (a b) -> p a b", a=8),
                bview(sb["exptot"][:, q * 8:(q + 1) * 8], HD)), inc="pool")
            smq = p.ev("pool")
            p.wait("vector", smq)
            p.wait("vector", stq[q])
            p.do("vector", lambda e, q=q, Sb=Sb: e.tensor_add(
                Sb[:, q * 512:(q + 1) * 512], Sb[:, q * 512:(q + 1) * 512],
                ps_y[:, q * 512:(q + 1) * 512]), inc="dve")
            saq[q] = p.ev("dve")
            p.wait("scalar", saq[q])
            p.do("scalar", lambda e, q=q, Sb=Sb, stilb=stilb: e.activation(
                stilb[:, q * 512:(q + 1) * 512],
                Sb[:, q * 512:(q + 1) * 512], AF.Identity), inc="act", chain=False)
            scq[q] = p.ev("act")
            psyg[q] = saq[q]
        st[("sadd_q", sq)], st[("scan_q", sq)] = saq, scq
        p.wait("vector", ye_ev)
        p.wait("vector", dx_ev)
        for half in range(2):
            p.do("vector", lambda e, half=half: e.tensor_add(
                w["y0"][:, half * 1024:(half + 1) * 1024],
                w["y0"][:, half * 1024:(half + 1) * 1024],
                w["dx"][:, half * 1024:(half + 1) * 1024]), inc="dve",
                chain=(half == 0))
        p.wait("vector", fev["sz"])
        for half in range(2):
            p.do("vector", lambda e, half=half, szb=szb: e.tensor_mul(
                w["dx"][:, half * 1024:(half + 1) * 1024],
                w["y0"][:, half * 1024:(half + 1) * 1024],
                szb[:, half * 1024:(half + 1) * 1024]), inc="dve")
        y2_ev = p.ev("dve")
        p.wait("scalar", y2_ev)
        p.do("scalar", lambda e: e.activation(
            sbm["scrA"][:, 0:2048], w["dx"][:],
            AF.Square, accum_out=sbm["col1"][:, 10:11]), inc="act")
        sq_ev = p.ev("act")
        p.wait("vector", sq_ev)
        p.do("vector", lambda e: e.tensor_scalar(sbm["col1"][:, 11:12], sbm["col1"][:, 10:11],
                                                 1.0 / DI, None, ALU.mult), inc="dve")
        ms_ev = p.ev("dve")
        p.wait("scalar", ms_ev)
        p.do("scalar", lambda e: e.activation(sbm["col1"][:, 12:13], sbm["col1"][:, 11:12],
                                              AF.Ln, bias=EPS), inc="act")
        p.do("scalar", lambda e: e.activation(sbm["col1"][:, 13:14], sbm["col1"][:, 12:13],
                                              AF.Exp, scale=-0.5), inc="act")
        rs_ev = p.ev("act")
        p.wait("vector", rs_ev)
        p.do("vector", lambda e: e.scalar_tensor_tensor(
            w["y0"][:], w["dx"][:], sbm["col1"][:, 13:14], sbm[f"nwbc_{dd}"][:],
            ALU.mult, ALU.mult), inc="dve")
        y_ev = p.ev("dve")
        st["yread_dve"] = y_ev
        st["mask_pe"] = st["states_pe"]
        return y_ev

    def front_a(p, seq, c, par, st):
        sb = sml[par]
        fev = {}
        xbuf = wa["xA0"] if par == 0 else wa["xA1"]
        bend = st["bend"].get(par)
        for e_ in ("vector", "scalar", "sync", "tensor", "gpsimd"):
            p.wait_snap(e_, bend)
        p.do("sync", lambda e, seq=seq, c=c, xbuf=xbuf: e.dma_start(
            xbuf[:], x_in.ap()[seq, c * CK:(c + 1) * CK, :]), inc="dld", n=16)
        xld = p.ev("dld")
        p.wait("vector", xld)
        p.wait("scalar", xld)
        p.do("scalar", lambda e, xbuf=xbuf: e.activation(
            sbm["scrA"][:, 0:1024], xbuf[:], AF.Square,
            accum_out=sbm["col1"][:, 0:1]), inc="act")
        sq_ev = p.ev("act")
        p.do("vector", lambda e, xbuf=xbuf: e.reduce_sum(sbm["col1"][:, 1:2], xbuf[:], AX.X), inc="dve")
        p.wait("vector", sq_ev)
        p.do("vector", lambda e: e.tensor_scalar(sbm["col1"][:, 2:3], sbm["col1"][:, 0:1],
                                                 1.0 / D, None, ALU.mult), inc="dve")
        p.do("vector", lambda e: e.tensor_scalar(sbm["col1"][:, 3:4], sbm["col1"][:, 1:2],
                                                 1.0 / D, None, ALU.mult), inc="dve")
        p.do("vector", lambda e: e.tensor_mul(sbm["col1"][:, 4:5], sbm["col1"][:, 3:4],
                                              sbm["col1"][:, 3:4]), inc="dve")
        p.do("vector", lambda e: e.tensor_sub(sbm["col1"][:, 5:6], sbm["col1"][:, 2:3],
                                              sbm["col1"][:, 4:5]), inc="dve")
        var_ev = p.ev("dve")
        p.wait("scalar", var_ev)
        p.do("scalar", lambda e: e.activation(sbm["col1"][:, 6:7], sbm["col1"][:, 5:6],
                                              AF.Ln, bias=EPS), inc="act")
        p.do("scalar", lambda e: e.activation(sbm["col1"][:, 7:8], sbm["col1"][:, 6:7],
                                              AF.Exp, scale=-0.5), inc="act")
        rstd_ev = p.ev("act")
        p.wait("vector", rstd_ev)
        p.wait("vector", st.get("xn_free"))
        p.wait("vector", st.get("xn_free_sp"))
        p.do("vector", lambda e, xbuf=xbuf: e.scalar_tensor_tensor(
            wa["xn"][:], xbuf[:], sbm["col1"][:, 3:4],
            sbm["col1"][:, 7:8].broadcast_to([128, D]),
            ALU.subtract, ALU.mult), inc="dve")
        xn_ev = p.ev("dve")
        p.wait("scalar", xn_ev)
        p.do("scalar", lambda e, seq=seq, c=c: e.dma_start(
            sp_xn.ap()[seq, c], wa["xn"][:]), inc="dsp", n=16)
        xnsp_ev = p.ev("dsp")
        st["xn_free_sp"] = xnsp_ev
        psv = ps_s[:].bitcast(BF16)
        _, xnT_ev = pe_tp(p, [wa["xn"][:, k * 128:(k + 1) * 128] for k in range(8)],
                          psv, wa["xnT"][:],
                          pre_waits_pe=(xn_ev, st.get("ps_s_g"), st.get("inproj_pe")),
                          pre_waits_dve=(st.get("inproj_pe"),))
        st["ps_s_g"] = xnT_ev
        st["xn_free"] = p.ev("pe")
        szbuf = wa["siluz0"] if par == 0 else wa["siluz1"]
        p.wait("tensor", xnT_ev)
        jev = st.get("jev", {})
        halo_evs = []
        for j in range(4):
            psl = ps_a[:, (j % 2) * 512:(j % 2) * 512 + 512]
            p.wait("tensor", jev.get(j % 2))
            for k in range(8):
                p.do("tensor", lambda e, j=j, k=k, psl=psl: e.matmul(
                    psl[:], wa["xnT"][:, k * 128:(k + 1) * 128],
                    wa["w_inT"][:, k * DIP + j * 512:k * DIP + j * 512 + 512],
                    start=(k == 0), stop=(k == 7)), inc=("pe" if k == 7 else None))
            mm_ev = p.ev("pe")
            p.wait("scalar", mm_ev)
            if j == 0:
                p.wait("scalar", st.get("gate_dve"))
            p.do("scalar", lambda e, j=j, psl=psl, szbuf=szbuf: e.activation(
                szbuf[:, j * 512:(j + 1) * 512], psl[:], AF.Identity), inc="act",
                chain=(j == 0))
            jev[j % 2] = p.ev("act")
        hv = wa["halo"][:].rearrange("p (i s) -> p i s", i=18)
        for g in range(5):
            nt = 4 if g < 4 else 2
            psl = ps_a[:, (g % 2) * 512:(g % 2) * 512 + 512]
            p.wait("tensor", jev.get(g % 2))
            for t in range(nt):
                ct = g * 4 + t
                for k in range(8):
                    p.do("tensor", lambda e, t=t, ct=ct, k=k, psl=psl: e.matmul(
                        psl[:, t * 128:(t + 1) * 128],
                        wa["w_inT"][:, k * DIP + DI + ct * 128:k * DIP + DI + (ct + 1) * 128],
                        wa["xnT"][:, k * 128:(k + 1) * 128],
                        start=(k == 0), stop=(k == 7)),
                        inc=("pe" if (k == 7 and t == nt - 1) else None))
            mm_ev = p.ev("pe")
            p.wait("scalar", mm_ev)
            if g == 0:
                p.wait("scalar", st.get("halo_free"))
                p.wait("scalar", st.get("halo_free_cv"))
                p.wait("scalar", st.get("halo_free_tl"))
            p.do("scalar", lambda e, g=g, nt=nt, psl=psl: e.activation(
                hv[:, g * 4:g * 4 + nt, HDAT:HDAT + 128],
                psl[:, 0:nt * 128].rearrange("p (a b) -> p a b", b=128),
                AF.Identity), inc="act", chain=False)
            jev[g % 2] = p.ev("act")
            halo_evs.append(jev[g % 2])
        psl = ps_a[:, 512:1024]
        p.wait("tensor", jev.get(1))
        for k in range(8):
            p.do("tensor", lambda e, k=k, psl=psl: e.matmul(
                psl[:, 0:32], wa["xnT"][:, k * 128:(k + 1) * 128],
                wa["w_inT"][:, k * DIP + DI + CONVD:k * DIP + DIP],
                start=(k == 0), stop=(k == 7)), inc=("pe" if k == 7 else None))
        mm_ev = p.ev("pe")
        p.wait("scalar", mm_ev)
        p.do("scalar", lambda e, psl=psl, sb=sb: e.activation(
            sb["dtraw"][:], psl[:, 0:32], AF.Identity), inc="act", chain=False)
        jev[1] = p.ev("act")
        st["jev"] = jev
        st["inproj_pe"] = p.ev("pe")
        xbcT_ev = halo_evs[-1]
        convT = wa["convT0"] if par == 0 else wa["convT1"]
        p.wait("vector", xbcT_ev)
        p.wait("vector", p.ev("act"))
        tl = wa["tail0"] if seq % 2 == 0 else wa["tail1"]
        if c == 0:
            p.do("vector", lambda e, tl=tl: e.memset(tl[:], 0.0), inc="dve")
            p.do("vector", lambda e: e.memset(
                wa["halo"][:].rearrange("p (i s) -> p i s", i=18)[:, :, 0:16], 0.0), inc="dve")
            p.do("vector", lambda e: e.memset(
                wa["halo"][:].rearrange("p (i s) -> p i s", i=18)[:, :, 144:160], 0.0), inc="dve")
        p.do("vector", lambda e, tl=tl: e.tensor_copy(
            wa["halo"][:].rearrange("p (i s) -> p i s", i=18)[:, :, 13:16],
            tl[:].rearrange("p (i s) -> p i s", i=18)[:, :, 1:4]), inc="dve")
        batches, cfin = conv_batches(p, "f", wa["halo"], convT)
        vrd = dt_chain(p, "f", par, st, fill=batches)
        fev["vrd"] = vrd
        fev["vrows"] = vrd
        fev["exp"] = p.ev("act")
        fev["conv"] = cfin()
        p.do("scalar", lambda e, szbuf=szbuf: e.activation(
            szbuf[:], szbuf[:], AF.Silu), inc="act")
        fev["sz"] = sz_ev = p.ev("act")
        p.wait("scalar", sz_ev)
        p.do("scalar", lambda e, seq=seq, c=c, szbuf=szbuf: e.dma_start(
            sp_sz.ap()[seq, c], szbuf[:]), inc="dsp", n=16)
        p.do("vector", lambda e, tl=tl: e.tensor_copy(
            tl[:].rearrange("p (i s) -> p i s", i=18)[:, :, 1:4],
            wa["halo"][:].rearrange("p (i s) -> p i s", i=18)[:, :, 141:144]), inc="dve")
        tail_ev = p.ev("dve")
        p.wait("scalar", vrd)
        p.do("scalar", lambda e, seq=seq, c=c, sb=sb: e.dma_start(
            sp_flat.ap()[seq, c], sb["flatpk"][0:96, :]), inc="dsp", n=16)
        fev["flat"] = p.ev("dsp")
        p.wait("scalar", xbcT_ev)
        p.do("scalar", lambda e, seq=seq, c=c: e.dma_start(
            sp_xbc.ap()[seq, c], wa["halo"][:]), inc="dsp", n=16)
        hsp_ev = p.ev("dsp")
        p.do("scalar", lambda e, seq=seq, c=c, sb=sb: e.dma_start(
            sp_dt.ap()[seq, c], sb["dtraw"][:]), inc="dsp", n=16)
        st["halo_free"] = hsp_ev
        st["halo_free_cv"] = fev["conv"][1]
        st["halo_free_tl"] = tail_ev
        fev["szbuf"] = szbuf
        return fev

    def back_a(p, seq, c, par, st, fev):
        y_ev = back_ssd(p, "f", seq, c, par, wa, st, fev, c == 0)
        p.wait("scalar", y_ev)
        p.do("scalar", lambda e, seq=seq, c=c: e.dma_start(
            sp_yf.ap()[seq, c], wa["y0"][:]), inc="dsp", n=16)
        st["y0_free"] = p.ev("dsp")
        st["bend"][par] = p.snap()

    def front_b1(p, seq, c, par, st):
        sb = sml[par]
        fev = {}
        bend = st["bend"].get(par)
        for e_ in ENGINES:
            p.wait_snap(e_, bend)
        halo = wb["haloB0"] if par == 0 else wb["haloB1"]
        szb = wb["szB0"] if par == 0 else wb["szB1"]
        yfb = wb["yfB0"] if par == 0 else wb["yfB1"]
        p.do("sync", lambda e, seq=seq, c=c, halo=halo: e.dma_start(
            halo[:], sp_xbc.ap()[seq, c]), inc="dld", n=16)
        p.do("sync", lambda e, seq=seq, c=c, sb=sb: e.dma_start(
            sb["dtraw"][:], sp_dt.ap()[seq, c]), inc="dld", n=16)
        p.do("sync", lambda e, seq=seq, c=c, szb=szb: e.dma_start(
            szb[:], sp_sz.ap()[seq, c]), inc="dld", n=16)
        p.do("sync", lambda e, seq=seq, c=c, yfb=yfb: e.dma_start(
            yfb[:], sp_yf.ap()[seq, c]), inc="dld", n=16)
        ld_ev = p.ev("dld")
        fev["ld"] = ld_ev
        p.wait("vector", ld_ev)
        tlr = wb["tailR0"] if seq % 2 == 0 else wb["tailR1"]
        if c == NCK - 1:
            p.do("vector", lambda e, tlr=tlr: e.memset(tlr[:], 0.0), inc="dve")
        p.do("vector", lambda e, halo=halo, tlr=tlr: e.tensor_copy(
            halo[:].rearrange("p (i s) -> p i s", i=18)[:, :, 144:147],
            tlr[:].rearrange("p (i s) -> p i s", i=18)[:, :, 1:4]), inc="dve")
        p.do("vector", lambda e, halo=halo, tlr=tlr: e.tensor_copy(
            tlr[:].rearrange("p (i s) -> p i s", i=18)[:, :, 1:4],
            halo[:].rearrange("p (i s) -> p i s", i=18)[:, :, 16:19]), inc="dve")
        convT = wb["convT0"] if par == 0 else wb["convT1"]
        batches, cfin = conv_batches(p, "r", halo, convT)
        vrd = dt_chain(p, "r", par, st, fill=batches)
        fev["vrd"] = vrd
        fev["vrows"] = vrd
        fev["exp"] = p.ev("act")
        fev["conv"] = cfin()
        p.wait("scalar", vrd)
        p.do("scalar", lambda e, seq=seq, c=c, sb=sb: e.dma_start(
            sp_flat.ap()[seq, c], sb["flatpk"][0:96, :]), inc="dsp", n=16)
        fev["flat"] = p.ev("dsp")
        fev["sz"] = ld_ev
        fev["szbuf"] = szb
        fev["yfbuf"] = yfb
        return fev

    def back_b1(p, seq, c, par, st, fev):
        y_ev = back_ssd(p, "r", seq, c, par, wb, st, fev, c == NCK - 1)
        p.wait("vector", fev["ld"])
        p.do("vector", lambda e, yfb=fev["yfbuf"]: e.tensor_add(
            wb["y0"][:], wb["y0"][:], yfb[:]), inc="dve")
        yc_ev = p.ev("dve")
        st["yread_dve"] = yc_ev
        yes = []
        for q in range(2):
            pw = [yc_ev, st.get("sevac_dve")] if q == 0 else [yc_ev]
            _, ye_ = pe_tp(p, [wb["y0"][:, (q * 8 + i) * 128:(q * 8 + i + 1) * 128]
                               for i in range(8)],
                           ps_a[:, q * 512:(q + 1) * 512].bitcast(BF16),
                           wb["ycombT"][:, q * 1024:(q + 1) * 1024],
                           pre_waits_pe=pw, eng="scalar")
            yes.append(ye_)
        st["y0_free"] = p.ev("pe")
        p.wait("tensor", yes[0])
        p.wait("tensor", yes[1])
        for j in range(2):
            for k in range(16):
                p.do("tensor", lambda e, j=j, k=k: e.matmul(
                    ps_a[:, j * 512:(j + 1) * 512], wb["ycombT"][:, k * 128:(k + 1) * 128],
                    wb["w_outT"][:, k * D + j * 512:k * D + j * 512 + 512],
                    start=(k == 0), stop=(k == 15)), inc=("pe" if k == 15 else None))
        st["op_pe"] = p.ev("pe")
        p.wait("vector", st["op_pe"])
        p.wait("vector", st.get("s_spill"))
        p.do("vector", lambda e: e.tensor_scalar(wb["sevac"][:], ps_a[:], 0.5, None,
                                                 ALU.mult), inc="dve")
        st["sevac_dve"] = p.ev("dve")
        p.wait("scalar", st["sevac_dve"])
        p.do("scalar", lambda e, seq=seq, c=c: e.dma_start(
            sp_s.ap()[seq, c], wb["sevac"][:]), inc="dsp", n=16)
        st["s_spill"] = p.ev("dsp")
        st["bend"][par] = p.snap()

    def front_b2(p, seq, c, par, st):
        fev = {}
        bend = st["bend"].get(par)
        for e_ in ENGINES:
            p.wait_snap(e_, bend)
        xb = wc["xC0"] if par == 0 else wc["xC1"]
        sb_ = wc["sC0"] if par == 0 else wc["sC1"]
        xnT = wc["xnTC0"] if par == 0 else wc["xnTC1"]
        p.wait("sync", st.get("xnC_free"))
        p.do("sync", lambda e, seq=seq, c=c: e.dma_start(
            wc["xnC"][:], sp_xn.ap()[seq, c]), inc="dld", n=16)
        ld1 = p.ev("dld")
        p.do("sync", lambda e, seq=seq, c=c, xb=xb: e.dma_start(
            xb[:], x_in.ap()[seq, c * CK:(c + 1) * CK, :]), inc="dld", n=16)
        p.do("sync", lambda e, seq=seq, c=c, sb_=sb_: e.dma_start(
            sb_[:], sp_s.ap()[seq, c]), inc="dld", n=16)
        ld_ev = p.ev("dld")
        psv = ps_s[:].bitcast(BF16)
        _, xnT_ev2 = pe_tp(p, [wc["xnC"][:, k * 128:(k + 1) * 128] for k in range(8)],
                           psv, xnT[:],
                           pre_waits_pe=(ld1, st.get("ps_s_g2")))
        st["ps_s_g2"] = xnT_ev2
        st["xnC_free"] = p.ev("pe")
        fev["xnT_ev"] = xnT_ev2
        fev["ld"] = ld_ev
        fev["bufs"] = (xb, sb_, xnT)
        return fev

    def back_b2(p, seq, c, par, st, fev):
        xb, sb_, xnT = fev["bufs"]
        p.wait("tensor", fev["xnT_ev"])
        jev = st.get("jev2", {})
        for j in range(8):
            s3 = j % 3
            psl = (ps_a[:, 0:512], ps_a[:, 512:1024], ps_x[:])[s3]
            p.wait("tensor", jev.get(s3))
            for k in range(8):
                p.do("tensor", lambda e, j=j, k=k, psl=psl, xnT=xnT: e.matmul(
                    psl[:], xnT[:, k * 128:(k + 1) * 128],
                    wc["w1T"][:, k * 4 * D + j * 512:k * 4 * D + j * 512 + 512],
                    start=(k == 0), stop=False))
            p.do("tensor", lambda e, j=j, psl=psl: e.matmul(
                psl[:], sbm["onesbf"][0:1, :], wc["b1row"][0:1, j * 512:(j + 1) * 512],
                start=False, stop=True), inc="pe")
            mm_ev = p.ev("pe")
            p.wait("scalar", mm_ev)
            if j == 0:
                p.wait("scalar", st.get("hT_dtr"))
            p.do("scalar", lambda e, j=j, psl=psl: e.activation(
                wc["hbuf"][:, j * 512:(j + 1) * 512], psl[:], AF.Gelu), inc="act")
            jev[s3] = p.ev("act")
        st["jev2"] = jev
        h_ev = p.ev("act")
        hq = st.get("hq", {})
        for g in range(4):
            qq = 2 + (g % 2)
            pw = [h_ev, hq.get(qq)]
            _, he_ = pe_tp(p, [wc["hbuf"][:, (g * 8 + i) * 128:(g * 8 + i + 1) * 128]
                               for i in range(8)],
                           ps_y[:, qq * 512:(qq + 1) * 512].bitcast(BF16),
                           wc["hT"][:, g * 1024:(g + 1) * 1024],
                           pre_waits_pe=pw)
            hq[qq] = he_
        st["hq"] = hq
        hT_ev = p.ev("dve")
        st["hT_dtr"] = p.ev("pe")
        p.wait("tensor", hT_ev)
        p.wait("tensor", st.get("g_dve"))
        for j in range(2):
            for k in range(32):
                p.do("tensor", lambda e, j=j, k=k: e.matmul(
                    ps_y[:, j * 512:(j + 1) * 512], wc["hT"][:, k * 128:(k + 1) * 128],
                    wc["w2T"][:, k * D + j * 512:k * D + j * 512 + 512],
                    start=(k == 0), stop=False))
            p.do("tensor", lambda e, j=j: e.matmul(
                ps_y[:, j * 512:(j + 1) * 512], sbm["onesbf"][0:1, :],
                wc["b2row"][0:1, j * 512:(j + 1) * 512], start=False, stop=True), inc="pe")
        st["ffn2_pe"] = p.ev("pe")
        p.wait("scalar", st["ffn2_pe"])
        p.do("scalar", lambda e: e.activation(wc["thC"][:], ps_y[:, 0:1024], AF.Tanh,
                                              scale=0.5), inc="act")
        th_ev = p.ev("act")
        p.wait("vector", th_ev)
        p.wait("vector", fev["ld"])
        p.do("vector", lambda e: e.scalar_tensor_tensor(
            wc["gC"][:], wc["thC"][:], 1.0, ps_y[:, 0:1024], ALU.add, ALU.mult), inc="dve")
        st["g_dve"] = p.ev("dve")
        p.do("vector", lambda e, sb_=sb_: e.tensor_mul(sb_[:], sb_[:], wc["gC"][:]), inc="dve")
        p.wait("vector", st.get("out_spill"))
        p.do("vector", lambda e, xb=xb, sb_=sb_: e.tensor_add(wc["outC"][:], xb[:], sb_[:]), inc="dve")
        o_ev = p.ev("dve")
        p.wait("scalar", o_ev)
        p.do("scalar", lambda e, seq=seq, c=c: e.dma_start(
            out_t.ap()[seq, c * CK:(c + 1) * CK, :], wc["outC"][:]), inc="dsp", n=16)
        st["out_spill"] = p.ev("dsp")
        st["bend"][par] = p.snap()

    def prep(p: P):
        sb = sbm
        for dd in ("f", "r"):
            p.do("sync", lambda e, dd=dd: e.dma_start(
                sb[f"convw_{dd}"][:].rearrange("p (i k) -> p i k", k=4),
                pr[f"convw_{dd}"].ap().rearrange("(i p) k -> p i k", p=128)), inc="dld", n=16)
        p.do("sync", lambda e: e.dma_start(sb["triu"][:], c_triu.ap()), inc="dld", n=16)
        p.do("sync", lambda e: e.dma_start(sb["tril"][:], c_tril.ap()), inc="dld", n=16)
        p.do("sync", lambda e: e.dma_start(sb["identf"][:], c_identf.ap()), inc="dld", n=16)
        p.do("sync", lambda e: e.dma_start(sb["identb"][:], c_identb.ap()), inc="dld", n=16)
        p.do("sync", lambda e: e.dma_start(sb["rhs_mask"][0:64, :], c_E2.ap()), inc="dld", n=16)
        p.do("sync", lambda e: e.dma_start(sb["cbias"][:], c_bias.ap()), inc="dld", n=16)
        for di, dd in enumerate(("f", "r")):
            p.do("sync", lambda e, dd=dd: e.dma_start(
                sb["cbst"][0:18, :], pr[f"convb_{dd}"].ap().rearrange("(i p) -> i p", p=128)),
                inc="dld", n=16)
            cb_ld = p.ev("dld")
            p.wait("tensor", cb_ld)
            p.do("tensor", lambda e, di=di: e.transpose(
                ps_s[:, 384 + di * 32:384 + di * 32 + 32], sb["cbst"][0:32, :],
                sb["identf"][0:32, 0:32]), inc="pe")
            cb_tr = p.ev("pe")
            p.wait("vector", cb_tr)
            p.do("vector", lambda e, dd=dd, di=di: e.tensor_copy(
                sb[f"convb_{dd}"][:], ps_s[:, 384 + di * 32:384 + di * 32 + 18]), inc="dve")
            cb_cp = p.ev("dve")
            p.wait("sync", cb_cp)
        p.do("gpsimd", lambda e: e.dma_start(sb["rowf"][0:1, 0:32], pr["dtb_f"].ap()[None, :]), inc="dgp", n=16)
        p.do("gpsimd", lambda e: e.dma_start(sb["rowf"][0:1, 32:64], pr["dtb_r"].ap()[None, :]), inc="dgp", n=16)
        p.do("gpsimd", lambda e: e.dma_start(sb["rowf"][0:1, 64:96], pr["alog_f"].ap()[None, :]), inc="dgp", n=16)
        p.do("gpsimd", lambda e: e.dma_start(sb["rowf"][0:1, 96:128], pr["alog_r"].ap()[None, :]), inc="dgp", n=16)
        p.do("gpsimd", lambda e: e.dma_start(sb["rowD"][0:1, 0:32], pr["dsk_f"].ap()[None, :]), inc="dgp", n=16)
        p.do("gpsimd", lambda e: e.dma_start(sb["rowD"][0:1, 32:64], pr["dsk_r"].ap()[None, :]), inc="dgp", n=16)
        p.do("gpsimd", lambda e: e.dma_start(wp["wtmp0"][0:1, 0:DI], pr["nrm_f"].ap()[None, :]), inc="dgp", n=16)
        p.do("gpsimd", lambda e: e.dma_start(wp["wtmp1"][0:1, 0:DI], pr["nrm_r"].ap()[None, :]), inc="dgp", n=16)
        ld0 = p.ev("dgp")
        p.do("vector", lambda e: e.memset(sb["onesbf"][:], 1.0), inc="dve")
        p.do("vector", lambda e: e.memset(sb["onescol"][:], 1.0), inc="dve")
        for par in range(2):
            p.do("vector", lambda e, par=par: e.memset(sml[par]["lhsT_mask"][64:66, :], 1.0), inc="dve")
        p.wait("scalar", ld0)
        p.wait("scalar", p.ev("dld"))
        p.do("scalar", lambda e: e.activation(sb["rowf"][0:1, 64:128], sb["rowf"][0:1, 64:128],
                                              AF.Exp), inc="act")
        p.do("scalar", lambda e: e.mul(sb["rowf"][0:1, 64:128], sb["rowf"][0:1, 64:128], -1.0), inc="act")
        a_ev = p.ev("act")
        p.wait("vector", a_ev)
        emit_hilo(p, sb["rowf"][0:1, 0:128], sb["rowt"][0:1, 0:128], sb["rowf"][0:1, 128:256],
                  sb["rowt"][0:1, 128:256])
        hl_ev = p.ev("dve")
        p.wait("tensor", hl_ev)
        p.do("tensor", lambda e: e.matmul(ps_y[:, 0:128], sb["onesbf"][0:1, :],
                                          sb["rowt"][0:1, 0:128], start=True, stop=False))
        p.do("tensor", lambda e: e.matmul(ps_y[:, 0:128], sb["onesbf"][0:1, :],
                                          sb["rowt"][0:1, 128:256], start=False, stop=True), inc="pe")
        p.do("tensor", lambda e: e.matmul(ps_y[:, 128:192], sb["onesbf"][0:1, :],
                                          sb["rowD"][0:1, 0:64], start=True, stop=True), inc="pe")
        pb_ev = p.ev("pe")
        p.wait("vector", pb_ev)
        p.do("vector", lambda e: e.tensor_copy(sb["dtb_f"][:], ps_y[:, 0:32]), inc="dve")
        p.do("vector", lambda e: e.tensor_copy(sb["dtb_r"][:], ps_y[:, 32:64]), inc="dve")
        p.do("vector", lambda e: e.tensor_copy(sb["negA_f"][:], ps_y[:, 64:96]), inc="dve")
        p.do("vector", lambda e: e.tensor_copy(sb["negA_r"][:], ps_y[:, 96:128]), inc="dve")
        p.do("vector", lambda e: e.tensor_copy(sb["Dbc_f"][:], ps_y[:, 128:160]), inc="dve")
        p.do("vector", lambda e: e.tensor_copy(sb["Dbc_r"][:], ps_y[:, 160:192]), inc="dve")
        dcp_ev = p.ev("dve")
        p.wait("tensor", dcp_ev)
        for j in range(8):
            dd = "f" if j < 4 else "r"
            jj = j % 4
            wsrc_t = wp["wtmp0"] if j < 4 else wp["wtmp1"]
            p.do("tensor", lambda e, j=j, jj=jj, wsrc_t=wsrc_t: e.matmul(
                ps_y[:, 512 + (j % 2) * 512: 512 + (j % 2) * 512 + 512],
                sb["onesbf"][0:1, :],
                wsrc_t[0:1, jj * 512:(jj + 1) * 512],
                start=True, stop=True), inc="pe")
            nw_ev = p.ev("pe")
            p.wait("vector", nw_ev)
            p.do("vector", lambda e, dd=dd, jj=jj, j=j: e.tensor_copy(
                sb[f"nwbc_{dd}"][:, jj * 512:(jj + 1) * 512],
                ps_y[:, 512 + (j % 2) * 512:512 + (j % 2) * 512 + 512]), inc="dve")
            p.wait("tensor", p.ev("dve"))
        trx_prev = [Z]
        def tr8(wt, col0, ntile, rows, dst3, lev):
            psb = ps_x[:].bitcast(BF16)
            p.wait("tensor", lev)
            p.wait("tensor", trx_prev[0])
            for k in range(ntile):
                p.do("tensor", lambda e, k=k: e.transpose(
                    psb[:, k * 128:k * 128 + rows],
                    wt[0:rows, col0 + k * 128:col0 + (k + 1) * 128],
                    sbm["identb"][0:rows, 0:rows]),
                    inc=("pe" if k == ntile - 1 else None))
            tev = p.ev("pe")
            p.wait("vector", tev)
            p.do("vector", lambda e: e.tensor_copy(
                dst3, psb[:, 0:ntile * 128].rearrange(
                    "p (k c) -> p k c", k=ntile)[:, :, 0:rows]), inc="dve")
            trx_prev[0] = p.ev("dve")
            return tev, trx_prev[0]
        wfree = {}
        for jt in range(35):
            rows = 128 if jt < 34 else 32
            par = jt % 2
            wt = wp["wtmp0"] if par == 0 else wp["wtmp1"]
            p.wait("gpsimd", wfree.get(par, p.ev("pe") if jt < 2 else None))
            p.do("gpsimd", lambda e, jt=jt, wt=wt, rows=rows: e.dma_start(
                wt[0:rows, 0:1024], w_in.ap()[jt * 128:jt * 128 + rows, :]), inc="dgp", n=16)
            lev = p.ev("dgp")
            dst3 = wa["w_inT"][:].rearrange("p (k c) -> p k c", c=DIP)[
                :, :, jt * 128:jt * 128 + rows]
            tev, _ = tr8(wt, 0, 8, rows, dst3, lev)
            wfree[par] = tev
        for rt in range(8):
            par = rt % 2
            wt = wp["wtmp0"] if par == 0 else wp["wtmp1"]
            p.wait("gpsimd", wfree.get(par))
            p.wait("gpsimd", p.ev("dsp"))
            p.do("gpsimd", lambda e, rt=rt, wt=wt: e.dma_start(
                wt[:, 0:2048], w_out.ap()[rt * 128:(rt + 1) * 128, :]), inc="dgp", n=16)
            lev = p.ev("dgp")
            evs = []
            for hb in range(2):
                d3 = wp["trst"][:, hb * 1024:(hb + 1) * 1024].rearrange(
                    "p (k c) -> p k c", k=8)
                tev, dev = tr8(wt, hb * 1024, 8, 128, d3, lev)
                evs.append(dev)
            wfree[par] = tev
            p.wait("scalar", evs[1])
            for ct in range(16):
                p.do("scalar", lambda e, rt=rt, ct=ct: e.dma_start(
                    sp_wot.ap()[ct, :, rt * 128:(rt + 1) * 128],
                    wp["trst"][:, ct * 128:(ct + 1) * 128]), inc="dsp", n=16)
        def ternary(wsrc, nrtile, ncol, spill):
            nhalf = (ncol + 2047) // 2048
            for eng in ("gpsimd", "scalar", "sync", "vector"):
                for s_ in ("dtr", "dsp", "dve", "act"):
                    p.wait(eng, p.ev(s_))
            p.do("vector", lambda e: e.memset(wp["acc"][:], 0.0), inc="dve")
            absev = {}
            for it in range(nrtile):
                for hf in range(nhalf):
                    cw = min(2048, ncol - hf * 2048)
                    par = (it * nhalf + hf) % 2
                    wf = wp["wf0"] if par == 0 else wp["wf1"]
                    if par in absev:
                        p.wait("gpsimd", absev[par])
                    p.do("gpsimd", lambda e, it=it, hf=hf, cw=cw, wf=wf: e.dma_start(
                        wf[:, 0:cw], wsrc.ap()[it * 128:(it + 1) * 128,
                                               hf * 2048:hf * 2048 + cw]), inc="dgp", n=16)
                    lev = p.ev("dgp")
                    p.wait("scalar", lev)
                    p.wait("scalar", p.ev("dve"))
                    p.do("scalar", lambda e, wf=wf, cw=cw: e.activation(
                        wp["wi0"][:].bitcast(F32)[:, 0:cw],
                        wf[:, 0:cw], AF.Abs, accum_out=wp["acc"][:, 1:2]), inc="act")
                    absev[par] = p.ev("act")
                    p.wait("vector", absev[par])
                    p.do("vector", lambda e: e.tensor_add(wp["acc"][:, 0:1], wp["acc"][:, 0:1],
                                                          wp["acc"][:, 1:2]), inc="dve")
            emit_hilo(p, wp["acc"][:, 0:1], sb["colbf"][:, 0:1],
                      wp["acc"][:, 3:4], sb["colbf"][:, 1:2])
            hl2 = p.ev("dve")
            p.wait("tensor", hl2)
            p.do("tensor", lambda e: e.matmul(ps_s[0:1, 0:1], sb["colbf"][:, 0:1],
                                              sb["onescol"][:, 0:1], start=True, stop=False))
            p.do("tensor", lambda e: e.matmul(ps_s[0:1, 0:1], sb["colbf"][:, 1:2],
                                              sb["onescol"][:, 0:1], start=False, stop=True), inc="pe")
            ps_ev = p.ev("pe")
            p.wait("vector", ps_ev)
            p.do("vector", lambda e: e.tensor_scalar(wp["acc"][0:1, 4:5], ps_s[0:1, 0:1],
                                                     1.0 / (nrtile * 128 * ncol), 1e-8,
                                                     ALU.mult, ALU.max), inc="dve")
            p.do("vector", lambda e: e.reciprocal(wp["acc"][0:1, 5:6], wp["acc"][0:1, 4:5]), inc="dve")
            emit_hilo(p, wp["acc"][0:1, 5:6], sb["rowbf"][0:1, 0:1], wp["acc"][0:1, 6:7],
                      sb["rowbf"][0:1, 1:2])
            iv_ev = p.ev("dve")
            p.wait("tensor", iv_ev)
            p.do("tensor", lambda e: e.matmul(ps_s[:, 4:5], sb["onesbf"][0:1, :],
                                              sb["rowbf"][0:1, 0:1], start=True, stop=False))
            p.do("tensor", lambda e: e.matmul(ps_s[:, 4:5], sb["onesbf"][0:1, :],
                                              sb["rowbf"][0:1, 1:2], start=False, stop=True), inc="pe")
            ib_ev = p.ev("pe")
            p.wait("vector", ib_ev)
            p.do("vector", lambda e: e.tensor_copy(wp["acc"][:, 7:8], ps_s[:, 4:5]), inc="dve")
            trev = {}
            for it in range(nrtile):
                for hf in range(nhalf):
                    cw = min(2048, ncol - hf * 2048)
                    par = (it * nhalf + hf) % 2
                    wf = wp["wf0"] if par == 0 else wp["wf1"]
                    wt = wp["wtmp0"] if par == 0 else wp["wtmp1"]
                    if par in trev:
                        p.wait("gpsimd", trev[par])
                    p.do("gpsimd", lambda e, it=it, hf=hf, cw=cw, wf=wf: e.dma_start(
                        wf[:, 0:cw], wsrc.ap()[it * 128:(it + 1) * 128,
                                               hf * 2048:hf * 2048 + cw]), inc="dgp", n=16)
                    lev = p.ev("dgp")
                    p.wait("vector", lev)
                    p.do("vector", lambda e, wf=wf, cw=cw: e.tensor_scalar(
                        wf[:, 0:cw], wf[:, 0:cw], wp["acc"][:, 7:8], None, ALU.mult), inc="dve")
                    q1_ = p.ev("dve")
                    p.wait("scalar", q1_)
                    if par in trev:
                        p.wait("scalar", trev[par])
                    av = wp["wi0"][:].bitcast(F32)
                    p.do("scalar", lambda e, wf=wf, cw=cw, av=av: e.activation(
                        av[:, 0:cw], wf[:, 0:cw], AF.Abs), inc="act")
                    p.do("scalar", lambda e, wf=wf, wt=wt, cw=cw: e.activation(
                        wt[:, 0:cw], wf[:, 0:cw], AF.Sign), inc="act")
                    q2_ = p.ev("act")
                    p.wait("vector", q2_)
                    p.do("vector", lambda e, cw=cw, av=av: e.tensor_scalar(
                        av[:, 0:cw], av[:, 0:cw], 0.5, None, ALU.is_ge), inc="dve")
                    p.do("vector", lambda e, wt=wt, cw=cw, av=av: e.tensor_mul(
                        wt[:, 0:cw], wt[:, 0:cw], av[:, 0:cw]), inc="dve")
                    qev = p.ev("dve")
                    p.wait("vector", p.ev("dsp"))
                    devs = []
                    for hb in range(cw // 1024):
                        d3 = wp["trst"][:, hb * 1024:(hb + 1) * 1024].rearrange(
                            "p (k c) -> p k c", k=8)
                        tev, dev = tr8(wt, hb * 1024, 8, 128, d3, qev)
                        devs.append(dev)
                    trev[par] = tev
                    p.wait("scalar", devs[-1])
                    for ct in range(cw // 128):
                        ctg = hf * 16 + ct
                        p.do("scalar", lambda e, it=it, ctg=ctg, ct=ct: e.dma_start(
                            spill.ap()[ctg, :, it * 128:(it + 1) * 128],
                            wp["trst"][:, ct * 128:(ct + 1) * 128]), inc="dsp", n=16)
        ternary(w1, 32, 1024, sp_w1t)
        ternary(w2, 8, 4096, sp_w2t)

    def run_pass(p, front, back, order):
        st = {"bend": {}, "psyg": [None] * 4}
        fevs = {}
        items = list(order)
        for i in range(len(items) + 1):
            if i >= 1:
                seq, c = items[i - 1]
                back(p, seq, c, (i - 1) % 2, st, fevs.pop(i - 1))
            if i < len(items):
                seq, c = items[i]
                fevs[i] = front(p, seq, c, i % 2, st)

    def emit(p: P):
        prep(p)
        p.barrier()

        def fa(p_, seq, c, par, st):
            if c == 0:
                sq = seq % 2
                p_.do("vector", lambda e, sq=sq: e.memset(
                    wa["S0" if sq == 0 else "S1"][:], 0.0), inc="dve")
                p_.do("vector", lambda e, sq=sq: e.memset(
                    wa["stil0" if sq == 0 else "stil1"][:], 0.0), inc="dve")
                ev = p_.ev("dve")
                st[("sadd_q", sq)] = [ev] * 4
                st[("scan_q", sq)] = [ev] * 4
            return front_a(p_, seq, c, par, st)
        orderA = [(s, c) for s in range(NSEQ) for c in range(NCK)]
        run_pass(p, fa, back_a, orderA)
        p.barrier()
        for kt in range(16):
            p.do("sync", lambda e, kt=kt: e.dma_start(
                wb["w_outT"][:, kt * D:(kt + 1) * D], sp_wot.ap()[kt]), inc="dld", n=16)
        wo_ev = p.ev("dld")

        def fb1(p_, seq, c, par, st):
            p_.wait("tensor", wo_ev)
            if c == NCK - 1:
                sq = seq % 2
                p_.do("vector", lambda e, sq=sq: e.memset(
                    wb["S0" if sq == 0 else "S1"][:], 0.0), inc="dve")
                p_.do("vector", lambda e, sq=sq: e.memset(
                    wb["stil0" if sq == 0 else "stil1"][:], 0.0), inc="dve")
                ev = p_.ev("dve")
                st[("sadd_q", sq)] = [ev] * 4
                st[("scan_q", sq)] = [ev] * 4
            return front_b1(p_, seq, c, par, st)
        orderB1 = [(s, NCK - 1 - ci) for s in range(NSEQ) for ci in range(NCK)]
        run_pass(p, fb1, back_b1, orderB1)
        p.barrier()
        for kt in range(8):
            p.do("sync", lambda e, kt=kt: e.dma_start(
                wc["w1T"][:, kt * 4 * D:(kt + 1) * 4 * D], sp_w1t.ap()[kt]), inc="dld", n=16)
        for kt in range(32):
            p.do("sync", lambda e, kt=kt: e.dma_start(
                wc["w2T"][:, kt * D:(kt + 1) * D], sp_w2t.ap()[kt]), inc="dld", n=16)
        p.do("gpsimd", lambda e: e.dma_start(wc["b1row"][:], b1.ap()[None, :]), inc="dgp", n=16)
        p.do("gpsimd", lambda e: e.dma_start(wc["b2row"][:], b2.ap()[None, :]), inc="dgp", n=16)
        wl_ev = p.ev("dld")
        bl_ev = p.ev("dgp")

        def fb2(p_, seq, c, par, st):
            p_.wait("tensor", wl_ev)
            p_.wait("tensor", bl_ev)
            p_.wait("sync", wl_ev)
            return front_b2(p_, seq, c, par, st)
        orderB2 = [(s, c) for s in range(NSEQ) for c in range(NCK)]
        run_pass(p, fb2, back_b2, orderB2)
        for e in ENGINES:
            for s in p.sems:
                if p.c[s] > 0:
                    p.wait(e, (s, p.c[s]))

    sems = {s: nc.alloc_semaphore(s) for s in sems_proto}
    with nc.Block() as blk:
        def mk(role):
            def body(eng):
                emit(P(nc, role, eng, sems))
            return body
        blk.sync(mk("sync"))
        blk.tensor(mk("tensor"))
        blk.vector(mk("vector"))
        blk.scalar(mk("scalar"))
        blk.gpsimd(mk("gpsimd"))
    return nc


def make_in_maps(inputs, xs, nseq):
    base = {
        "W_in": np.asarray(inputs["L0_attn_W_in"], np.float32),
        "W_out": np.asarray(inputs["L0_attn_W_out"], np.float32),
        "W1": np.asarray(inputs["L0_ffn_W1"], np.float32),
        "b1": np.asarray(inputs["L0_ffn_b1"], np.float32),
        "W2": np.asarray(inputs["L0_ffn_W2"], np.float32),
        "b2": np.asarray(inputs["L0_ffn_b2"], np.float32),
    }
    for dd in ("f", "r"):
        base[f"conv_w_{dd}"] = np.asarray(inputs[f"L0_attn_conv_w_{dd}"], np.float32)
        base[f"conv_b_{dd}"] = np.asarray(inputs[f"L0_attn_conv_b_{dd}"], np.float32)
        base[f"dt_bias_{dd}"] = np.asarray(inputs[f"L0_attn_dt_bias_{dd}"], np.float32)
        base[f"A_log_{dd}"] = np.asarray(inputs[f"L0_attn_A_log_{dd}"], np.float32)
        base[f"D_{dd}"] = np.asarray(inputs[f"L0_attn_D_{dd}"], np.float32)
        base[f"norm_w_{dd}"] = np.asarray(inputs[f"L0_attn_norm_w_{dd}"], np.float32)
    in_maps = []
    for core in range(NCORE):
        m = dict(base)
        m["x"] = np.ascontiguousarray(xs[nseq * core:nseq * core + nseq])
        in_maps.append(m)
    return in_maps


def kernel(**inputs):
    xs = np.asarray(inputs["L0_attn_x"], np.float32)
    nc = build_nc(NSEQ=2, NCK=32)
    in_maps = make_in_maps(inputs, xs, 2)
    res = run_bass_kernel_spmd(nc, in_maps, core_ids=list(range(NCORE)))
    return np.concatenate([res.results[i]["out"] for i in range(NCORE)], axis=0)
